# version 05
# baseline (speedup 1.0000x reference)
import numpy as np
from contextlib import ExitStack
import concourse.bass as bass
import concourse.mybir as mybir
from concourse.bass_utils import run_bass_kernel_spmd

F32 = mybir.dt.float32
BF16 = mybir.dt.bfloat16
I32 = mybir.dt.int32
AF = mybir.ActivationFunctionType
ALU = mybir.AluOpType
AX = mybir.AxisListType
AP = bass.AP

B, S, D = 16, 2048, 1024
NCORE = 8
SEQ_PER_CORE = B // NCORE
DT = D // 128
TCH = 512
NTCH = S // TCH
NH, HD = 16, 64
NPAIR = NH // 2
EPS_LN = 1e-5
EPS_RMS = 1e-8
EPS_RENORM = 1e-8
INNER_LR = 0.01
QD = 1.0 / 127.0
H0, H1, H1A = 3072, 2816, 4096

ENGINES = ("tensor", "vector", "scalar", "gpsimd", "sync")
SUBLAYERS = ["l0a", "l0f1", "l0b", "l0f2", "l1a", "l1f1", "l1b", "l1f2"]
TTT_STAGES = 99
DBG = None


class Sem:
    __slots__ = ("h", "n", "name", "is_dma")

    def __init__(self, handle, name):
        self.h = handle
        self.n = 0
        self.name = name
        self.is_dma = False


class Prog:
    def __init__(self, nc):
        self.nc = nc
        self.q = {e: [] for e in ENGINES}
        self.es = ExitStack()
        self.mark = {}

    def sem(self, name):
        if not hasattr(self, "_sems"):
            self._sems = {}
        if name not in self._sems:
            self._sems[name] = Sem(
                self.es.enter_context(self.nc.semaphore(name)), name)
        return self._sems[name]

    def sbuf(self, name, shape, dtype):
        return self.es.enter_context(self.nc.sbuf_tensor(name, shape, dtype))

    def psum(self, name, shape, dtype=F32):
        return self.es.enter_context(self.nc.psum_tensor(name, shape, dtype))

    def emit(self, eng, fn):
        self.q[eng].append(fn)

    def op(self, eng, method, *args, inc=None, **kw):
        if inc is not None:
            inc[0].n += inc[1]

        def fn(e, method=method, args=args, kw=kw, inc=inc):
            inst = getattr(e, method)(*args, **kw)
            if inc is not None:
                inst.then_inc(inc[0].h, inc[1])
        self.emit(eng, fn)

    def wait(self, eng, sem, val=None):
        v = sem.n if val is None else val
        if v <= 0:
            return
        self.emit(eng, lambda e, s=sem.h, v=v: e.wait_ge(s, v))

    def dma(self, qname, out, in_, sem, **kw):
        sem.is_dma = True
        self.op(qname, "dma_start", out, in_, inc=(sem, 16), **kw)

    def set_mark(self, key, sem):
        self.mark[key] = (sem, sem.n)

    def wait_mark(self, eng, key):
        if key in self.mark:
            sem, n = self.mark[key]
            self.wait(eng, sem, n)

    def finalize(self, block):
        for eng in ENGINES:
            fns = self.q[eng]
            if not fns:
                continue

            def body(e, fns=fns):
                for fn in fns:
                    fn(e)
            getattr(block, eng)(body)


def _sublayer_barrier(c):
    P = c.P
    for eng in ("tensor", "vector", "scalar", "sync", "gpsimd"):
        for s in (c.s_t, c.s_v, c.s_s):
            P.wait(eng, s)
        for s in list(P._sems.values()):
            if s.is_dma:
                P.wait(eng, s)



_UNIQ = [0]


def _u(name):
    _UNIQ[0] += 1
    return f"{name}_{_UNIQ[0]}"


def host_constants():
    ident = np.eye(128, dtype=np.float32)
    inv_freq = 1.0 / (10000.0 ** (np.arange(0, HD, 2, dtype=np.float64) / HD))
    t = np.arange(S, dtype=np.float64)
    emb = t[:, None] * np.concatenate([inv_freq, inv_freq])[None, :]
    cos = np.cos(emb).astype(np.float32)
    sin = np.sin(emb).astype(np.float32)
    sgn = np.where(np.arange(HD) % 2 == 0, -1.0, 1.0).astype(np.float32)
    sin_eff = sin * sgn[None, :]
    cos_fm = np.concatenate([cos.T, cos.T], axis=0).copy()
    sin_fm = np.concatenate([sin_eff.T, sin_eff.T], axis=0).copy()
    perm = np.zeros((128, 128), dtype=np.float32)
    for i in range(128):
        perm[i ^ 1, i] = 1.0
    mask_u = np.triu(np.ones((128, 128), dtype=np.float32), 1)
    mask_l = np.tril(np.ones((128, 128), dtype=np.float32), -1)
    return {"c_ident": ident, "c_cos": cos_fm, "c_sin": sin_fm,
            "c_perm": perm, "c_mask_u": mask_u, "c_mask_l": mask_l}


def fmc(d, t0, n):
    return slice(d * S + t0, d * S + t0 + n)


class Ctx:
    pass


def build_program(active=None):
    if active is None:
        active = SUBLAYERS
    nc = bass.Bass("TRN2", target_bir_lowering=False, debug=False)
    P = Prog(nc)
    c = Ctx()
    c.nc, c.P = nc, P

    def din(name, shape, dtype=F32):
        return nc.dram_tensor(name, list(shape), dtype, kind="ExternalInput").ap()

    c.x_in = din("L0_attn_a_x", (SEQ_PER_CORE, S, D))
    c.k0 = din("L0_attn_a_kernel", (3, D, D))
    c.l0b_w = {n: din("L0_attn_b_" + n + "_w", (D, D)) for n in "qkvo"}
    c.l0b_b = {n: din("L0_attn_b_" + n + "_b", (D,)) for n in "qkvo"}
    c.l0b_qn = din("L0_attn_b_qn_scale", (D,))
    c.l0b_kn = din("L0_attn_b_kn_scale", (D,))
    c.l0f = {"w_gate": din("L0_ffn_w_gate", (H0, D), I32),
             "w_up": din("L0_ffn_w_up", (H0, D), I32),
             "w_down": din("L0_ffn_w_down", (D, H0), I32)}
    c.l1a_w = {n: din("L1_attn_a_" + n, (D, D)) for n in ("wq", "wk", "wv", "wo")}
    c.l1a_w1 = din("L1_attn_a_w1", (H1A, D))
    c.l1a_w2 = din("L1_attn_a_w2", (D, H1A))
    c.l1a_v = {n: din("L1_attn_a_" + n, (D,)) for n in
               ("ln1_w", "ln1_b", "ln2_w", "ln2_b", "b2")}
    c.l1a_b1 = din("L1_attn_a_b1", (H1A,))
    c.l1a_lr = din("L1_attn_a_lr_scale", (NH, 1, 1))
    c.l1b_win = din("L1_attn_b_w_in", (D, 3 * D))
    c.l1b_bin = din("L1_attn_b_b_in", (3 * D,))
    c.l1b_wout = din("L1_attn_b_w_out", (D, D))
    c.l1b_bout = din("L1_attn_b_b_out", (D,))
    c.l1f = {"w_gate": din("L1_ffn_w_gate", (H1, D)),
             "w_up": din("L1_ffn_w_up", (H1, D)),
             "w_down": din("L1_ffn_w_down", (D, H1))}
    c.consts_np = host_constants()
    c.consts = {n: din(n, v.shape) for n, v in c.consts_np.items()}
    c.out = nc.dram_tensor("out", [SEQ_PER_CORE, S, D], F32,
                           kind="ExternalOutput").ap()
    c.dbg = nc.dram_tensor("dbg", [128, DT * S], F32,
                           kind="ExternalOutput").ap() if DBG else None

    def dsc(name, shape, dtype=BF16):
        return nc.dram_tensor(name, list(shape), dtype, kind="Internal").ap()

    c.st = {}
    for n in "qkvo":
        c.st["l0b_" + n] = dsc("st_l0b_" + n, (D, D))
        c.st["l0b_" + n + "T"] = dsc("st_l0b_" + n + "T", (D, D))
    for n in ("wq", "wk", "wv", "wo"):
        c.st["l1a_" + n] = dsc("st_l1a_" + n, (D, D))
        c.st["l1a_" + n + "T"] = dsc("st_l1a_" + n + "T", (D, D))
    c.st["l1a_w1"] = dsc("st_l1a_w1", (H1A, D))
    c.st["l1a_w1T"] = dsc("st_l1a_w1T", (D, H1A))
    c.st["l1a_w2"] = dsc("st_l1a_w2", (D, H1A))
    c.st["l1a_w2T"] = dsc("st_l1a_w2T", (H1A, D))
    c.st["l0f_g"] = dsc("st_l0f_g", (H0, D))
    c.st["l0f_gT"] = dsc("st_l0f_gT", (D, H0))
    c.st["l0f_u"] = dsc("st_l0f_u", (H0, D))
    c.st["l0f_uT"] = dsc("st_l0f_uT", (D, H0))
    c.st["l0f_d"] = dsc("st_l0f_d", (D, H0))
    c.st["l0f_dT"] = dsc("st_l0f_dT", (H0, D))
    c.st["l0a_k"] = dsc("st_l0a_k", (3, D, D))
    c.st["l1b_win"] = dsc("st_l1b_win", (D, 3 * D))
    c.st["l1b_wout"] = dsc("st_l1b_wout", (D, D))
    c.st["l1f_gT"] = dsc("st_l1f_gT", (D, H1))
    c.st["l1f_uT"] = dsc("st_l1f_uT", (D, H1))
    c.st["l1f_dT"] = dsc("st_l1f_dT", (H1, D))

    c.s_t = P.sem("s_t")
    c.s_v = P.sem("s_v")
    c.s_s = P.sem("s_s")
    c.s_stage = P.sem("s_stage")
    c.s_const = P.sem("s_const")
    c.s_gp = P.sem("s_gp")

    c.XR = P.sbuf("XR", [128, DT * S], F32)
    c.XT = P.sbuf("XT", [128, DT * S], BF16)
    c.CID = P.sbuf("CID", [128, 128], F32)
    c.CIDB = P.sbuf("CIDB", [128, 128], BF16)
    c.CPERM = P.sbuf("CPERM", [128, 128], F32)
    c.ONES = P.sbuf("ONES", [128, 64], F32)
    c.ONER = P.sbuf("ONER", [1, 128], F32)
    c.BIAS = P.sbuf("BIAS", [128, 192], F32)
    c.MASKU = P.sbuf("MASKU", [128, 128], F32)
    c.MASKL = P.sbuf("MASKL", [128, 128], F32)
    c.CNST = P.sbuf("CNST", [128, 4], F32)
    c.ROWA = P.sbuf("ROWA", [1, TCH], F32)
    c.ROWB = P.sbuf("ROWB", [1, TCH], F32)
    c.ln_bufs = None

    for src, dst in (
        (c.l0b_w["q"], c.st["l0b_q"]), (c.l0b_w["k"], c.st["l0b_k"]),
        (c.l0b_w["v"], c.st["l0b_v"]), (c.l0b_w["o"], c.st["l0b_o"]),
        (c.l1a_w["wq"], c.st["l1a_wq"]), (c.l1a_w["wk"], c.st["l1a_wk"]),
        (c.l1a_w["wv"], c.st["l1a_wv"]), (c.l1a_w["wo"], c.st["l1a_wo"]),
        (c.l1a_w1, c.st["l1a_w1"]), (c.l1a_w2, c.st["l1a_w2"]),
        (c.l0f["w_gate"], c.st["l0f_g"]), (c.l0f["w_up"], c.st["l0f_u"]),
        (c.l0f["w_down"], c.st["l0f_d"]),
        (c.k0, c.st["l0a_k"]),
        (c.l1b_win, c.st["l1b_win"]), (c.l1b_wout, c.st["l1b_wout"]),
    ):
        P.dma("gpsimd", dst[...], src[...], c.s_stage)

    P.dma("sync", c.CID[:, :], c.consts["c_ident"][:, :], c.s_const)
    P.dma("sync", c.CPERM[:, :], c.consts["c_perm"][:, :], c.s_const)
    P.dma("sync", c.MASKU[:, :], c.consts["c_mask_u"][:, :], c.s_const)
    P.dma("sync", c.MASKL[:, :], c.consts["c_mask_l"][:, :], c.s_const)
    P.op("gpsimd", "memset", c.CNST[:, 0:1], EPS_LN)
    P.op("gpsimd", "memset", c.CNST[:, 1:2], EPS_RMS)
    P.op("gpsimd", "memset", c.CNST[:, 2:3], EPS_RENORM)
    P.op("gpsimd", "memset", c.ONES[:, :], 1.0)
    P.op("gpsimd", "memset", c.ONER[:, :], 1.0, inc=(c.s_gp, 1))
    P.wait("vector", c.s_const)
    P.op("vector", "tensor_copy", c.CIDB[:, :], c.CID[:, :], inc=(c.s_v, 1))

    _emit_bias_pack(c)

    _emit_pretranspose_weights(c, active)
    if ("l1f1" in active) or ("l1f2" in active):
        _emit_l1f_wquant(c)

    for eng in ("tensor", "vector", "scalar"):
        P.wait(eng, c.s_gp)
        P.wait(eng, c.s_const)

    for seq in range(SEQ_PER_CORE):
        c.seq = seq
        _emit_load_x(c)
        if DBG and seq == 0 and DBG[1] == -1:
            _emit_dbg_dump(c, DBG[0])
        for idx_, name in enumerate(active):
            c.tag = f"s{seq}{name}"
            if name in ("l0f1", "l0f2"):
                _emit_ln(c)
                _emit_l0ffn(c)
            elif name == "l0a":
                _emit_ln(c)
                _emit_l0a(c)
            elif name == "l0b":
                _emit_ln(c)
                _emit_l0b(c)
            elif name == "l1b":
                _emit_ln(c)
                _emit_l1b(c)
            elif name in ("l1f1", "l1f2"):
                _emit_l1ffn(c)
            elif name == "l1a":
                _emit_l1a(c)
            if DBG and seq == 0 and DBG[1] == idx_:
                _emit_dbg_dump(c, DBG[0])
        _emit_store_x(c)

    for s in P._sems.values():
        if s.name.startswith(("s_o", "s_x", "s_vstg", "s_const", "s_stage",
                              "s_w")):
            P.wait("sync", s)
    with P.es:
        blk = P.es.enter_context(nc.Block())
        P.finalize(blk)
    return c


def _emit_bias_pack(c):
    P, nc = c.P, c.nc
    vec_srcs = [
        (c.l0b_b["q"], 0, 8), (c.l0b_b["k"], 8, 8), (c.l0b_b["v"], 16, 8),
        (c.l0b_b["o"], 24, 8), (c.l0b_qn, 32, 8), (c.l0b_kn, 40, 8),
        (c.l1a_v["ln1_w"], 48, 8), (c.l1a_v["ln1_b"], 56, 8),
        (c.l1a_v["ln2_w"], 64, 8), (c.l1a_v["ln2_b"], 72, 8),
        (c.l1a_b1, 80, 32), (c.l1a_v["b2"], 112, 8),
        (c.l1b_bin, 120, 24), (c.l1b_bout, 144, 8),
    ]
    P.wait("tensor", c.s_gp)
    P.wait("tensor", c.s_const)
    es = ExitStack()
    VS = es.enter_context(nc.sbuf_tensor(_u("VSTG"), [32, 128], F32))
    LR = es.enter_context(nc.sbuf_tensor(_u("LRROW"), [1, 16], F32))
    PSV = es.enter_context(nc.psum_tensor(_u("PS_V"), [128, 512], F32))
    with es:
        s_vstg = P.sem("s_vstg")
        for src, col0, n in vec_srcs:
            P.wait_mark("sync", "VSTG")
            P.dma("sync", VS[0:n, :], src.rearrange("(a p) -> a p", p=128), s_vstg)
            P.wait("tensor", s_vstg)
            P.wait_mark("tensor", "PSV")
            P.op("tensor", "transpose", PSV[:, 0:n], VS[0:n, :],
                 c.CID[0:n, 0:n], inc=(c.s_t, 1))
            P.set_mark("VSTG", c.s_t)
            P.wait("scalar", c.s_t)
            P.op("scalar", "copy", c.BIAS[:, col0:col0 + n], PSV[:, 0:n],
                 inc=(c.s_s, 1))
            P.set_mark("PSV", c.s_s)
        P.wait_mark("sync", "VSTG")
        P.dma("sync", LR[0:1, :], c.l1a_lr.rearrange("h a b -> (a b) h"), s_vstg)
        P.wait("tensor", s_vstg)
        P.wait_mark("tensor", "PSV")
        P.op("tensor", "matmul", PSV[:, 16:32], c.ONER[0:1, :], LR[0:1, :],
             start=True, stop=True)
        P.op("tensor", "matmul", PSV[0:64, 32:40], c.ONER[0:1, 0:64],
             LR[0:1, 0:8], start=True, stop=True, inc=(c.s_t, 1))
        P.op("gpsimd", "memset", c.BIAS[0:64, 184:185], 1.0)
        P.op("gpsimd", "memset", c.BIAS[64:128, 184:185], 0.0)
        P.op("gpsimd", "memset", c.BIAS[0:64, 185:186], 0.0)
        P.op("gpsimd", "memset", c.BIAS[64:128, 185:186], 1.0,
             inc=(c.s_gp, 1))
        P.wait("scalar", c.s_t)
        P.op("scalar", "mul", c.BIAS[:, 152:168], PSV[:, 16:32], INNER_LR,
             inc=(c.s_s, 1))
        P.wait("scalar", c.s_s)
        P.op("scalar", "mul", c.BIAS[:, 168:184], PSV[:, 16:32], -INNER_LR,
             inc=(c.s_s, 1))
        P.set_mark("PSV", c.s_s)


def bias_col(c, col):
    return c.BIAS[:, col:col + 1]


def _emit_dbg_dump(c, which):
    P, nc = c.P, c.nc
    _sublayer_barrier(c)
    buf = {"xt": c.XT, "xr": c.XR}.get(which)
    es = ExitStack()
    TMP = es.enter_context(nc.sbuf_tensor(_u("dbgt"), [128, 2048], F32))
    with es:
        for d in range(DT):
            P.wait_mark("vector", "dbgtmp")
            P.op("vector", "tensor_copy", TMP[:, :], buf[:, fmc(d, 0, S)],
                 inc=(c.s_v, 1))
            s_od = P.sem("s_odbg")
            P.wait("sync", c.s_v)
            P.dma("sync", c.dbg[:, d * S:(d + 1) * S], TMP[:, :], s_od)
            P.set_mark("dbgtmp", s_od)


def _emit_load_x(c):
    P, nc = c.P, c.nc
    _sublayer_barrier(c)
    es = ExitStack()
    TMP = es.enter_context(nc.sbuf_tensor(_u(f"ldx{c.seq}"), [128, 2 * D], F32))
    PS = es.enter_context(nc.psum_tensor(_u(f"ldxp{c.seq}"), [128, 2048], F32))
    with es:
        for tt in range(S // 128):
            pr = tt % 2
            s_x = P.sem(f"s_x{pr}")
            P.wait_mark("sync", f"ldx{pr}")
            P.dma("sync", TMP[:, pr * D:(pr + 1) * D],
                  c.x_in[c.seq, tt * 128:(tt + 1) * 128, :], s_x)
            P.wait("tensor", s_x)
            P.wait_mark("tensor", f"ldxp{pr}")
            for d in range(DT):
                P.op("tensor", "transpose", PS[:, (pr * 8 + d) * 128:(pr * 8 + d + 1) * 128],
                     TMP[:, (pr * D) + d * 128:(pr * D) + (d + 1) * 128], c.CID[:, :],
                     inc=(c.s_t, 1) if d == DT - 1 else None)
            P.set_mark(f"ldx{pr}", c.s_t)
            P.wait("vector", c.s_t)
            for d in range(DT):
                P.op("vector", "tensor_copy", c.XR[:, fmc(d, tt * 128, 128)],
                     PS[:, (pr * 8 + d) * 128:(pr * 8 + d + 1) * 128],
                     inc=(c.s_v, 1) if d == DT - 1 else None)
            P.set_mark(f"ldxp{pr}", c.s_v)


def _emit_store_x(c):
    P, nc = c.P, c.nc
    _sublayer_barrier(c)
    es = ExitStack()
    TMP = es.enter_context(nc.sbuf_tensor(_u(f"stx{c.seq}"), [128, 2 * D], F32))
    PS = es.enter_context(nc.psum_tensor(_u(f"stxp{c.seq}"), [128, 2048], F32))
    with es:
        for tt in range(S // 128):
            pr = tt % 2
            P.wait_mark("tensor", f"stxp{pr}")
            for d in range(DT):
                P.op("tensor", "transpose", PS[:, (pr * 8 + d) * 128:(pr * 8 + d + 1) * 128],
                     c.XR[:, fmc(d, tt * 128, 128)], c.CID[:, :],
                     inc=(c.s_t, 1) if d == DT - 1 else None)
            P.wait("vector", c.s_t)
            P.wait_mark("vector", f"stx{pr}")
            for d in range(DT):
                P.op("vector", "tensor_copy", TMP[:, pr * D + d * 128:pr * D + (d + 1) * 128],
                     PS[:, (pr * 8 + d) * 128:(pr * 8 + d + 1) * 128],
                     inc=(c.s_v, 1) if d == DT - 1 else None)
            P.set_mark(f"stxp{pr}", c.s_v)
            s_o = P.sem(f"s_o{pr}")
            P.wait("sync", c.s_v)
            P.dma("sync", c.out[c.seq, tt * 128:(tt + 1) * 128, :],
                  TMP[:, pr * D:(pr + 1) * D], s_o)
            P.set_mark(f"stx{pr}", s_o)


def _emit_ln(c, src=None, dst=None, w_col=None, b_col=None, dst_of=None,
             t_lo=0, t_hi=S, barrier=True):
    P, nc = c.P, c.nc
    if barrier:
        _sublayer_barrier(c)
    if src is None:
        src = c.XR
    if dst is None:
        dst = c.XT
    if dst_of is None:
        dst_of = lambda d, t0, n: dst[:, fmc(d, t0, n)]
    es = ExitStack()
    ext = getattr(c, "ln_bufs", None)
    if ext is None:
        SQ = es.enter_context(nc.sbuf_tensor(_u("ln_sq"), [128, 2 * TCH], F32))
        TM = es.enter_context(nc.sbuf_tensor(_u("ln_tm"), [128, TCH], F32))
        PSS = es.enter_context(nc.psum_tensor(_u("ln_pss"), [1, 1024], F32))
        PSB = es.enter_context(nc.psum_tensor(_u("ln_psb"), [128, 1024], F32))
    else:
        SQ, TM, PSS, PSB = ext
    RA = c.ROWA
    RB = c.ROWB
    with es:
        for t0 in range(t_lo, t_hi, TCH):
            for d in range(DT):
                pr = d % 2
                P.wait_mark("scalar", f"lnsq{pr}")
                P.op("scalar", "activation", SQ[:, pr * TCH:(pr + 1) * TCH],
                     src[:, fmc(d, t0, TCH)], AF.Square, inc=(c.s_s, 1))
                P.wait("tensor", c.s_s)
                if d == 0:
                    P.wait_mark("tensor", "lnpss")
                P.op("tensor", "matmul", PSS[0:1, 0:TCH], c.ONES[:, 0:1],
                     src[:, fmc(d, t0, TCH)], start=(d == 0), stop=(d == DT - 1))
                P.op("tensor", "matmul", PSS[0:1, TCH:2 * TCH], c.ONES[:, 0:1],
                     SQ[:, pr * TCH:(pr + 1) * TCH], start=(d == 0),
                     stop=(d == DT - 1), inc=(c.s_t, 1))
                P.set_mark(f"lnsq{pr}", c.s_t)
            P.wait("vector", c.s_t)
            P.op("vector", "tensor_scalar_mul", RA[0:1, :], PSS[0:1, 0:TCH],
                 1.0 / D, inc=(c.s_v, 1))
            P.wait("vector", c.s_v)
            P.op("vector", "scalar_tensor_tensor", RB[0:1, :], RA[0:1, :], 1.0,
                 RA[0:1, :], ALU.mult, ALU.mult, inc=(c.s_v, 1))
            P.wait("vector", c.s_v)
            P.op("vector", "scalar_tensor_tensor", RB[0:1, :],
                 PSS[0:1, TCH:2 * TCH], 1.0 / D, RB[0:1, :],
                 ALU.mult, ALU.subtract, inc=(c.s_v, 1))
            P.set_mark("lnpss", c.s_v)
            P.wait("scalar", c.s_v)
            P.op("scalar", "activation", RB[0:1, :], RB[0:1, :], AF.Sqrt,
                 bias=c.CNST[0:1, 0:1], scale=1.0, inc=(c.s_s, 1))
            P.wait("vector", c.s_s)
            P.op("vector", "reciprocal", RB[0:1, :], RB[0:1, :], inc=(c.s_v, 1))
            P.wait("vector", c.s_v)
            P.op("vector", "scalar_tensor_tensor", RA[0:1, :], RA[0:1, :], -1.0,
                 RB[0:1, :], ALU.mult, ALU.mult, inc=(c.s_v, 1))
            P.wait("tensor", c.s_v)
            P.wait_mark("tensor", "lnpsb")
            P.op("tensor", "matmul", PSB[:, 0:TCH], c.ONER[0:1, :], RB[0:1, :],
                 start=True, stop=True)
            P.op("tensor", "matmul", PSB[:, TCH:2 * TCH], c.ONER[0:1, :],
                 RA[0:1, :], start=True, stop=True, inc=(c.s_t, 1))
            P.wait("vector", c.s_t)
            for d in range(DT):
                P.wait("vector", c.s_v)
                P.op("vector", "tensor_mul", TM[:, :],
                     src[:, fmc(d, t0, TCH)], PSB[:, 0:TCH], inc=(c.s_v, 1))
                P.wait("vector", c.s_v)
                if w_col is None:
                    P.op("vector", "tensor_add", dst_of(d, t0, TCH),
                         TM[:, :], PSB[:, TCH:2 * TCH], inc=(c.s_v, 1))
                else:
                    P.op("vector", "tensor_add", TM[:, :], TM[:, :],
                         PSB[:, TCH:2 * TCH], inc=(c.s_v, 1))
                    P.wait("scalar", c.s_v)
                    P.op("scalar", "activation", dst_of(d, t0, TCH),
                         TM[:, :], AF.Identity, bias=bias_col(c, b_col + d),
                         scale=bias_col(c, w_col + d), inc=(c.s_s, 1))
                if w_col is not None:
                    P.wait("vector", c.s_s)
            if w_col is None:
                P.set_mark("lnpsb", c.s_v)
            else:
                P.set_mark("lnpsb", c.s_s)


def _emit_l0ffn(c):
    P, nc = c.P, c.nc
    _sublayer_barrier(c)
    NHT = H0 // 128
    GRP = 8
    NGRP = NHT // GRP
    es = ExitStack()
    WG = es.enter_context(nc.sbuf_tensor(_u("f0_wg"), [128, GRP * 1024], BF16))
    WU = es.enter_context(nc.sbuf_tensor(_u("f0_wu"), [128, GRP * 1024], BF16))
    WD = es.enter_context(nc.sbuf_tensor(_u("f0_wd"), [128, GRP * 1024], BF16))
    HGB = es.enter_context(nc.sbuf_tensor(_u("f0_hg"), [128, GRP * S], BF16))
    SGT = es.enter_context(nc.sbuf_tensor(_u("f0_sg"), [128, 2 * TCH], BF16))
    PSG = es.enter_context(nc.psum_tensor(_u("f0_psg"), [128, 1024], F32))
    PSU = es.enter_context(nc.psum_tensor(_u("f0_psu"), [128, 1024], F32))
    PSD = es.enter_context(nc.psum_tensor(_u("f0_psd"), [128, 1024], F32))
    with es:
        for g in range(NGRP):
            s_fw = P.sem("s_w0f")
            P.wait_mark("sync", "f0_w")
            _ld3(P, WG[:, :], 1024, c.st["l0f_gT"], H0, 0, DT, g * 1024, s_fw)
            _ld3(P, WU[:, :], 1024, c.st["l0f_uT"], H0, 0, DT, g * 1024, s_fw)
            _ld3(P, WD[:, :], 1024, c.st["l0f_dT"], D, g * GRP * 128, GRP, 0,
                 s_fw)
            w_ready = s_fw
            for h in range(GRP):
                for tc in range(NTCH):
                    pr = tc % 2
                    t0 = tc * TCH
                    P.wait("tensor", w_ready)
                    P.wait_mark("tensor", f"f0_psg{pr}")
                    for d in range(DT):
                        P.op("tensor", "matmul", PSG[:, pr * TCH:(pr + 1) * TCH],
                             WG[:, d * 1024 + h * 128:d * 1024 + (h + 1) * 128],
                             c.XT[:, fmc(d, t0, TCH)], start=(d == 0),
                             stop=(d == DT - 1),
                             inc=(c.s_t, 1) if d == DT - 1 else None)
                    P.wait_mark("tensor", f"f0_psu{pr}")
                    for d in range(DT):
                        P.op("tensor", "matmul", PSU[:, pr * TCH:(pr + 1) * TCH],
                             WU[:, d * 1024 + h * 128:d * 1024 + (h + 1) * 128],
                             c.XT[:, fmc(d, t0, TCH)], start=(d == 0),
                             stop=(d == DT - 1),
                             inc=(c.s_t, 1) if d == DT - 1 else None)
                    P.wait("scalar", c.s_t, c.s_t.n - 1)
                    P.wait_mark("scalar", f"f0_sg{pr}")
                    P.op("scalar", "activation", SGT[:, pr * TCH:(pr + 1) * TCH],
                         PSG[:, pr * TCH:(pr + 1) * TCH], AF.Silu, bias=0.0,
                         scale=QD, inc=(c.s_s, 1))
                    P.set_mark(f"f0_psg{pr}", c.s_s)
                    P.wait("vector", c.s_s)
                    P.wait("vector", c.s_t)
                    P.op("vector", "scalar_tensor_tensor",
                         HGB[:, h * S + t0:h * S + t0 + TCH],
                         SGT[:, pr * TCH:(pr + 1) * TCH], QD,
                         PSU[:, pr * TCH:(pr + 1) * TCH], ALU.mult, ALU.mult,
                         inc=(c.s_v, 1))
                    P.set_mark(f"f0_psu{pr}", c.s_v)
                    P.set_mark(f"f0_sg{pr}", c.s_v)
            h_ready = c.s_v.n
            for dcx in range(DT):
                for tc in range(NTCH):
                    pr = tc % 2
                    t0 = tc * TCH
                    P.wait("tensor", c.s_v, h_ready)
                    P.wait_mark("tensor", f"f0_psd{pr}")
                    for h in range(GRP):
                        P.op("tensor", "matmul", PSD[:, pr * TCH:(pr + 1) * TCH],
                             WD[:, h * 1024 + dcx * 128:h * 1024 + (dcx + 1) * 128],
                             HGB[:, h * S + t0:h * S + t0 + TCH], start=(h == 0),
                             stop=(h == GRP - 1),
                             inc=(c.s_t, 1) if h == GRP - 1 else None)
                    P.wait("vector", c.s_t)
                    P.op("vector", "scalar_tensor_tensor",
                         c.XR[:, fmc(dcx, t0, TCH)],
                         PSD[:, pr * TCH:(pr + 1) * TCH], QD,
                         c.XR[:, fmc(dcx, t0, TCH)], ALU.mult, ALU.add,
                         inc=(c.s_v, 1))
                    P.set_mark(f"f0_psd{pr}", c.s_v)
            P.set_mark("f0_w", c.s_t)


def _emit_attn_core(c, QH, KH, VH, pair, kind, proj_scale, es_parent,
                    out_hook=None):
    P, nc = c.P, c.nc
    QG = 512
    NQG = S // QG
    VT = c.a_VT
    RS = c.a_RS
    ES = c.a_ES
    MROW = c.a_MROW
    BIASM = c.a_BIASM
    RECR = c.a_RECR
    PVN = c.a_PVN
    NB = c.a_NB
    p_sc = c.p_sc
    p_pv = c.p_pv
    p_bc = c.p_bc
    p_mv = c.p_mv
    p_tr = c.p_tr
    p_sm = c.p_sm

    kTR = c.p_tr.name
    kSC = c.p_sc.name
    kSM = c.p_sm.name
    kBC = c.p_bc.name
    kPV = c.p_pv.name
    kMV = c.p_mv.name
    kRS = RS.name if RS is not None else None
    kES = ES.name
    P.wait("tensor", c.s_s)
    for kt in range(16):
        P.wait_mark("tensor", kTR)
        P.op("tensor", "transpose", p_tr[:, 0:128],
             VH[:, kt * 128:(kt + 1) * 128], c.CIDB[:, :], inc=(c.s_t, 1))
        P.wait("vector", c.s_t)
        P.op("vector", "tensor_copy", VT[:, kt * 130:kt * 130 + 64],
             p_tr[:, 0:64])
        P.op("vector", "tensor_copy", VT[:, kt * 130 + 65:kt * 130 + 129],
             p_tr[:, 64:128], inc=(c.s_v, 1))
        P.set_mark(kTR, c.s_v)

    for h in range(2):
        hb = h * 64
        for qg in range(NQG):
            q0 = qg * QG
            for kt in range(16):
                pr = kt % 2
                P.wait_mark("tensor", kSC + f"_{pr}")
                P.op("tensor", "matmul", p_sc[:, pr * 512:(pr + 1) * 512],
                     KH[hb:hb + 64, kt * 128:(kt + 1) * 128],
                     QH[hb:hb + 64, q0:q0 + QG], start=True, stop=True,
                     inc=(c.s_t, 1))
                if kt == 0:
                    P.wait_mark("scalar", kRS if kind == "softmax" else kES)
                P.wait("scalar", c.s_t)
                if kind == "softmax":
                    P.op("scalar", "copy", RS[:, kt * 512:(kt + 1) * 512],
                         p_sc[:, pr * 512:(pr + 1) * 512], inc=(c.s_s, 1))
                else:
                    P.op("scalar", "activation", ES[:, kt * 512:(kt + 1) * 512],
                         p_sc[:, pr * 512:(pr + 1) * 512], AF.Sigmoid,
                         bias=0.0, scale=proj_scale, inc=(c.s_s, 1))
                P.set_mark(kSC + f"_{pr}", c.s_s)
            if kind == "softmax":
                P.wait("vector", c.s_s)
                P.op("vector", "tensor_reduce", MROW[:, 0:1],
                     AP(RS[:, :].tensor, RS[:, :].offset,
                        [RS[:, :].ap[0], [1, 512], [512, 16]]),
                     AX.XY, ALU.max, inc=(c.s_v, 1))
                P.wait("tensor", c.s_v)
                P.wait_mark("tensor", kSM)
                P.op("tensor", "transpose", p_sm[0:1, 0:128], MROW[:, 0:1],
                     c.CID[:, :], inc=(c.s_t, 1))
                P.wait("vector", c.s_t)
                P.op("vector", "tensor_reduce", MROW[0:1, 1:2], p_sm[0:1, 0:128],
                     AX.X, ALU.max, negate=True, inc=(c.s_v, 1))
                P.wait("vector", c.s_v)
                P.op("vector", "tensor_scalar_mul", MROW[0:1, 2:3],
                     MROW[0:1, 1:2], proj_scale, inc=(c.s_v, 1))
                P.set_mark(kSM, c.s_v)
                P.wait("tensor", c.s_v)
                P.wait_mark("tensor", kBC)
                P.op("tensor", "matmul", p_bc[:, 0:1], c.ONER[0:1, :],
                     MROW[0:1, 2:3], start=True, stop=True, inc=(c.s_t, 1))
                P.wait("scalar", c.s_t)
                P.op("scalar", "copy", BIASM[:, 0:1], p_bc[:, 0:1],
                     inc=(c.s_s, 1))
                P.set_mark(kBC, c.s_s)
                P.wait("scalar", c.s_s)
                P.op("scalar", "activation", ES[:, :], RS[:, :], AF.Exp,
                     bias=BIASM[:, 0:1], scale=proj_scale, inc=(c.s_s, 1))
                P.set_mark(kRS, c.s_s)
            P.wait("tensor", c.s_s)
            P.wait("tensor", c.s_v)
            P.wait_mark("tensor", kPV)
            for kt in range(16):
                P.op("tensor", "matmul", p_pv[0:65, :],
                     VT[:, kt * 130 + 65 * h:kt * 130 + 65 * h + 65],
                     ES[:, kt * 512:(kt + 1) * 512], start=(kt == 0),
                     stop=(kt == 15), inc=(c.s_t, 1) if kt == 15 else None)
            P.set_mark(kES, c.s_t)
            P.wait("vector", c.s_t)
            if kind == "sigmoid":
                P.op("vector", "tensor_scalar_add", RECR[64:65, :],
                     p_pv[64:65, :], EPS_RENORM, inc=(c.s_v, 1))
                P.wait("vector", c.s_v)
                P.op("vector", "reciprocal", RECR[64:65, :], RECR[64:65, :],
                     inc=(c.s_v, 1))
            else:
                P.op("vector", "reciprocal", RECR[64:65, :], p_pv[64:65, :],
                     inc=(c.s_v, 1))
            P.wait("tensor", c.s_v)
            P.wait_mark("tensor", kBC + "b")
            P.op("tensor", "matmul", p_bc[0:64, :], c.ONES[64:65, 0:64],
                 RECR[64:65, :], start=True, stop=True, inc=(c.s_t, 1))
            P.wait("scalar", c.s_t)
            P.op("scalar", "copy", PVN[0:64, :], p_pv[0:64, :], inc=(c.s_s, 1))
            P.set_mark(kPV, c.s_s)
            P.wait("vector", c.s_s)
            P.op("vector", "tensor_mul", PVN[0:64, :], PVN[0:64, :],
                 p_bc[0:64, :], inc=(c.s_v, 1))
            P.set_mark(kBC + "b", c.s_v)
            if out_hook is not None:
                out_hook(h, qg, PVN)
            elif h == 0:
                P.wait("vector", c.s_v)
                P.op("vector", "tensor_add", c.XR[0:64, fmc(pair, q0, QG)],
                     c.XR[0:64, fmc(pair, q0, QG)], PVN[0:64, :],
                     inc=(c.s_v, 1))
            else:
                P.wait("vector", c.s_v)
                P.op("vector", "tensor_copy", NB[0:64, :], PVN[0:64, :],
                     inc=(c.s_v, 1))
                P.wait("tensor", c.s_v)
                P.wait_mark("tensor", kMV)
                P.op("tensor", "matmul", p_mv[64:128, :], c.CIDB[0:64, 0:64],
                     NB[0:64, :], start=True, stop=True,
                     tile_position=(0, 64), inc=(c.s_t, 1))
                P.wait("vector", c.s_t)
                P.op("vector", "tensor_add", c.XR[64:128, fmc(pair, q0, QG)],
                     c.XR[64:128, fmc(pair, q0, QG)], p_mv[64:128, :],
                     inc=(c.s_v, 1))
                P.set_mark(kMV, c.s_v)


def _alloc_attn_scratch(c, es, softmax=True):
    P, nc = c.P, c.nc
    c.a_VT = es.enter_context(nc.sbuf_tensor(_u("aVT"), [128, 16 * 130], BF16))
    c.a_RS = es.enter_context(nc.sbuf_tensor(_u("aRS"), [128, 16 * 512], F32)) \
        if softmax else None
    c.a_ES = es.enter_context(nc.sbuf_tensor(_u("aES"), [128, 16 * 512], BF16))
    c.a_MROW = es.enter_context(nc.sbuf_tensor(_u("aMR"), [128, 8], F32))
    c.a_BIASM = es.enter_context(nc.sbuf_tensor(_u("aBM"), [128, 2], F32))
    c.a_PVN = es.enter_context(nc.sbuf_tensor(_u("aPV"), [128, 512], F32))
    c.a_RECR = c.a_PVN
    c.a_NB = es.enter_context(nc.sbuf_tensor(_u("aNB"), [128, 512], BF16))
    c.p_sc = es.enter_context(nc.psum_tensor(_u("psc"), [128, 1024], F32))
    c.p_pv = es.enter_context(nc.psum_tensor(_u("ppv"), [128, 512], F32))
    c.p_bc = es.enter_context(nc.psum_tensor(_u("pbc"), [128, 512], F32))
    c.p_mv = es.enter_context(nc.psum_tensor(_u("pmv"), [128, 512], F32))
    c.p_tr = es.enter_context(nc.psum_tensor(_u("ptr"), [128, 1024], BF16))
    c.p_sm = es.enter_context(nc.psum_tensor(_u("psm"), [128, 512], F32))
    P.wait("vector", c.s_v)
    vt = c.a_VT[:, :]
    P.op("vector", "memset",
         AP(vt.tensor, vt.offset + 64, [vt.ap[0], [130, 16], [1, 1]]), 1.0)
    P.op("vector", "memset",
         AP(vt.tensor, vt.offset + 129, [vt.ap[0], [130, 16], [1, 1]]), 1.0,
         inc=(c.s_v, 1))


def _emit_proj_pair(c, Wsl, dst, scale_col=None, bias_col_=None, s_w=None,
                    neg=False):
    P = c.P
    kSC = c.p_sc.name
    for tc in range(NTCH):
        pr = tc % 2
        t0 = tc * TCH
        if s_w is not None:
            P.wait("tensor", s_w)
        P.wait_mark("tensor", kSC + f"_{pr}")
        for d in range(DT):
            P.op("tensor", "matmul", c.p_sc[:, pr * 512:(pr + 1) * 512],
                 Wsl[:, d * 128:(d + 1) * 128], c.XT[:, fmc(d, t0, TCH)],
                 start=(d == 0), stop=(d == DT - 1),
                 inc=(c.s_t, 1) if d == DT - 1 else None)
        P.wait("scalar", c.s_t)
        P.op("scalar", "activation", dst[:, t0:t0 + TCH],
             c.p_sc[:, pr * 512:(pr + 1) * 512], AF.Identity,
             bias=bias_col_ if bias_col_ is not None else 0.0,
             scale=scale_col if scale_col is not None else 1.0,
             inc=(c.s_s, 1))
        P.set_mark(kSC + f"_{pr}", c.s_s)


def _emit_l0a(c):
    P, nc = c.P, c.nc
    _sublayer_barrier(c)
    es = ExitStack()
    QH = es.enter_context(nc.sbuf_tensor(_u("l0aQ"), [128, S], BF16))
    KH = es.enter_context(nc.sbuf_tensor(_u("l0aK"), [128, S], BF16))
    VH = es.enter_context(nc.sbuf_tensor(_u("l0aV"), [128, S], BF16))
    WP = es.enter_context(nc.sbuf_tensor(_u("l0aW"), [128, DT * 128], BF16))
    with es:
        _alloc_attn_scratch(c, es, softmax=True)
        s_w = P.sem("s_wl0a")
        for pair in range(NPAIR):
            c.tag = f"s{c.seq}l0a_p{pair}"
            for i, dst in enumerate((QH, KH, VH)):
                P.wait_mark("sync", "l0aW")
                ksrc = c.st["l0a_k"]
                in3 = AP(ksrc.tensor, ksrc.offset + i * D * D + pair * 128,
                         [[D, 128], [128 * D, DT], [1, 128]])
                P.dma("sync", WP[:, :].rearrange("p (t q) -> p t q", q=128),
                      in3, s_w)
                _emit_proj_pair(c, WP[:, :], dst, s_w=s_w)
                P.set_mark("l0aW", c.s_t)
            _emit_attn_core(c, QH, KH, VH, pair, "softmax", 0.125, es)


def _emit_oproj(c, AO, WO, bout_col, s_w):
    P, nc = c.P, c.nc
    es = ExitStack()
    TMO = es.enter_context(nc.sbuf_tensor(_u("oprT"), [128, TCH], F32))
    p_o = es.enter_context(nc.psum_tensor(_u("opr"), [128, 1024], F32))
    with es:
        for dcx in range(DT):
            for tc in range(NTCH):
                pr = tc % 2
                t0 = tc * TCH
                P.wait("tensor", s_w)
                P.wait_mark("tensor", p_o.name + f"_{pr}")
                for d in range(DT):
                    P.op("tensor", "matmul", p_o[:, pr * 512:(pr + 1) * 512],
                         WO[:, d * 1024 + dcx * 128:d * 1024 + (dcx + 1) * 128],
                         AO[:, fmc(d, t0, TCH)], start=(d == 0),
                         stop=(d == DT - 1),
                         inc=(c.s_t, 1) if d == DT - 1 else None)
                P.wait("scalar", c.s_t)
                P.wait_mark("scalar", TMO.name)
                P.op("scalar", "activation", TMO[:, :],
                     p_o[:, pr * 512:(pr + 1) * 512], AF.Identity,
                     bias=bias_col(c, bout_col + dcx), scale=1.0,
                     inc=(c.s_s, 1))
                P.set_mark(p_o.name + f"_{pr}", c.s_s)
                P.wait("vector", c.s_s)
                P.op("vector", "tensor_add", c.XR[:, fmc(dcx, t0, TCH)],
                     c.XR[:, fmc(dcx, t0, TCH)], TMO[:, :], inc=(c.s_v, 1))
                P.set_mark(TMO.name, c.s_v)


def _emit_l1b(c):
    P, nc = c.P, c.nc
    _sublayer_barrier(c)
    es = ExitStack()
    QH = es.enter_context(nc.sbuf_tensor(_u("l1bQ"), [128, S], BF16))
    KH = es.enter_context(nc.sbuf_tensor(_u("l1bK"), [128, S], BF16))
    VH = es.enter_context(nc.sbuf_tensor(_u("l1bV"), [128, S], BF16))
    WP = es.enter_context(nc.sbuf_tensor(_u("l1bW"), [128, DT * 128], BF16))
    AO = es.enter_context(nc.sbuf_tensor(_u("l1bAO"), [128, DT * S], BF16))
    with es:
        es2 = ExitStack()
        _alloc_attn_scratch(c, es2, softmax=True)
        s_w = P.sem("s_wl1b")

        def hook(h, qg, PVN):
            q0 = qg * 512
            pair = c.cur_pair
            P.wait("vector", c.s_v)
            if h == 0:
                P.op("vector", "tensor_copy", AO[0:64, fmc(pair, q0, 512)],
                     PVN[0:64, :], inc=(c.s_v, 1))
            else:
                P.op("vector", "tensor_copy", c.a_NB[0:64, :], PVN[0:64, :],
                     inc=(c.s_v, 1))
                P.wait("tensor", c.s_v)
                P.wait_mark("tensor", c.p_mv.name)
                P.op("tensor", "matmul", c.p_mv[64:128, :], c.CIDB[0:64, 0:64],
                     c.a_NB[0:64, :], start=True, stop=True,
                     tile_position=(0, 64), inc=(c.s_t, 1))
                P.wait("vector", c.s_t)
                P.op("vector", "tensor_copy", AO[64:128, fmc(pair, q0, 512)],
                     c.p_mv[64:128, :], inc=(c.s_v, 1))
                P.set_mark(c.p_mv.name, c.s_v)

        with es2:
            for pair in range(NPAIR):
                c.cur_pair = pair
                c.tag = f"s{c.seq}l1b_p{pair}"
                for i, (dst, bc) in enumerate(((QH, 120), (KH, 128),
                                               (VH, 136))):
                    P.wait_mark("sync", "l1bW")
                    _ld3(P, WP[:, :], 128, c.st["l1b_win"], 3 * D, 0, DT,
                         i * D + pair * 128, s_w)
                    _emit_proj_pair(c, WP[:, :], dst,
                                    bias_col_=bias_col(c, bc + pair), s_w=s_w)
                    P.set_mark("l1bW", c.s_t)
                _emit_attn_core(c, QH, KH, VH, pair, "softmax", 1.0, es2,
                                out_hook=hook)
        _sublayer_barrier(c)
        WO = es.enter_context(nc.sbuf_tensor(_u("l1bWO"), [128, DT * 1024], BF16))
        s_wo = P.sem("s_wl1bo")
        P.wait_mark("sync", "l1bWO")
        _ld3(P, WO[:, :], 1024, c.st["l1b_wout"], D, 0, DT, 0, s_wo)
        _emit_oproj(c, AO, WO, 144, s_wo)
        P.set_mark("l1bWO", c.s_t)


def _emit_l0b(c):
    P, nc = c.P, c.nc
    _sublayer_barrier(c)
    es = ExitStack()
    QH = es.enter_context(nc.sbuf_tensor(_u("l0bQ"), [128, S], BF16))
    KH = es.enter_context(nc.sbuf_tensor(_u("l0bK"), [128, S], BF16))
    VH = es.enter_context(nc.sbuf_tensor(_u("l0bV"), [128, S], BF16))
    QF = es.enter_context(nc.sbuf_tensor(_u("l0bQF"), [128, S], F32))
    CS = es.enter_context(nc.sbuf_tensor(_u("l0bCS"), [128, 2 * S], BF16))
    WP = es.enter_context(nc.sbuf_tensor(_u("l0bW"), [128, DT * 128], BF16))
    AO = es.enter_context(nc.sbuf_tensor(_u("l0bAO"), [128, DT * S], BF16))
    SQ = es.enter_context(nc.sbuf_tensor(_u("l0bSQ"), [128, 2 * TCH], F32))
    T1 = es.enter_context(nc.sbuf_tensor(_u("l0bT1"), [128, TCH], F32))
    T2 = es.enter_context(nc.sbuf_tensor(_u("l0bT2"), [128, TCH], F32))
    RRROW = es.enter_context(nc.sbuf_tensor(_u("l0bRw"), [1, S], F32))
    s_cs = P.sem("s_l0bcs")
    s_w = P.sem("s_wl0b")
    with es:
        es2 = ExitStack()
        _alloc_attn_scratch(c, es2, softmax=False)
        p_st = c.p_bc
        p_rr = c.p_mv
        p_rp = c.p_pv
        P.dma("gpsimd", CS[:, 0:S], c.consts["c_cos"][:, :], s_cs)
        P.dma("gpsimd", CS[:, S:2 * S], c.consts["c_sin"][:, :], s_cs)
        for tc in range(NTCH):
            t0 = tc * TCH
            for d in range(DT):
                pr = d % 2
                P.wait_mark("scalar", SQ.name + f"_{pr}")
                P.op("scalar", "activation", SQ[:, pr * TCH:(pr + 1) * TCH],
                     c.XT[:, fmc(d, t0, TCH)], AF.Square, inc=(c.s_s, 1))
                P.wait("tensor", c.s_s)
                if d == 0:
                    P.wait_mark("tensor", p_st.name)
                P.op("tensor", "matmul", p_st[0:1, 0:512], c.ONES[:, 0:1],
                     SQ[:, pr * TCH:(pr + 1) * TCH], start=(d == 0),
                     stop=(d == DT - 1), inc=(c.s_t, 1))
                P.set_mark(SQ.name + f"_{pr}", c.s_t)
            P.wait("vector", c.s_t)
            P.op("vector", "tensor_scalar_mul", RRROW[0:1, t0:t0 + TCH],
                 p_st[0:1, 0:512], 1.0 / D, inc=(c.s_v, 1))
            P.set_mark(p_st.name, c.s_v)
            P.wait("scalar", c.s_v)
            P.op("scalar", "activation", RRROW[0:1, t0:t0 + TCH],
                 RRROW[0:1, t0:t0 + TCH], AF.Sqrt, bias=c.CNST[0:1, 1:2],
                 scale=1.0, inc=(c.s_s, 1))
            P.wait("vector", c.s_s)
            P.op("vector", "reciprocal", RRROW[0:1, t0:t0 + TCH],
                 RRROW[0:1, t0:t0 + TCH], inc=(c.s_v, 1))
        rr_done = c.s_v.n
        P.wait("vector", s_cs)

        with es2:
            for pair in range(NPAIR):
                c.cur_pair = pair
                c.tag = f"s{c.seq}l0b_p{pair}"
                for side, dst, bcol in ((0, QH, 0), (1, KH, 8)):
                    wname = "qk"[side]
                    P.wait_mark("sync", "l0bW")
                    _ld3(P, WP[:, :], 128, c.st["l0b_" + wname + "T"], D, 0,
                         DT, pair * 128, s_w)
                    P.wait("scalar", s_w)
                    colbase = 32 if side == 0 else 40
                    for d in range(DT):
                        P.op("scalar", "activation",
                             WP[:, d * 128:(d + 1) * 128],
                             WP[:, d * 128:(d + 1) * 128],
                             AF.Identity, bias=0.0,
                             scale=bias_col(c, colbase + d), inc=(c.s_s, 1))
                    P.wait("tensor", c.s_s)
                    _emit_proj_pair(c, WP[:, :], QF)
                    P.set_mark("l0bW", c.s_t)
                    for tc in range(NTCH):
                        t0 = tc * TCH
                        P.wait("tensor", c.s_v, rr_done)
                        P.wait("tensor", c.s_s)
                        P.wait_mark("tensor", p_rr.name)
                        P.op("tensor", "matmul", p_rr[:, 0:512], c.ONER[0:1, :],
                             RRROW[0:1, t0:t0 + TCH], start=True, stop=True,
                             inc=(c.s_t, 1))
                        P.wait("vector", c.s_t)
                        P.wait_mark("vector", T1.name)
                        P.op("vector", "tensor_mul", T1[:, :],
                             QF[:, t0:t0 + TCH], p_rr[:, 0:512], inc=(c.s_v, 1))
                        P.set_mark(p_rr.name, c.s_v)
                        P.wait("vector", c.s_v)
                        P.op("vector", "tensor_scalar_add", T1[:, :], T1[:, :],
                             bias_col(c, bcol + pair), inc=(c.s_v, 1))
                        P.wait("tensor", c.s_v)
                        P.wait_mark("tensor", p_rp.name)
                        P.op("tensor", "matmul", p_rp[:, 0:512], c.CPERM[:, :],
                             T1[:, :], start=True, stop=True, inc=(c.s_t, 1))
                        P.wait("vector", c.s_t)
                        P.op("vector", "tensor_mul", T2[:, :], p_rp[:, 0:512],
                             CS[:, S + t0:S + t0 + TCH], inc=(c.s_v, 1))
                        P.set_mark(p_rp.name, c.s_v)
                        P.op("vector", "tensor_mul", T1[:, :], T1[:, :],
                             CS[:, t0:t0 + TCH], inc=(c.s_v, 1))
                        P.wait("vector", c.s_v)
                        P.op("vector", "tensor_add", dst[:, t0:t0 + TCH],
                             T1[:, :], T2[:, :], inc=(c.s_v, 1))
                        P.set_mark(T1.name, c.s_v)
                P.wait_mark("sync", "l0bW")
                _ld3(P, WP[:, :], 128, c.st["l0b_vT"], D, 0, DT, pair * 128,
                     s_w)
                _emit_proj_pair(c, WP[:, :], VH,
                                bias_col_=bias_col(c, 16 + pair), s_w=s_w)
                P.set_mark("l0bW", c.s_t)
                _emit_attn_core(c, QH, KH, VH, pair, "sigmoid", 0.125, es2,
                                out_hook=_mk_ao_hook(c, AO))
        _sublayer_barrier(c)
        WO = es.enter_context(nc.sbuf_tensor(_u("l0bWO"), [128, DT * 1024], BF16))
        s_wo = P.sem("s_wl0bo")
        P.wait_mark("sync", "l0bWO")
        _ld3(P, WO[:, :], 1024, c.st["l0b_oT"], D, 0, DT, 0, s_wo)
        _emit_oproj(c, AO, WO, 24, s_wo)
        P.set_mark("l0bWO", c.s_t)


def _mk_ao_hook(c, AO):
    P = c.P

    def hook(h, qg, PVN):
        q0 = qg * 512
        pair = c.cur_pair
        P.wait("vector", c.s_v)
        if h == 0:
            P.op("vector", "tensor_copy", AO[0:64, fmc(pair, q0, 512)],
                 PVN[0:64, :], inc=(c.s_v, 1))
        else:
            P.op("vector", "tensor_copy", c.a_NB[0:64, :], PVN[0:64, :],
                 inc=(c.s_v, 1))
            P.wait("tensor", c.s_v)
            P.wait_mark("tensor", c.p_mv.name)
            P.op("tensor", "matmul", c.p_mv[64:128, :], c.CIDB[0:64, 0:64],
                 c.a_NB[0:64, :], start=True, stop=True,
                 tile_position=(0, 64), inc=(c.s_t, 1))
            P.wait("vector", c.s_t)
            P.op("vector", "tensor_copy", AO[64:128, fmc(pair, q0, 512)],
                 c.p_mv[64:128, :], inc=(c.s_v, 1))
            P.set_mark(c.p_mv.name, c.s_v)
    return hook


def _emit_l1a(c):
    P, nc = c.P, c.nc
    _sublayer_barrier(c)
    es = ExitStack()
    XI = es.enter_context(nc.sbuf_tensor(_u("aXI"), [128, DT * S], F32))
    with es:
        _emit_ln(c, src=c.XR, dst=XI, barrier=False)
        _emit_l1a_ttt(c, XI)
        _emit_l1a_mlp(c, XI)
        _sublayer_barrier(c)
        for d in range(DT):
            for tc in range(NTCH):
                P.wait("vector", c.s_v)
                P.op("vector", "tensor_add",
                     c.XR[:, fmc(d, tc * TCH, TCH)],
                     c.XR[:, fmc(d, tc * TCH, TCH)],
                     XI[:, fmc(d, tc * TCH, TCH)], inc=(c.s_v, 1))


def _emit_l1a_ttt(c, XI):
    P, nc = c.P, c.nc
    _sublayer_barrier(c)
    GT = 512
    XT = c.XT
    QG0, KG0, VG0 = 4096, 8192, 12288
    KE0 = 0
    GM0 = 1024
    VT0 = 1536
    KT0 = 1792
    TO0 = 2048
    es = ExitStack()
    W16 = es.enter_context(nc.sbuf_tensor(_u("aW16"), [128, 8192], BF16))
    WO = es.enter_context(nc.sbuf_tensor(_u("aWO"), [128, 4 * 1024], BF16))
    U = es.enter_context(nc.sbuf_tensor(_u("aU"), [128, 1024], F32))
    UB = es.enter_context(nc.sbuf_tensor(_u("aUB"), [128, 1024], BF16))
    TML = es.enter_context(nc.sbuf_tensor(_u("aTML"), [128, TCH], F32))
    W0T = es.enter_context(nc.sbuf_tensor(_u("aW0T"), [128, 512], F32))
    W0B = es.enter_context(nc.sbuf_tensor(_u("aW0B"), [128, 1024], BF16))
    pA = es.enter_context(nc.psum_tensor(_u("apA"), [128, 1024], F32))
    pB = es.enter_context(nc.psum_tensor(_u("apB"), [128, 512], F32))
    pBT = es.enter_context(nc.psum_tensor(_u("apBT"), [128, 512], F32))
    pU = es.enter_context(nc.psum_tensor(_u("apU"), [128, 1024], F32))
    pT = es.enter_context(nc.psum_tensor(_u("apT"), [128, 2048], BF16))
    s_w = P.sem("s_wl1a")
    with es:
        P.wait("gpsimd", c.s_v)
        P.wait("gpsimd", c.s_s)
        P.op("gpsimd", "memset", W0T[:, :], 0.0)
        P.op("gpsimd", "memset", W0B[:, :], 0.0, inc=(c.s_gp, 1))
        P.wait("tensor", c.s_gp)
        P.wait("scalar", c.s_gp)
        P.wait("vector", c.s_gp)

        for g in range(S // GT):
            t0g = g * GT
            P.wait("scalar", c.s_v)
            P.wait("scalar", c.s_t)
            P.wait("vector", c.s_t)
            P.wait("vector", c.s_s)
            c.ln_bufs = (U, TML, pU[0:1, 0:1024], pA[:, 0:1024])
            _emit_ln(c, src=XI, w_col=48, b_col=56,
                     dst_of=lambda d, t0, n: XT[:, d * GT + (t0 - t0g):
                                                d * GT + (t0 - t0g) + n],
                     t_lo=t0g, t_hi=t0g + GT, barrier=False)
            c.ln_bufs = None
            for wname, dst0 in (("wq", QG0), ("wk", KG0), ("wv", VG0)):
                P.wait_mark("sync", "aW16")
                _ld3(P, W16[:, :], 1024, c.st["l1a_" + wname + "T"], D, 0,
                     DT, 0, s_w)
                P.wait("tensor", s_w)
                P.wait("tensor", c.s_s)
                for jt in range(DT):
                    prj = jt % 2
                    P.wait_mark("tensor", pA.name + str(prj))
                    for d in range(DT):
                        P.op("tensor", "matmul",
                             pA[:, prj * 512:(prj + 1) * 512],
                             W16[:, d * 1024 + jt * 128:d * 1024 + (jt + 1) * 128],
                             XT[:, d * GT:(d + 1) * GT], start=(d == 0),
                             stop=(d == DT - 1),
                             inc=(c.s_t, 1) if d == DT - 1 else None)
                    P.wait("scalar", c.s_t)
                    P.op("scalar", "copy", XT[:, dst0 + jt * GT:dst0 + (jt + 1) * GT],
                         pA[:, prj * 512:(prj + 1) * 512], inc=(c.s_s, 1))
                    P.set_mark(pA.name + str(prj), c.s_s)
                P.set_mark("aW16", c.s_t)
            for cc in range(GT // 128):
                gt0 = cc * 128
                _emit_ttt_chunk(c, XI, XT, W16, WO, U, UB, W0T, W0B,
                                pA, pB, pBT, pU, pT,
                                QG0, KG0, VG0, KE0, GM0, VT0, KT0, TO0,
                                g, gt0)


def _emit_ttt_chunk(c, XI, XT, W16, WO, U, UB, W0T, W0B,
                    pA, pB, pBT, pU, pT, QG0, KG0, VG0, KE0, GM0, VT0, KT0,
                    TO0, g, gt0):
    P = c.P
    GT = 512
    chunk_t0 = g * GT + gt0
    KZ0 = {0: 0, 1: 1024}
    GM = 2048
    VT = 2560
    KT = 2816
    TO = 3072
    P.wait("scalar", c.s_s)
    P.wait("scalar", c.s_t)
    P.wait("scalar", c.s_v)
    P.wait_mark("scalar", "aKZ")
    for par in range(2):
        for p in range(8):
            P.op("scalar", "activation",
                 XT[:, KZ0[par] + p * 128:KZ0[par] + (p + 1) * 128],
                 XT[:, KG0 + p * GT + gt0:KG0 + p * GT + gt0 + 128],
                 AF.Identity, bias=0.0, scale=bias_col(c, 184 + par),
                 inc=(c.s_s, 1))
    for hg in range(4):
        bk = hg % 2
        heads = [4 * hg + i for i in range(4)]
        P.wait("tensor", c.s_s)
        P.wait_mark("tensor", pA.name + str(bk))
        for i, h in enumerate(heads):
            par = h % 2
            p = h // 2
            P.op("tensor", "matmul",
                 pA[:, bk * 512 + i * 128:bk * 512 + (i + 1) * 128],
                 XT[:, KZ0[par] + p * 128:KZ0[par] + (p + 1) * 128],
                 XT[:, KG0 + p * GT + gt0:KG0 + p * GT + gt0 + 128],
                 start=True, stop=True, inc=(c.s_t, 1) if i == 3 else None)
        P.wait("vector", c.s_t)
        P.wait_mark("vector", "aW16L")
        for i, h in enumerate(heads):
            asl = pA[:, bk * 512 + i * 128:bk * 512 + (i + 1) * 128]
            P.op("vector", "scalar_tensor_tensor",
                 W16[:, i * 128:(i + 1) * 128], asl, bias_col(c, 152 + h),
                 c.MASKL[:, :], ALU.mult, ALU.mult)
            P.op("vector", "scalar_tensor_tensor",
                 W16[:, 512 + i * 128:512 + (i + 1) * 128], asl,
                 bias_col(c, 152 + h), c.MASKU[:, :], ALU.mult, ALU.mult,
                 inc=(c.s_v, 1) if i == 3 else None)
        P.set_mark(pA.name + str(bk), c.s_v)
        for L in range(1, 7):
            P.wait("tensor", c.s_v)
            P.wait_mark("tensor", pB.name)
            P.wait_mark("tensor", pBT.name)
            for i in range(4):
                Bp = W16[:, (L - 1) * 1024 + i * 128:(L - 1) * 1024 + (i + 1) * 128]
                BTp = W16[:, (L - 1) * 1024 + 512 + i * 128:(L - 1) * 1024 + 512 + (i + 1) * 128]
                P.op("tensor", "matmul", pB[:, i * 128:(i + 1) * 128], BTp, Bp,
                     start=True, stop=True)
                P.op("tensor", "matmul", pBT[:, i * 128:(i + 1) * 128], Bp, BTp,
                     start=True, stop=True,
                     inc=(c.s_t, 1) if i == 3 else None)
            P.wait("vector", c.s_t)
            P.op("vector", "tensor_copy", W16[:, L * 1024:L * 1024 + 512],
                 pB[:, :], inc=(c.s_v, 1))
            P.op("vector", "tensor_copy", W16[:, L * 1024 + 512:(L + 1) * 1024],
                 pBT[:, :], inc=(c.s_v, 1))
            P.set_mark(pB.name, c.s_v)
            P.set_mark(pBT.name, c.s_v)
        P.set_mark("aW16L", c.s_v)
        for ip, p in enumerate((2 * hg, 2 * hg + 1)):
            prt = ip % 2
            P.wait_mark("tensor", pT.name + str(prt))
            P.op("tensor", "transpose", pT[:, prt * 1024:prt * 1024 + 128],
                 XT[:, VG0 + p * GT + gt0:VG0 + p * GT + gt0 + 128],
                 c.CIDB[:, :], inc=(c.s_t, 1))
            P.op("tensor", "transpose", pT[:, prt * 1024 + 128:prt * 1024 + 256],
                 XT[:, KG0 + p * GT + gt0:KG0 + p * GT + gt0 + 128],
                 c.CIDB[:, :], inc=(c.s_t, 1))
            P.wait("vector", c.s_t)
            P.op("vector", "tensor_copy",
                 XT[:, VT + ip * 128:VT + (ip + 1) * 128],
                 pT[:, prt * 1024:prt * 1024 + 128])
            P.op("vector", "tensor_copy",
                 XT[:, KT + ip * 128:KT + (ip + 1) * 128],
                 pT[:, prt * 1024 + 128:prt * 1024 + 256], inc=(c.s_v, 1))
            P.set_mark(pT.name + str(prt), c.s_v)
        P.wait("scalar", c.s_v)
        P.wait("scalar", c.s_s)
        for i, h in enumerate(heads):
            par = h % 2
            p = h // 2
            P.op("scalar", "activation", W0B[:, h * 64:(h + 1) * 64],
                 W0T[:, p * 64:(p + 1) * 64], AF.Identity, bias=0.0,
                 scale=bias_col(c, 184 + par), inc=(c.s_s, 1))
        P.wait("tensor", c.s_s)
        P.wait_mark("tensor", pU.name + str(bk))
        for i, h in enumerate(heads):
            par = h % 2
            p = h // 2
            P.op("tensor", "matmul",
                 pU[:, bk * 512 + i * 64:bk * 512 + (i + 1) * 64],
                 XT[:, KZ0[par] + p * 128:KZ0[par] + (p + 1) * 128],
                 W0B[:, h * 64:(h + 1) * 64],
                 start=True, stop=True, inc=(c.s_t, 1) if i == 3 else None)
        P.wait("vector", c.s_t)
        P.wait("vector", c.s_v)
        P.op("vector", "tensor_sub", U[:, hg * 256:(hg + 1) * 256],
             pU[:, bk * 512:bk * 512 + 256], XT[:, VT:VT + 256],
             inc=(c.s_v, 1))
        P.set_mark(pU.name + str(bk), c.s_v)
        P.wait("scalar", c.s_v)
        P.op("scalar", "copy", UB[:, hg * 256:(hg + 1) * 256],
             U[:, hg * 256:(hg + 1) * 256], inc=(c.s_s, 1))
        for L in range(6, -1, -1):
            P.wait("tensor", c.s_s)
            P.wait_mark("tensor", pU.name + str(1 - bk))
            for i in range(4):
                BTp = W16[:, L * 1024 + 512 + i * 128:L * 1024 + 512 + (i + 1) * 128]
                P.op("tensor", "matmul",
                     pU[:, (1 - bk) * 512 + i * 64:(1 - bk) * 512 + (i + 1) * 64],
                     BTp, UB[:, (hg * 4 + i) * 64:(hg * 4 + i + 1) * 64],
                     start=True, stop=True, inc=(c.s_t, 1) if i == 3 else None)
            P.wait("vector", c.s_t)
            P.op("vector", "tensor_tensor", U[:, hg * 256:(hg + 1) * 256],
                 U[:, hg * 256:(hg + 1) * 256],
                 pU[:, (1 - bk) * 512:(1 - bk) * 512 + 256],
                 ALU.add if L > 0 else ALU.subtract, inc=(c.s_v, 1))
            P.set_mark(pU.name + str(1 - bk), c.s_v)
            P.wait("scalar", c.s_v)
            P.op("scalar", "copy", UB[:, hg * 256:(hg + 1) * 256],
                 U[:, hg * 256:(hg + 1) * 256], inc=(c.s_s, 1))
        P.wait("scalar", c.s_s)
        for i, h in enumerate(heads):
            P.op("scalar", "activation",
                 UB[:, (hg * 4 + i) * 64:(hg * 4 + i + 1) * 64],
                 U[:, (hg * 4 + i) * 64:(hg * 4 + i + 1) * 64], AF.Identity,
                 bias=0.0, scale=bias_col(c, 168 + h), inc=(c.s_s, 1))
        P.wait("tensor", c.s_s)
        P.wait_mark("tensor", pA.name + str(bk))
        for i, h in enumerate(heads):
            par = h % 2
            p = h // 2
            P.op("tensor", "matmul",
                 pA[:, bk * 512 + i * 128:bk * 512 + (i + 1) * 128],
                 XT[:, KZ0[par] + p * 128:KZ0[par] + (p + 1) * 128],
                 XT[:, QG0 + p * GT + gt0:QG0 + p * GT + gt0 + 128],
                 start=True, stop=True, inc=(c.s_t, 1) if i == 3 else None)
        P.wait("vector", c.s_t)
        av = pA[:, bk * 512:(bk + 1) * 512]
        a3 = AP(av.tensor, av.offset, [av.ap[0], [128, 4], [1, 128]])
        mu = c.MASKU[:, :]
        mu3 = AP(mu.tensor, mu.offset, [mu.ap[0], [0, 4], [1, 128]])
        gm = XT[:, GM:GM + 512]
        gm3 = AP(gm.tensor, gm.offset, [gm.ap[0], [128, 4], [1, 128]])
        P.op("vector", "tensor_tensor", gm3, a3, mu3, ALU.mult, inc=(c.s_v, 1))
        P.set_mark(pA.name + str(bk), c.s_v)
        P.wait("tensor", c.s_v)
        P.wait("tensor", c.s_s)
        P.wait_mark("tensor", pB.name)
        for i, h in enumerate(heads):
            rb = (h % 2) * 64
            p = h // 2
            ip = p - 2 * hg
            P.op("tensor", "matmul", pB[rb:rb + 64, ip * 128:(ip + 1) * 128],
                 W0B[:, h * 64:(h + 1) * 64],
                 XT[:, QG0 + p * GT + gt0:QG0 + p * GT + gt0 + 128],
                 start=True, stop=False, tile_position=(0, rb))
            P.op("tensor", "matmul", pB[rb:rb + 64, ip * 128:(ip + 1) * 128],
                 UB[:, (hg * 4 + i) * 64:(hg * 4 + i + 1) * 64],
                 XT[:, GM + i * 128:GM + (i + 1) * 128],
                 start=False, stop=True, tile_position=(0, rb),
                 inc=(c.s_t, 1) if i == 3 else None)
        P.wait("vector", c.s_t)
        P.op("vector", "tensor_copy", XT[:, TO + hg * 256:TO + (hg + 1) * 256],
             pB[:, 0:256], inc=(c.s_v, 1))
        P.set_mark(pB.name, c.s_v)
        P.wait("tensor", c.s_v)
        P.wait_mark("tensor", pBT.name)
        for i, h in enumerate(heads):
            rb = (h % 2) * 64
            p = h // 2
            ip = p - 2 * hg
            P.op("tensor", "matmul", pBT[rb:rb + 64, ip * 64:(ip + 1) * 64],
                 XT[:, KT + ip * 128 + rb:KT + ip * 128 + rb + 64],
                 UB[:, (hg * 4 + i) * 64:(hg * 4 + i + 1) * 64],
                 start=True, stop=True, tile_position=(0, rb),
                 inc=(c.s_t, 1) if i == 3 else None)
        P.wait("vector", c.s_t)
        P.op("vector", "tensor_add", W0T[:, hg * 128:(hg + 1) * 128],
             W0T[:, hg * 128:(hg + 1) * 128], pBT[:, 0:128], inc=(c.s_v, 1))
        P.set_mark(pBT.name, c.s_v)
    P.set_mark("aKZ", c.s_t)
    if TTT_STAGES < 11:
        return
    P.wait("tensor", c.s_v)
    P.wait_mark("tensor", pU.name + "o0")
    P.wait_mark("tensor", pU.name + "o1")
    s_wo = c.P.sem("s_wl1ao")
    for half in range(2):
        c.P.wait_mark("sync", "aWOh")
        _ld3(c.P, WO[:, :], 1024, c.st["l1a_woT"], D, half * 4 * 128, 4, 0,
             s_wo)
        P.wait("tensor", s_wo)
        for jt in range(DT):
            prj = jt % 2
            P.wait_mark("tensor", pU.name + "o" + str(prj))
            for p in range(4):
                P.op("tensor", "matmul",
                     pU[:, prj * 512:prj * 512 + 128],
                     WO[:, p * 1024 + jt * 128:p * 1024 + (jt + 1) * 128],
                     XT[:, TO + (half * 4 + p) * 128:TO + (half * 4 + p + 1) * 128],
                     start=(p == 0), stop=(p == 3),
                     inc=(c.s_t, 1) if p == 3 else None)
            P.wait("vector", c.s_t)
            P.op("vector", "tensor_add",
                 XI[:, fmc(jt, chunk_t0, 128)], XI[:, fmc(jt, chunk_t0, 128)],
                 pU[:, prj * 512:prj * 512 + 128], inc=(c.s_v, 1))
            P.set_mark(pU.name + "o" + str(prj), c.s_v)
        P.set_mark("aWOh", c.s_t)


def _emit_l1a_mlp(c, XI):
    P, nc = c.P, c.nc
    _sublayer_barrier(c)
    _emit_ln(c, src=XI, dst=c.XT, w_col=64, b_col=72, barrier=False)
    NHT = H1A // 128
    GRP = 4
    es = ExitStack()
    W1 = es.enter_context(nc.sbuf_tensor(_u("m1W1"), [128, GRP * 1024], BF16))
    W2 = es.enter_context(nc.sbuf_tensor(_u("m1W2"), [128, GRP * 1024], BF16))
    HGB = es.enter_context(nc.sbuf_tensor(_u("m1HG"), [128, GRP * S], BF16))
    PSG = es.enter_context(nc.psum_tensor(_u("m1psg"), [128, 1024], F32))
    PSD = es.enter_context(nc.psum_tensor(_u("m1psd"), [128, 1024], F32))
    s_fw = P.sem("s_wl1am")
    with es:
        P.wait("sync", c.s_s)
        P.wait("sync", c.s_v)
        P.wait("sync", c.s_t)
        for grp in range(NHT // GRP):
            P.wait_mark("sync", "m1W")
            P.wait("sync", c.s_stage)
            gw = GRP * 128
            _ld3(P, W1[:, :], gw, c.st["l1a_w1T"], H1A, 0, DT, grp * gw, s_fw)
            _ld3(P, W2[:, :], 1024, c.st["l1a_w2T"], D, grp * GRP * 128, GRP,
                 0, s_fw)
            w_ready = s_fw
            for hh in range(GRP):
                ht = grp * GRP + hh
                for tc in range(NTCH):
                    prx = tc % 2
                    t0 = tc * TCH
                    P.wait("tensor", w_ready)
                    P.wait_mark("tensor", PSG.name + str(prx))
                    for d in range(DT):
                        P.op("tensor", "matmul", PSG[:, prx * 512:(prx + 1) * 512],
                             W1[:, d * gw + hh * 128:d * gw + (hh + 1) * 128],
                             c.XT[:, fmc(d, t0, TCH)], start=(d == 0),
                             stop=(d == DT - 1),
                             inc=(c.s_t, 1) if d == DT - 1 else None)
                    P.wait("scalar", c.s_t)
                    P.op("scalar", "activation",
                         HGB[:, hh * S + t0:hh * S + t0 + TCH],
                         PSG[:, prx * 512:(prx + 1) * 512], AF.Gelu,
                         bias=bias_col(c, 80 + ht), scale=1.0, inc=(c.s_s, 1))
                    P.set_mark(PSG.name + str(prx), c.s_s)
            h_ready = c.s_s
            hn = c.s_s.n
            for dcx in range(DT):
                for tc in range(NTCH):
                    prx = tc % 2
                    t0 = tc * TCH
                    P.wait("tensor", h_ready, hn)
                    P.wait_mark("tensor", PSD.name + str(prx))
                    for hh in range(GRP):
                        P.op("tensor", "matmul", PSD[:, prx * 512:(prx + 1) * 512],
                             W2[:, hh * 1024 + dcx * 128:hh * 1024 + (dcx + 1) * 128],
                             HGB[:, hh * S + t0:hh * S + t0 + TCH],
                             start=(hh == 0), stop=(hh == GRP - 1),
                             inc=(c.s_t, 1) if hh == GRP - 1 else None)
                    P.wait("vector", c.s_t)
                    if grp == NHT // GRP - 1:
                        P.op("vector", "scalar_tensor_tensor",
                             XI[:, fmc(dcx, t0, TCH)],
                             PSD[:, prx * 512:(prx + 1) * 512],
                             bias_col(c, 112 + dcx), XI[:, fmc(dcx, t0, TCH)],
                             ALU.add, ALU.add, inc=(c.s_v, 1))
                    else:
                        P.op("vector", "tensor_add", XI[:, fmc(dcx, t0, TCH)],
                             XI[:, fmc(dcx, t0, TCH)],
                             PSD[:, prx * 512:(prx + 1) * 512], inc=(c.s_v, 1))
                    P.set_mark(PSD.name + str(prx), c.s_v)
            P.set_mark("m1W", c.s_t)


def _alloc_quant_scratch(c, es, n):
    nc = c.nc
    q = {}
    q["yn"] = es.enter_context(nc.sbuf_tensor(_u("qyn"), [128, n], F32))
    q["c1"] = es.enter_context(nc.sbuf_tensor(_u("qc1"), [128, n], F32))
    q["c2"] = es.enter_context(nc.sbuf_tensor(_u("qc2"), [128, n], F32))
    q["sg"] = es.enter_context(nc.sbuf_tensor(_u("qsg"), [128, n], BF16))
    q["bm"] = es.enter_context(nc.sbuf_tensor(_u("qbm"), [128, n // 16], F32))
    q["rs"] = es.enter_context(nc.sbuf_tensor(_u("qrs"), [128, n // 16], F32))
    return q


def _win16(ap_2d):
    a = ap_2d
    n = a.ap[-1][1]
    return AP(a.tensor, a.offset, [a.ap[0], [16, n // 16], [1, 16]])


def _bc16(ap_small, n):
    a = ap_small
    return AP(a.tensor, a.offset, [a.ap[0], [1, n // 16], [0, 16]])


def _emit_quant(c, q, src, dst, n, m=128):
    P = c.P
    V, SV = "vector", c.s_v
    yn, c1, c2, sg = (q[k][0:m, 0:n] for k in ("yn", "c1", "c2", "sg"))
    bm, rs = q["bm"][0:m, 0:n // 16], q["rs"][0:m, 0:n // 16]
    P.op(V, "tensor_reduce", bm, _win16(src), AX.X, ALU.max,
         apply_absolute_value=True, inc=(SV, 1))
    P.wait("scalar", c.s_s)
    P.wait("scalar", c.s_v)
    P.op("scalar", "activation", sg, src, AF.Sign, inc=(c.s_s, 1))
    P.wait(V, SV)
    P.op(V, "tensor_scalar", bm, bm, 1.0 / 6.0, 1e-12, ALU.mult, ALU.max,
         inc=(SV, 1))
    P.op(V, "scalar_tensor_tensor", yn, src, -1.0, src, ALU.mult, ALU.max)
    P.wait(V, SV)
    P.op(V, "reciprocal", rs, bm, inc=(SV, 1))
    P.wait(V, SV)
    P.op(V, "tensor_tensor", _win16(yn), _win16(yn), _bc16(rs, n), ALU.mult,
         inc=(SV, 1))
    P.wait(V, SV)
    P.op(V, "tensor_single_scalar", c1, yn, 0.25, ALU.is_gt)
    P.op(V, "tensor_single_scalar", c2, yn, 2.5, ALU.is_gt, inc=(SV, 1))
    P.wait(V, SV)
    P.op(V, "scalar_tensor_tensor", c1, yn, 0.75, c1, ALU.is_gt, ALU.add)
    P.op(V, "scalar_tensor_tensor", c2, yn, 3.5, c2, ALU.is_gt, ALU.add,
         inc=(SV, 1))
    P.wait(V, SV)
    P.op(V, "scalar_tensor_tensor", c1, yn, 1.25, c1, ALU.is_gt, ALU.add)
    P.op(V, "scalar_tensor_tensor", c2, yn, 5.0, c2, ALU.is_gt, ALU.add,
         inc=(SV, 1))
    P.wait(V, SV)
    P.op(V, "scalar_tensor_tensor", c1, yn, 1.75, c1, ALU.is_gt, ALU.add)
    P.op(V, "scalar_tensor_tensor", c2, yn, 5.0, c2, ALU.is_gt, ALU.add,
         inc=(SV, 1))
    P.wait(V, SV)
    P.op(V, "scalar_tensor_tensor", c2, c1, 0.5, c2, ALU.mult, ALU.add,
         inc=(SV, 1))
    P.wait(V, c.s_s)
    P.wait(V, SV)
    P.op(V, "tensor_mul", c1, c2, sg, inc=(SV, 1))
    P.wait(V, SV)
    P.op(V, "tensor_tensor", _win16(c1), _win16(c1), _bc16(bm, n), ALU.mult,
         inc=(SV, 1))
    P.wait(V, SV)
    P.op(V, "tensor_copy", dst, c1, inc=(SV, 1))


def _ld3(P, dst_ap, ncol, srcT, rowstride, row0, nrow_t, col0, sem):
    out3 = dst_ap.rearrange("p (t q) -> p t q", q=ncol)
    in3 = AP(srcT.tensor, srcT.offset + row0 * rowstride + col0,
             [[rowstride, 128], [128 * rowstride, nrow_t], [1, ncol]])
    P.dma("sync", out3, in3, sem)


def _emit_pretranspose_weights(c, active):
    P, nc = c.P, c.nc
    jobs = []
    if "l0b" in active:
        for n in "qkvo":
            jobs.append((c.st["l0b_" + n], c.st["l0b_" + n + "T"], D, D))
    if "l1a" in active:
        for n in ("wq", "wk", "wv", "wo"):
            jobs.append((c.st["l1a_" + n], c.st["l1a_" + n + "T"], D, D))
        jobs.append((c.st["l1a_w1"], c.st["l1a_w1T"], H1A, D))
        jobs.append((c.st["l1a_w2"], c.st["l1a_w2T"], D, H1A))
    if ("l0f1" in active) or ("l0f2" in active):
        jobs.append((c.st["l0f_g"], c.st["l0f_gT"], H0, D))
        jobs.append((c.st["l0f_u"], c.st["l0f_uT"], H0, D))
        jobs.append((c.st["l0f_d"], c.st["l0f_dT"], D, H0))
    if not jobs:
        return
    _sublayer_barrier(c)
    es = ExitStack()
    LT = es.enter_context(nc.sbuf_tensor(_u("ptL"), [128, 2 * 4096], BF16))
    TT = es.enter_context(nc.sbuf_tensor(_u("ptT"), [128, 2 * 4096], BF16))
    pX = es.enter_context(nc.psum_tensor(_u("ptP"), [128, 2048], BF16))
    s_l = [P.sem("s_pt0"), P.sem("s_pt1")]
    s_s2 = [P.sem("s_pt2"), P.sem("s_pt3")]
    with es:
        P.wait("sync", c.s_stage)
        for wsrc, wdst, O, I in jobs:
            for ot in range(O // 128):
                pr = ot % 2
                P.wait_mark("sync", f"ptL{pr}")
                P.dma("sync", LT[:, pr * 4096:pr * 4096 + I],
                      wsrc[ot * 128:(ot + 1) * 128, :], s_l[pr])
                P.wait("tensor", s_l[pr])
                P.wait_mark("tensor", f"ptT{pr}")
                for itc in range(I // 1024):
                    pp = itc % 2
                    P.wait_mark("tensor", pX.name + str(pp))
                    for j in range(8):
                        it = itc * 8 + j
                        P.op("tensor", "transpose",
                             pX[:, pp * 1024 + j * 128:pp * 1024 + (j + 1) * 128],
                             LT[:, pr * 4096 + it * 128:pr * 4096 + (it + 1) * 128],
                             c.CIDB[:, :],
                             inc=(c.s_t, 1) if j == 7 else None)
                    P.wait("vector", c.s_t)
                    P.op("vector", "tensor_copy",
                         TT[:, pr * 4096 + itc * 1024:pr * 4096 + (itc + 1) * 1024],
                         pX[:, pp * 1024:(pp + 1) * 1024], inc=(c.s_v, 1))
                    P.set_mark(pX.name + str(pp), c.s_v)
                P.set_mark(f"ptL{pr}", c.s_t)
                P.wait("sync", c.s_v)
                tv = TT[:, pr * 4096:pr * 4096 + I]
                src3 = AP(tv.tensor, tv.offset, [tv.ap[0], [128, I // 128], [1, 128]])
                dst3 = AP(wdst.tensor, wdst.offset + ot * 128,
                          [[O, 128], [128 * O, I // 128], [1, 128]])
                P.dma("sync", dst3, src3, s_s2[pr])
                P.set_mark(f"ptT{pr}", s_s2[pr])


def _emit_l1f_wquant(c):
    P, nc = c.P, c.nc
    _sublayer_barrier(c)
    es = ExitStack()
    with es:
        WF = es.enter_context(nc.sbuf_tensor(_u("wqF"), [128, 2 * 2816], F32))
        WQ = es.enter_context(nc.sbuf_tensor(_u("wqQ"), [128, 2 * 2816], BF16))
        WT = es.enter_context(nc.sbuf_tensor(_u("wqT"), [128, 2816], BF16))
        p_tq = es.enter_context(nc.psum_tensor(_u("wqp"), [128, 2048], BF16))
        q = _alloc_quant_scratch(c, es, 2816)
        q1k = _alloc_quant_scratch(c, es, 1024)
        jobs = []
        for wsrc, wdst, ntile, width in (
            (c.l1f["w_gate"], c.st["l1f_gT"], 22, 1024),
            (c.l1f["w_up"], c.st["l1f_uT"], 22, 1024),
            (c.l1f["w_down"], c.st["l1f_dT"], 8, 2816),
        ):
            for i in range(ntile):
                jobs.append((wsrc, wdst, i, ntile, width))
        for idx, (wsrc, wdst, i, ntile, width) in enumerate(jobs):
            pr = idx % 2
            s_lw = P.sem(f"s_l1fq_ld{pr}")
            s_sw = P.sem(f"s_l1fq_st{pr}")
            qq = q1k if width == 1024 else q
            P.wait_mark("sync", f"wqF{pr}")
            P.dma("sync", WF[:, pr * 2816:pr * 2816 + width],
                  wsrc[i * 128:(i + 1) * 128, :], s_lw)
            P.wait("vector", s_lw)
            P.wait("scalar", s_lw)
            P.wait_mark("vector", f"wqQ{pr}")
            _emit_quant(c, qq, WF[:, pr * 2816:pr * 2816 + width],
                        WQ[:, pr * 2816:pr * 2816 + width], width)
            P.set_mark(f"wqF{pr}", c.s_v)
            nblk = width // 128
            ow = ntile * 128
            P.wait("tensor", c.s_v)
            P.wait_mark("vector", "wqT")
            for j in range(nblk):
                pj = j % 2
                P.wait_mark("tensor", p_tq.name + str(pj))
                P.op("tensor", "transpose", p_tq[:, pj * 1024:pj * 1024 + 128],
                     WQ[:, pr * 2816 + j * 128:pr * 2816 + (j + 1) * 128],
                     c.CIDB[:, :], inc=(c.s_t, 1))
                P.wait("vector", c.s_t)
                P.op("vector", "tensor_copy", WT[:, j * 128:(j + 1) * 128],
                     p_tq[:, pj * 1024:pj * 1024 + 128], inc=(c.s_v, 1))
                P.set_mark(p_tq.name + str(pj), c.s_v)
            P.set_mark(f"wqQ{pr}", c.s_v)
            P.wait("sync", c.s_v)
            wtv = WT[:, :]
            src3 = AP(wtv.tensor, wtv.offset, [wtv.ap[0], [128, nblk], [1, 128]])
            dst3 = AP(wdst.tensor, wdst.offset + i * 128,
                      [[ow, 128], [128 * ow, nblk], [1, 128]])
            P.dma("sync", dst3, src3, s_sw)
            P.set_mark("wqT", s_sw)
    c.l1f_wq_done = True


def _emit_l1ffn(c):
    P, nc = c.P, c.nc
    _sublayer_barrier(c)
    HCH = [(0, 512), (512, 512), (1024, 512), (1536, 512), (2048, 512),
           (2560, 256)]
    es = ExitStack()
    XTM = es.enter_context(nc.sbuf_tensor(_u("f1XTM"), [128, D], F32))
    XQT = es.enter_context(nc.sbuf_tensor(_u("f1XQT"), [128, D], BF16))
    HQF = es.enter_context(nc.sbuf_tensor(_u("f1HQF"), [128, 22 * TCH], BF16))
    HTM = es.enter_context(nc.sbuf_tensor(_u("f1HTM"), [128, 2048], F32))
    HQT = es.enter_context(nc.sbuf_tensor(_u("f1HQT"), [128, 2048], BF16))
    SGT = es.enter_context(nc.sbuf_tensor(_u("f1SG"), [128, TCH], F32))
    WB = es.enter_context(nc.sbuf_tensor(_u("f1WB"), [128, 22 * 1024], BF16))
    p_a = es.enter_context(nc.psum_tensor(_u("f1pa"), [128, 1024], F32))
    p_b = es.enter_context(nc.psum_tensor(_u("f1pb"), [128, 1024], F32))
    p_t = es.enter_context(nc.psum_tensor(_u("f1pt"), [128, 2048], BF16))
    p_d = es.enter_context(nc.psum_tensor(_u("f1pd"), [128, 1024], F32))
    s_w = P.sem("s_wl1f")
    s_wd = P.sem("s_wl1fd")
    with es:
        q1k = _alloc_quant_scratch(c, es, 1024)
        ROWS = es.enter_context(nc.sbuf_tensor(_u("f1RO"), [128, 8], F32))
        for tt in range(S // 128):
            P.wait_mark("tensor", XTM.name)
            for d in range(DT):
                pr2 = d % 2
                P.wait_mark("tensor", p_a.name + "x" + str(pr2))
                P.op("tensor", "transpose", p_a[:, pr2 * 512:pr2 * 512 + 128],
                     c.XR[:, fmc(d, tt * 128, 128)], c.CID[:, :],
                     inc=(c.s_t, 1))
                P.wait("scalar", c.s_t)
                P.op("scalar", "copy", XTM[:, d * 128:(d + 1) * 128],
                     p_a[:, pr2 * 512:pr2 * 512 + 128], inc=(c.s_s, 1))
                P.set_mark(p_a.name + "x" + str(pr2), c.s_s)
            P.wait("vector", c.s_s)
            P.op("vector", "tensor_reduce", ROWS[:, 0:1], XTM[:, :], AX.X,
                 ALU.add, negate=True, inc=(c.s_v, 1))
            P.wait("scalar", c.s_v)
            P.op("scalar", "activation", q1k["c1"][:, :], XTM[:, :],
                 AF.Square, inc=(c.s_s, 1))
            P.wait("vector", c.s_s)
            P.op("vector", "tensor_reduce", ROWS[:, 1:2], q1k["c1"][:, :],
                 AX.X, ALU.add, inc=(c.s_v, 1))
            P.wait("vector", c.s_v)
            P.op("vector", "tensor_scalar_mul", ROWS[:, 2:3], ROWS[:, 0:1],
                 1.0 / D, inc=(c.s_v, 1))
            P.wait("vector", c.s_v)
            P.op("vector", "tensor_mul", ROWS[:, 3:4], ROWS[:, 2:3],
                 ROWS[:, 2:3], inc=(c.s_v, 1))
            P.wait("vector", c.s_v)
            P.op("vector", "scalar_tensor_tensor", ROWS[:, 4:5], ROWS[:, 1:2],
                 1.0 / D, ROWS[:, 3:4], ALU.mult, ALU.subtract,
                 inc=(c.s_v, 1))
            P.wait("scalar", c.s_v)
            P.op("scalar", "activation", ROWS[:, 4:5], ROWS[:, 4:5], AF.Sqrt,
                 bias=c.CNST[:, 0:1], scale=1.0, inc=(c.s_s, 1))
            P.wait("vector", c.s_s)
            P.op("vector", "reciprocal", ROWS[:, 5:6], ROWS[:, 4:5],
                 inc=(c.s_v, 1))
            P.wait("vector", c.s_v)
            P.op("vector", "tensor_scalar", XTM[:, :], XTM[:, :],
                 ROWS[:, 2:3], ROWS[:, 5:6], ALU.add, ALU.mult,
                 inc=(c.s_v, 1))
            P.wait("vector", c.s_v)
            _emit_quant(c, q1k, XTM[:, :], XQT[:, :], 1024)
            P.set_mark(XTM.name, c.s_v)
            P.wait("tensor", c.s_v)
            for d in range(DT):
                pr2 = d % 2
                P.wait_mark("tensor", p_t.name + str(pr2))
                P.op("tensor", "transpose", p_t[:, pr2 * 1024:pr2 * 1024 + 128],
                     XQT[:, d * 128:(d + 1) * 128], c.CIDB[:, :],
                     inc=(c.s_t, 1))
                P.wait("vector", c.s_t)
                P.op("vector", "tensor_copy", c.XT[:, fmc(d, tt * 128, 128)],
                     p_t[:, pr2 * 1024:pr2 * 1024 + 128], inc=(c.s_v, 1))
                P.set_mark(p_t.name + str(pr2), c.s_v)
        for tc in range(NTCH):
            t0 = tc * TCH
            for hci, (h0, hn) in enumerate(HCH):
                P.wait_mark("sync", WB.name)
                for buf0, wt in ((0, c.st["l1f_gT"]), (4096, c.st["l1f_uT"])):
                    bv = WB[:, buf0:buf0 + DT * 512]
                    out3 = AP(bv.tensor, bv.offset,
                              [bv.ap[0], [512, DT], [1, hn]])
                    in3 = AP(wt.tensor, wt.offset + h0,
                             [[H1, 128], [128 * H1, DT], [1, hn]])
                    P.dma("sync", out3, in3, s_w)
                P.wait("tensor", s_w)
                for ttl in range(4):
                    tt0 = t0 + ttl * 128
                    prx = ttl % 2
                    P.wait_mark("tensor", p_a.name + str(prx))
                    for d in range(DT):
                        P.op("tensor", "matmul", p_a[:, prx * 512:prx * 512 + hn],
                             c.XT[:, fmc(d, tt0, 128)],
                             WB[:, d * 512:d * 512 + hn],
                             start=(d == 0), stop=(d == DT - 1),
                             inc=(c.s_t, 1) if d == DT - 1 else None)
                    P.wait_mark("tensor", p_b.name + str(prx))
                    for d in range(DT):
                        P.op("tensor", "matmul", p_b[:, prx * 512:prx * 512 + hn],
                             c.XT[:, fmc(d, tt0, 128)],
                             WB[:, 4096 + d * 512:4096 + d * 512 + hn],
                             start=(d == 0), stop=(d == DT - 1),
                             inc=(c.s_t, 1) if d == DT - 1 else None)
                    P.wait("scalar", c.s_t, c.s_t.n - 1)
                    P.wait_mark("scalar", SGT.name)
                    P.op("scalar", "activation", SGT[:, 0:hn],
                         p_a[:, prx * 512:prx * 512 + hn], AF.Silu, bias=0.0,
                         scale=1.0, inc=(c.s_s, 1))
                    P.set_mark(p_a.name + str(prx), c.s_s)
                    P.wait("vector", c.s_s)
                    P.wait("vector", c.s_t)
                    P.wait_mark("vector", HTM.name + str(ttl))
                    P.op("vector", "tensor_mul",
                         HTM[:, ttl * 512:ttl * 512 + hn], SGT[:, 0:hn],
                         p_b[:, prx * 512:prx * 512 + hn], inc=(c.s_v, 1))
                    P.set_mark(p_b.name + str(prx), c.s_v)
                    P.set_mark(SGT.name, c.s_v)
                P.set_mark(WB.name, c.s_t)
                P.wait("vector", c.s_v)
                for ttl in range(4):
                    _emit_quant(c, q1k, HTM[:, ttl * 512:ttl * 512 + hn],
                                HQT[:, ttl * 512:ttl * 512 + hn], hn)
                    P.set_mark(HTM.name + str(ttl), c.s_v)
                P.wait("tensor", c.s_v)
                nht = hn // 128
                for ttl in range(4):
                    for hh in range(nht):
                        pr2 = hh % 2
                        P.wait_mark("tensor", p_t.name + str(pr2))
                        P.op("tensor", "transpose", p_t[:, pr2 * 1024:pr2 * 1024 + 128],
                             HQT[:, ttl * 512 + hh * 128:ttl * 512 + (hh + 1) * 128],
                             c.CIDB[:, :], inc=(c.s_t, 1))
                        P.wait("vector", c.s_t)
                        P.op("vector", "tensor_copy",
                             HQF[:, (h0 // 128 + hh) * TCH + ttl * 128:
                                 (h0 // 128 + hh) * TCH + (ttl + 1) * 128],
                             p_t[:, pr2 * 1024:pr2 * 1024 + 128],
                             inc=(c.s_v, 1))
                        P.set_mark(p_t.name + str(pr2), c.s_v)
            P.set_mark(WB.name, c.s_t)
            P.wait_mark("sync", WB.name)
            _ld3(P, WB[:, :], 1024, c.st["l1f_dT"], D, 0, 22, 0, s_wd)
            P.wait("tensor", s_wd)
            P.wait("tensor", c.s_v)
            for dcx in range(DT):
                prx = dcx % 2
                P.wait_mark("tensor", p_d.name + str(prx))
                for ht in range(22):
                    P.op("tensor", "matmul",
                         p_d[:, prx * 512:(prx + 1) * 512],
                         WB[:, ht * 1024 + dcx * 128:ht * 1024 + (dcx + 1) * 128],
                         HQF[:, ht * TCH:(ht + 1) * TCH],
                         start=(ht == 0), stop=(ht == 21),
                         inc=(c.s_t, 1) if ht == 21 else None)
                P.wait("vector", c.s_t)
                P.op("vector", "tensor_add", c.XR[:, fmc(dcx, t0, TCH)],
                     c.XR[:, fmc(dcx, t0, TCH)],
                     p_d[:, prx * 512:(prx + 1) * 512], inc=(c.s_v, 1))
                P.set_mark(p_d.name + str(prx), c.s_v)
            P.set_mark(WB.name, c.s_t)


def kernel(**inputs):
    active = SUBLAYERS
    c = build_program(active)
    consts = c.consts_np
    in_maps = []
    x_full = np.ascontiguousarray(inputs["L0_attn_a_x"])
    for core in range(NCORE):
        m = {}
        for k, v in inputs.items():
            if k == "L0_attn_a_x":
                m[k] = np.ascontiguousarray(
                    x_full[core * SEQ_PER_CORE:(core + 1) * SEQ_PER_CORE])
            else:
                m[k] = np.ascontiguousarray(v)
        for k, v in consts.items():
            m[k] = v
        in_maps.append(m)
    res = run_bass_kernel_spmd(c.nc, in_maps, core_ids=list(range(NCORE)))
    return np.concatenate([r["out"] for r in res.results], axis=0)


# version 07
# speedup vs baseline: 1.2163x; 1.2163x over previous
import numpy as np
from contextlib import ExitStack
import concourse.bass as bass
import concourse.mybir as mybir
from concourse.bass_utils import run_bass_kernel_spmd

F32 = mybir.dt.float32
BF16 = mybir.dt.bfloat16
I32 = mybir.dt.int32
AF = mybir.ActivationFunctionType
ALU = mybir.AluOpType
AX = mybir.AxisListType
AP = bass.AP

B, S, D = 16, 2048, 1024
NCORE = 8
SEQ_PER_CORE = B // NCORE
DT = D // 128
TCH = 512
NTCH = S // TCH
NH, HD = 16, 64
NPAIR = NH // 2
EPS_LN = 1e-5
EPS_RMS = 1e-8
EPS_RENORM = 1e-8
INNER_LR = 0.01
QD = 1.0 / 127.0
H0, H1, H1A = 3072, 2816, 4096

ENGINES = ("tensor", "vector", "scalar", "gpsimd", "sync")
SUBLAYERS = ["l0a", "l0f1", "l0b", "l0f2", "l1a", "l1f1", "l1b", "l1f2"]
TTT_STAGES = 99
DBG = None


class Sem:
    __slots__ = ("h", "n", "name", "is_dma")

    def __init__(self, handle, name):
        self.h = handle
        self.n = 0
        self.name = name
        self.is_dma = False


class Prog:
    def __init__(self, nc):
        self.nc = nc
        self.q = {e: [] for e in ENGINES}
        self.es = ExitStack()
        self.mark = {}

    def sem(self, name):
        if not hasattr(self, "_sems"):
            self._sems = {}
        if name not in self._sems:
            self._sems[name] = Sem(
                self.es.enter_context(self.nc.semaphore(name)), name)
        return self._sems[name]

    def sbuf(self, name, shape, dtype):
        return self.es.enter_context(self.nc.sbuf_tensor(name, shape, dtype))

    def psum(self, name, shape, dtype=F32):
        return self.es.enter_context(self.nc.psum_tensor(name, shape, dtype))

    def emit(self, eng, fn):
        self.q[eng].append(fn)

    def op(self, eng, method, *args, inc=None, **kw):
        if inc is not None:
            inc[0].n += inc[1]

        def fn(e, method=method, args=args, kw=kw, inc=inc):
            if method == "_recip_nr":
                out, in_, scratch = args
                import concourse.dve_ops as dve_ops
                inst = e._custom_dve(dve_ops.RECIPROCAL_APPROX_NR, out=out,
                                     in0=in_, in1=scratch, s0=2.0)
            else:
                inst = getattr(e, method)(*args, **kw)
            if inc is not None:
                inst.then_inc(inc[0].h, inc[1])
        self.emit(eng, fn)

    def wait(self, eng, sem, val=None):
        v = sem.n if val is None else val
        if v <= 0:
            return
        self.emit(eng, lambda e, s=sem.h, v=v: e.wait_ge(s, v))

    def dma(self, qname, out, in_, sem, **kw):
        sem.is_dma = True
        self.op(qname, "dma_start", out, in_, inc=(sem, 16), **kw)

    def set_mark(self, key, sem):
        self.mark[key] = (sem, sem.n)

    def wait_mark(self, eng, key):
        if key in self.mark:
            sem, n = self.mark[key]
            self.wait(eng, sem, n)

    def finalize(self, block):
        for eng in ENGINES:
            fns = self.q[eng]
            if not fns:
                continue

            def body(e, fns=fns):
                for fn in fns:
                    fn(e)
            getattr(block, eng)(body)


def _sublayer_barrier(c):
    P = c.P
    for eng in ("tensor", "vector", "scalar", "sync", "gpsimd"):
        for s in (c.s_t, c.s_v, c.s_s):
            P.wait(eng, s)
        for s in list(P._sems.values()):
            if s.is_dma:
                P.wait(eng, s)



_UNIQ = [0]


def _u(name):
    _UNIQ[0] += 1
    return f"{name}_{_UNIQ[0]}"


def host_constants():
    ident = np.eye(128, dtype=np.float32)
    inv_freq = 1.0 / (10000.0 ** (np.arange(0, HD, 2, dtype=np.float64) / HD))
    t = np.arange(S, dtype=np.float64)
    emb = t[:, None] * np.concatenate([inv_freq, inv_freq])[None, :]
    cos = np.cos(emb).astype(np.float32)
    sin = np.sin(emb).astype(np.float32)
    sgn = np.where(np.arange(HD) % 2 == 0, -1.0, 1.0).astype(np.float32)
    sin_eff = sin * sgn[None, :]
    cos_fm = np.concatenate([cos.T, cos.T], axis=0).copy()
    sin_fm = np.concatenate([sin_eff.T, sin_eff.T], axis=0).copy()
    perm = np.zeros((128, 128), dtype=np.float32)
    for i in range(128):
        perm[i ^ 1, i] = 1.0
    mask_u = np.triu(np.ones((128, 128), dtype=np.float32), 1)
    mask_l = np.tril(np.ones((128, 128), dtype=np.float32), -1)
    return {"c_ident": ident, "c_cos": cos_fm, "c_sin": sin_fm,
            "c_perm": perm, "c_mask_u": mask_u, "c_mask_l": mask_l}


def fmc(d, t0, n):
    return slice(d * S + t0, d * S + t0 + n)


class Ctx:
    pass


def build_program(active=None):
    if active is None:
        active = SUBLAYERS
    nc = bass.Bass("TRN2", target_bir_lowering=False, debug=False)
    P = Prog(nc)
    c = Ctx()
    c.nc, c.P = nc, P

    def din(name, shape, dtype=F32):
        return nc.dram_tensor(name, list(shape), dtype, kind="ExternalInput").ap()

    c.x_in = din("L0_attn_a_x", (SEQ_PER_CORE, S, D))
    c.k0 = din("L0_attn_a_kernel", (3, D, D))
    c.l0b_w = {n: din("L0_attn_b_" + n + "_w", (D, D)) for n in "qkvo"}
    c.l0b_b = {n: din("L0_attn_b_" + n + "_b", (D,)) for n in "qkvo"}
    c.l0b_qn = din("L0_attn_b_qn_scale", (D,))
    c.l0b_kn = din("L0_attn_b_kn_scale", (D,))
    c.l0f = {"w_gate": din("L0_ffn_w_gate", (H0, D), I32),
             "w_up": din("L0_ffn_w_up", (H0, D), I32),
             "w_down": din("L0_ffn_w_down", (D, H0), I32)}
    c.l1a_w = {n: din("L1_attn_a_" + n, (D, D)) for n in ("wq", "wk", "wv", "wo")}
    c.l1a_w1 = din("L1_attn_a_w1", (H1A, D))
    c.l1a_w2 = din("L1_attn_a_w2", (D, H1A))
    c.l1a_v = {n: din("L1_attn_a_" + n, (D,)) for n in
               ("ln1_w", "ln1_b", "ln2_w", "ln2_b", "b2")}
    c.l1a_b1 = din("L1_attn_a_b1", (H1A,))
    c.l1a_lr = din("L1_attn_a_lr_scale", (NH, 1, 1))
    c.l1b_win = din("L1_attn_b_w_in", (D, 3 * D))
    c.l1b_bin = din("L1_attn_b_b_in", (3 * D,))
    c.l1b_wout = din("L1_attn_b_w_out", (D, D))
    c.l1b_bout = din("L1_attn_b_b_out", (D,))
    c.l1f = {"w_gate": din("L1_ffn_w_gate", (H1, D)),
             "w_up": din("L1_ffn_w_up", (H1, D)),
             "w_down": din("L1_ffn_w_down", (D, H1))}
    c.consts_np = host_constants()
    c.consts = {n: din(n, v.shape) for n, v in c.consts_np.items()}
    c.out = nc.dram_tensor("out", [SEQ_PER_CORE, S, D], F32,
                           kind="ExternalOutput").ap()
    c.dbg = nc.dram_tensor("dbg", [128, DT * S], F32,
                           kind="ExternalOutput").ap() if DBG else None

    def dsc(name, shape, dtype=BF16):
        return nc.dram_tensor(name, list(shape), dtype, kind="Internal").ap()

    c.st = {}
    for n in "qkvo":
        c.st["l0b_" + n] = dsc("st_l0b_" + n, (D, D))
        c.st["l0b_" + n + "T"] = dsc("st_l0b_" + n + "T", (D, D))
    for n in ("wq", "wk", "wv", "wo"):
        c.st["l1a_" + n] = dsc("st_l1a_" + n, (D, D))
        c.st["l1a_" + n + "T"] = dsc("st_l1a_" + n + "T", (D, D))
    c.st["l1a_w1"] = dsc("st_l1a_w1", (H1A, D))
    c.st["l1a_w1T"] = dsc("st_l1a_w1T", (D, H1A))
    c.st["l1a_w2"] = dsc("st_l1a_w2", (D, H1A))
    c.st["l1a_w2T"] = dsc("st_l1a_w2T", (H1A, D))
    c.st["l0f_g"] = dsc("st_l0f_g", (H0, D))
    c.st["l0f_gT"] = dsc("st_l0f_gT", (D, H0))
    c.st["l0f_u"] = dsc("st_l0f_u", (H0, D))
    c.st["l0f_uT"] = dsc("st_l0f_uT", (D, H0))
    c.st["l0f_d"] = dsc("st_l0f_d", (D, H0))
    c.st["l0f_dT"] = dsc("st_l0f_dT", (H0, D))
    c.st["l0a_k"] = dsc("st_l0a_k", (3, D, D))
    c.st["l1b_win"] = dsc("st_l1b_win", (D, 3 * D))
    c.st["l1b_wout"] = dsc("st_l1b_wout", (D, D))
    c.st["l1f_gT"] = dsc("st_l1f_gT", (D, H1))
    c.st["l1f_uT"] = dsc("st_l1f_uT", (D, H1))
    c.st["l1f_dT"] = dsc("st_l1f_dT", (H1, D))

    c.s_t = P.sem("s_t")
    c.s_v = P.sem("s_v")
    c.s_s = P.sem("s_s")
    c.s_stage = P.sem("s_stage")
    c.s_const = P.sem("s_const")
    c.s_gp = P.sem("s_gp")

    c.XR = P.sbuf("XR", [128, DT * S], F32)
    c.XT = P.sbuf("XT", [128, DT * S], BF16)
    c.CID = P.sbuf("CID", [128, 128], F32)
    c.CIDB = P.sbuf("CIDB", [128, 128], BF16)
    c.CPERM = P.sbuf("CPERM", [128, 128], F32)
    c.ONES = P.sbuf("ONES", [128, 64], F32)
    c.ONER = P.sbuf("ONER", [1, 128], F32)
    c.BIAS = P.sbuf("BIAS", [128, 192], F32)
    c.MASKU = P.sbuf("MASKU", [128, 128], F32)
    c.MASKL = P.sbuf("MASKL", [128, 128], F32)
    c.CNST = P.sbuf("CNST", [128, 4], F32)
    c.ROWA = P.sbuf("ROWA", [1, TCH], F32)
    c.ROWB = P.sbuf("ROWB", [1, TCH], F32)
    c.ln_bufs = None

    for src, dst in (
        (c.l0b_w["q"], c.st["l0b_q"]), (c.l0b_w["k"], c.st["l0b_k"]),
        (c.l0b_w["v"], c.st["l0b_v"]), (c.l0b_w["o"], c.st["l0b_o"]),
        (c.l1a_w["wq"], c.st["l1a_wq"]), (c.l1a_w["wk"], c.st["l1a_wk"]),
        (c.l1a_w["wv"], c.st["l1a_wv"]), (c.l1a_w["wo"], c.st["l1a_wo"]),
        (c.l1a_w1, c.st["l1a_w1"]), (c.l1a_w2, c.st["l1a_w2"]),
        (c.l0f["w_gate"], c.st["l0f_g"]), (c.l0f["w_up"], c.st["l0f_u"]),
        (c.l0f["w_down"], c.st["l0f_d"]),
        (c.k0, c.st["l0a_k"]),
        (c.l1b_win, c.st["l1b_win"]), (c.l1b_wout, c.st["l1b_wout"]),
    ):
        P.dma("gpsimd", dst[...], src[...], c.s_stage)

    P.dma("sync", c.CID[:, :], c.consts["c_ident"][:, :], c.s_const)
    P.dma("sync", c.CPERM[:, :], c.consts["c_perm"][:, :], c.s_const)
    P.dma("sync", c.MASKU[:, :], c.consts["c_mask_u"][:, :], c.s_const)
    P.dma("sync", c.MASKL[:, :], c.consts["c_mask_l"][:, :], c.s_const)
    P.op("gpsimd", "memset", c.CNST[:, 0:1], EPS_LN)
    P.op("gpsimd", "memset", c.CNST[:, 1:2], EPS_RMS)
    P.op("gpsimd", "memset", c.CNST[:, 2:3], EPS_RENORM)
    P.op("gpsimd", "memset", c.ONES[:, :], 1.0)
    P.op("gpsimd", "memset", c.ONER[:, :], 1.0, inc=(c.s_gp, 1))
    P.wait("vector", c.s_const)
    P.op("vector", "tensor_copy", c.CIDB[:, :], c.CID[:, :], inc=(c.s_v, 1))

    _emit_bias_pack(c)

    _emit_pretranspose_weights(c, active)
    if ("l1f1" in active) or ("l1f2" in active):
        _emit_l1f_wquant(c)

    for eng in ("tensor", "vector", "scalar"):
        P.wait(eng, c.s_gp)
        P.wait(eng, c.s_const)

    for seq in range(SEQ_PER_CORE):
        c.seq = seq
        _emit_load_x(c)
        if DBG and seq == 0 and DBG[1] == -1:
            _emit_dbg_dump(c, DBG[0])
        for idx_, name in enumerate(active):
            c.tag = f"s{seq}{name}"
            if name in ("l0f1", "l0f2"):
                _emit_ln(c)
                _emit_l0ffn(c)
            elif name == "l0a":
                _emit_ln(c)
                _emit_l0a(c)
            elif name == "l0b":
                _emit_ln(c)
                _emit_l0b(c)
            elif name == "l1b":
                _emit_ln(c)
                _emit_l1b(c)
            elif name in ("l1f1", "l1f2"):
                _emit_l1ffn(c)
            elif name == "l1a":
                _emit_l1a(c)
            if DBG and seq == 0 and DBG[1] == idx_:
                _emit_dbg_dump(c, DBG[0])
        _emit_store_x(c)

    for s in P._sems.values():
        if s.name.startswith(("s_o", "s_x", "s_vstg", "s_const", "s_stage",
                              "s_w")):
            P.wait("sync", s)
    with P.es:
        blk = P.es.enter_context(nc.Block())
        P.finalize(blk)
    return c


def _emit_bias_pack(c):
    P, nc = c.P, c.nc
    vec_srcs = [
        (c.l0b_b["q"], 0, 8), (c.l0b_b["k"], 8, 8), (c.l0b_b["v"], 16, 8),
        (c.l0b_b["o"], 24, 8), (c.l0b_qn, 32, 8), (c.l0b_kn, 40, 8),
        (c.l1a_v["ln1_w"], 48, 8), (c.l1a_v["ln1_b"], 56, 8),
        (c.l1a_v["ln2_w"], 64, 8), (c.l1a_v["ln2_b"], 72, 8),
        (c.l1a_b1, 80, 32), (c.l1a_v["b2"], 112, 8),
        (c.l1b_bin, 120, 24), (c.l1b_bout, 144, 8),
    ]
    P.wait("tensor", c.s_gp)
    P.wait("tensor", c.s_const)
    es = ExitStack()
    VS = es.enter_context(nc.sbuf_tensor(_u("VSTG"), [32, 128], F32))
    LR = es.enter_context(nc.sbuf_tensor(_u("LRROW"), [1, 16], F32))
    PSV = es.enter_context(nc.psum_tensor(_u("PS_V"), [128, 512], F32))
    with es:
        s_vstg = P.sem("s_vstg")
        for src, col0, n in vec_srcs:
            P.wait_mark("sync", "VSTG")
            P.dma("sync", VS[0:n, :], src.rearrange("(a p) -> a p", p=128), s_vstg)
            P.wait("tensor", s_vstg)
            P.wait_mark("tensor", "PSV")
            P.op("tensor", "transpose", PSV[:, 0:n], VS[0:n, :],
                 c.CID[0:n, 0:n], inc=(c.s_t, 1))
            P.set_mark("VSTG", c.s_t)
            P.wait("scalar", c.s_t)
            P.op("scalar", "copy", c.BIAS[:, col0:col0 + n], PSV[:, 0:n],
                 inc=(c.s_s, 1))
            P.set_mark("PSV", c.s_s)
        P.wait_mark("sync", "VSTG")
        P.dma("sync", LR[0:1, :], c.l1a_lr.rearrange("h a b -> (a b) h"), s_vstg)
        P.wait("tensor", s_vstg)
        P.wait_mark("tensor", "PSV")
        P.op("tensor", "matmul", PSV[:, 16:32], c.ONER[0:1, :], LR[0:1, :],
             start=True, stop=True)
        P.op("tensor", "matmul", PSV[0:64, 32:40], c.ONER[0:1, 0:64],
             LR[0:1, 0:8], start=True, stop=True, inc=(c.s_t, 1))
        P.op("gpsimd", "memset", c.BIAS[0:64, 184:185], 1.0)
        P.op("gpsimd", "memset", c.BIAS[64:128, 184:185], 0.0)
        P.op("gpsimd", "memset", c.BIAS[0:64, 185:186], 0.0)
        P.op("gpsimd", "memset", c.BIAS[64:128, 185:186], 1.0,
             inc=(c.s_gp, 1))
        P.wait("scalar", c.s_t)
        P.op("scalar", "mul", c.BIAS[:, 152:168], PSV[:, 16:32], INNER_LR,
             inc=(c.s_s, 1))
        P.wait("scalar", c.s_s)
        P.op("scalar", "mul", c.BIAS[:, 168:184], PSV[:, 16:32], -INNER_LR,
             inc=(c.s_s, 1))
        P.set_mark("PSV", c.s_s)


def bias_col(c, col):
    return c.BIAS[:, col:col + 1]


def _emit_dbg_dump(c, which):
    P, nc = c.P, c.nc
    _sublayer_barrier(c)
    buf = {"xt": c.XT, "xr": c.XR}.get(which)
    es = ExitStack()
    TMP = es.enter_context(nc.sbuf_tensor(_u("dbgt"), [128, 2048], F32))
    with es:
        for d in range(DT):
            P.wait_mark("vector", "dbgtmp")
            P.op("vector", "tensor_copy", TMP[:, :], buf[:, fmc(d, 0, S)],
                 inc=(c.s_v, 1))
            s_od = P.sem("s_odbg")
            P.wait("sync", c.s_v)
            P.dma("sync", c.dbg[:, d * S:(d + 1) * S], TMP[:, :], s_od)
            P.set_mark("dbgtmp", s_od)


def _emit_load_x(c):
    P, nc = c.P, c.nc
    _sublayer_barrier(c)
    es = ExitStack()
    TMP = es.enter_context(nc.sbuf_tensor(_u(f"ldx{c.seq}"), [128, 2 * D], F32))
    PS = es.enter_context(nc.psum_tensor(_u(f"ldxp{c.seq}"), [128, 2048], F32))
    with es:
        for tt in range(S // 128):
            pr = tt % 2
            s_x = P.sem(f"s_x{pr}")
            P.wait_mark("sync", f"ldx{pr}")
            P.dma("sync", TMP[:, pr * D:(pr + 1) * D],
                  c.x_in[c.seq, tt * 128:(tt + 1) * 128, :], s_x)
            P.wait("tensor", s_x)
            P.wait_mark("tensor", f"ldxp{pr}")
            for d in range(DT):
                P.op("tensor", "transpose", PS[:, (pr * 8 + d) * 128:(pr * 8 + d + 1) * 128],
                     TMP[:, (pr * D) + d * 128:(pr * D) + (d + 1) * 128], c.CID[:, :],
                     inc=(c.s_t, 1) if d == DT - 1 else None)
            P.set_mark(f"ldx{pr}", c.s_t)
            P.wait("vector", c.s_t)
            for d in range(DT):
                P.op("vector", "tensor_copy", c.XR[:, fmc(d, tt * 128, 128)],
                     PS[:, (pr * 8 + d) * 128:(pr * 8 + d + 1) * 128],
                     inc=(c.s_v, 1) if d == DT - 1 else None)
            P.set_mark(f"ldxp{pr}", c.s_v)


def _emit_store_x(c):
    P, nc = c.P, c.nc
    _sublayer_barrier(c)
    es = ExitStack()
    TMP = es.enter_context(nc.sbuf_tensor(_u(f"stx{c.seq}"), [128, 2 * D], F32))
    PS = es.enter_context(nc.psum_tensor(_u(f"stxp{c.seq}"), [128, 2048], F32))
    with es:
        for tt in range(S // 128):
            pr = tt % 2
            P.wait_mark("tensor", f"stxp{pr}")
            for d in range(DT):
                P.op("tensor", "transpose", PS[:, (pr * 8 + d) * 128:(pr * 8 + d + 1) * 128],
                     c.XR[:, fmc(d, tt * 128, 128)], c.CID[:, :],
                     inc=(c.s_t, 1) if d == DT - 1 else None)
            P.wait("vector", c.s_t)
            P.wait_mark("vector", f"stx{pr}")
            for d in range(DT):
                P.op("vector", "tensor_copy", TMP[:, pr * D + d * 128:pr * D + (d + 1) * 128],
                     PS[:, (pr * 8 + d) * 128:(pr * 8 + d + 1) * 128],
                     inc=(c.s_v, 1) if d == DT - 1 else None)
            P.set_mark(f"stxp{pr}", c.s_v)
            s_o = P.sem(f"s_o{pr}")
            P.wait("sync", c.s_v)
            P.dma("sync", c.out[c.seq, tt * 128:(tt + 1) * 128, :],
                  TMP[:, pr * D:(pr + 1) * D], s_o)
            P.set_mark(f"stx{pr}", s_o)


def _emit_ln(c, src=None, dst=None, w_col=None, b_col=None, dst_of=None,
             t_lo=0, t_hi=S, barrier=True):
    P, nc = c.P, c.nc
    if barrier:
        _sublayer_barrier(c)
    if src is None:
        src = c.XR
    if dst is None:
        dst = c.XT
    if dst_of is None:
        dst_of = lambda d, t0, n: dst[:, fmc(d, t0, n)]
    es = ExitStack()
    ext = getattr(c, "ln_bufs", None)
    if ext is None:
        SQ = es.enter_context(nc.sbuf_tensor(_u("ln_sq"), [128, 2 * TCH], F32))
        TM = es.enter_context(nc.sbuf_tensor(_u("ln_tm"), [128, TCH], F32))
        PSS = es.enter_context(nc.psum_tensor(_u("ln_pss"), [1, 1024], F32))
        PSB = es.enter_context(nc.psum_tensor(_u("ln_psb"), [128, 1024], F32))
    else:
        SQ, TM, PSS, PSB = ext
    RA = c.ROWA
    RB = c.ROWB
    with es:
        for t0 in range(t_lo, t_hi, TCH):
            for d in range(DT):
                pr = d % 2
                P.wait_mark("scalar", f"lnsq{pr}")
                P.op("scalar", "activation", SQ[:, pr * TCH:(pr + 1) * TCH],
                     src[:, fmc(d, t0, TCH)], AF.Square, inc=(c.s_s, 1))
                P.wait("tensor", c.s_s)
                if d == 0:
                    P.wait_mark("tensor", "lnpss")
                P.op("tensor", "matmul", PSS[0:1, 0:TCH], c.ONES[:, 0:1],
                     src[:, fmc(d, t0, TCH)], start=(d == 0), stop=(d == DT - 1))
                P.op("tensor", "matmul", PSS[0:1, TCH:2 * TCH], c.ONES[:, 0:1],
                     SQ[:, pr * TCH:(pr + 1) * TCH], start=(d == 0),
                     stop=(d == DT - 1), inc=(c.s_t, 1))
                P.set_mark(f"lnsq{pr}", c.s_t)
            P.wait("vector", c.s_t)
            P.op("vector", "tensor_scalar_mul", RA[0:1, :], PSS[0:1, 0:TCH],
                 1.0 / D, inc=(c.s_v, 1))
            P.wait("vector", c.s_v)
            P.op("vector", "scalar_tensor_tensor", RB[0:1, :], RA[0:1, :], 1.0,
                 RA[0:1, :], ALU.mult, ALU.mult, inc=(c.s_v, 1))
            P.wait("vector", c.s_v)
            P.op("vector", "scalar_tensor_tensor", RB[0:1, :],
                 PSS[0:1, TCH:2 * TCH], 1.0 / D, RB[0:1, :],
                 ALU.mult, ALU.subtract, inc=(c.s_v, 1))
            P.set_mark("lnpss", c.s_v)
            P.wait("scalar", c.s_v)
            P.op("scalar", "activation", RB[0:1, :], RB[0:1, :], AF.Sqrt,
                 bias=c.CNST[0:1, 0:1], scale=1.0, inc=(c.s_s, 1))
            P.wait("vector", c.s_s)
            P.op("vector", "reciprocal", RB[0:1, :], RB[0:1, :], inc=(c.s_v, 1))
            P.wait("vector", c.s_v)
            P.op("vector", "scalar_tensor_tensor", RA[0:1, :], RA[0:1, :], -1.0,
                 RB[0:1, :], ALU.mult, ALU.mult, inc=(c.s_v, 1))
            P.wait("tensor", c.s_v)
            P.wait_mark("tensor", "lnpsb")
            P.op("tensor", "matmul", PSB[:, 0:TCH], c.ONER[0:1, :], RB[0:1, :],
                 start=True, stop=True)
            P.op("tensor", "matmul", PSB[:, TCH:2 * TCH], c.ONER[0:1, :],
                 RA[0:1, :], start=True, stop=True, inc=(c.s_t, 1))
            P.wait("vector", c.s_t)
            for d in range(DT):
                P.wait("vector", c.s_v)
                P.op("vector", "tensor_mul", TM[:, :],
                     src[:, fmc(d, t0, TCH)], PSB[:, 0:TCH], inc=(c.s_v, 1))
                P.wait("vector", c.s_v)
                if w_col is None:
                    P.op("vector", "tensor_add", dst_of(d, t0, TCH),
                         TM[:, :], PSB[:, TCH:2 * TCH], inc=(c.s_v, 1))
                else:
                    P.op("vector", "tensor_add", TM[:, :], TM[:, :],
                         PSB[:, TCH:2 * TCH], inc=(c.s_v, 1))
                    P.wait("scalar", c.s_v)
                    P.op("scalar", "activation", dst_of(d, t0, TCH),
                         TM[:, :], AF.Identity, bias=bias_col(c, b_col + d),
                         scale=bias_col(c, w_col + d), inc=(c.s_s, 1))
                if w_col is not None:
                    P.wait("vector", c.s_s)
            if w_col is None:
                P.set_mark("lnpsb", c.s_v)
            else:
                P.set_mark("lnpsb", c.s_s)


def _emit_l0ffn(c):
    P, nc = c.P, c.nc
    _sublayer_barrier(c)
    NHT = H0 // 128
    GRP = 8
    NGRP = NHT // GRP
    es = ExitStack()
    WG = es.enter_context(nc.sbuf_tensor(_u("f0_wg"), [128, GRP * 1024], BF16))
    WU = es.enter_context(nc.sbuf_tensor(_u("f0_wu"), [128, GRP * 1024], BF16))
    WD = es.enter_context(nc.sbuf_tensor(_u("f0_wd"), [128, GRP * 1024], BF16))
    HGB = es.enter_context(nc.sbuf_tensor(_u("f0_hg"), [128, GRP * S], BF16))
    SGT = es.enter_context(nc.sbuf_tensor(_u("f0_sg"), [128, 2 * TCH], BF16))
    PSG = es.enter_context(nc.psum_tensor(_u("f0_psg"), [128, 1024], F32))
    PSU = es.enter_context(nc.psum_tensor(_u("f0_psu"), [128, 1024], F32))
    PSD = es.enter_context(nc.psum_tensor(_u("f0_psd"), [128, 1024], F32))
    with es:
        for g in range(NGRP):
            s_fw = P.sem("s_w0f")
            P.wait_mark("sync", "f0_w")
            _ld3(P, WG[:, :], 1024, c.st["l0f_gT"], H0, 0, DT, g * 1024, s_fw)
            _ld3(P, WU[:, :], 1024, c.st["l0f_uT"], H0, 0, DT, g * 1024, s_fw)
            _ld3(P, WD[:, :], 1024, c.st["l0f_dT"], D, g * GRP * 128, GRP, 0,
                 s_fw)
            w_ready = s_fw
            for h in range(GRP):
                for tc in range(NTCH):
                    pr = tc % 2
                    t0 = tc * TCH
                    P.wait("tensor", w_ready)
                    P.wait_mark("tensor", f"f0_psg{pr}")
                    for d in range(DT):
                        P.op("tensor", "matmul", PSG[:, pr * TCH:(pr + 1) * TCH],
                             WG[:, d * 1024 + h * 128:d * 1024 + (h + 1) * 128],
                             c.XT[:, fmc(d, t0, TCH)], start=(d == 0),
                             stop=(d == DT - 1),
                             inc=(c.s_t, 1) if d == DT - 1 else None)
                    P.wait_mark("tensor", f"f0_psu{pr}")
                    for d in range(DT):
                        P.op("tensor", "matmul", PSU[:, pr * TCH:(pr + 1) * TCH],
                             WU[:, d * 1024 + h * 128:d * 1024 + (h + 1) * 128],
                             c.XT[:, fmc(d, t0, TCH)], start=(d == 0),
                             stop=(d == DT - 1),
                             inc=(c.s_t, 1) if d == DT - 1 else None)
                    P.wait("scalar", c.s_t, c.s_t.n - 1)
                    P.wait_mark("scalar", f"f0_sg{pr}")
                    P.op("scalar", "activation", SGT[:, pr * TCH:(pr + 1) * TCH],
                         PSG[:, pr * TCH:(pr + 1) * TCH], AF.Silu, bias=0.0,
                         scale=QD, inc=(c.s_s, 1))
                    P.set_mark(f"f0_psg{pr}", c.s_s)
                    P.wait("vector", c.s_s)
                    P.wait("vector", c.s_t)
                    P.op("vector", "scalar_tensor_tensor",
                         HGB[:, h * S + t0:h * S + t0 + TCH],
                         SGT[:, pr * TCH:(pr + 1) * TCH], QD,
                         PSU[:, pr * TCH:(pr + 1) * TCH], ALU.mult, ALU.mult,
                         inc=(c.s_v, 1))
                    P.set_mark(f"f0_psu{pr}", c.s_v)
                    P.set_mark(f"f0_sg{pr}", c.s_v)
            h_ready = c.s_v.n
            for dcx in range(DT):
                for tc in range(NTCH):
                    pr = tc % 2
                    t0 = tc * TCH
                    P.wait("tensor", c.s_v, h_ready)
                    P.wait_mark("tensor", f"f0_psd{pr}")
                    for h in range(GRP):
                        P.op("tensor", "matmul", PSD[:, pr * TCH:(pr + 1) * TCH],
                             WD[:, h * 1024 + dcx * 128:h * 1024 + (dcx + 1) * 128],
                             HGB[:, h * S + t0:h * S + t0 + TCH], start=(h == 0),
                             stop=(h == GRP - 1),
                             inc=(c.s_t, 1) if h == GRP - 1 else None)
                    P.wait("vector", c.s_t)
                    P.op("vector", "scalar_tensor_tensor",
                         c.XR[:, fmc(dcx, t0, TCH)],
                         PSD[:, pr * TCH:(pr + 1) * TCH], QD,
                         c.XR[:, fmc(dcx, t0, TCH)], ALU.mult, ALU.add,
                         inc=(c.s_v, 1))
                    P.set_mark(f"f0_psd{pr}", c.s_v)
            P.set_mark("f0_w", c.s_t)


def _emit_attn_core(c, QH, KH, VH, pair, kind, proj_scale, es_parent,
                    out_hook=None):
    P, nc = c.P, c.nc
    QG = 512
    NQG = S // QG
    VT = c.a_VT
    RS = c.a_RS
    ES = c.a_ES
    MROW = c.a_MROW
    BIASM = c.a_BIASM
    RECR = c.a_RECR
    PVN = c.a_PVN
    NB = c.a_NB
    p_sc = c.p_sc
    p_pv = c.p_pv
    p_bc = c.p_bc
    p_mv = c.p_mv
    p_tr = c.p_tr
    p_sm = c.p_sm

    kTR = c.p_tr.name
    kSC = c.p_sc.name
    kSM = c.p_sm.name
    kBC = c.p_bc.name
    kPV = c.p_pv.name
    kMV = c.p_mv.name
    kRS = RS.name if RS is not None else None
    kES = ES.name
    P.wait("tensor", c.s_s)
    for kt in range(16):
        P.wait_mark("tensor", kTR)
        P.op("tensor", "transpose", p_tr[:, 0:128],
             VH[:, kt * 128:(kt + 1) * 128], c.CIDB[:, :], inc=(c.s_t, 1))
        P.wait("vector", c.s_t)
        P.op("vector", "tensor_copy", VT[:, kt * 130:kt * 130 + 64],
             p_tr[:, 0:64])
        P.op("vector", "tensor_copy", VT[:, kt * 130 + 65:kt * 130 + 129],
             p_tr[:, 64:128], inc=(c.s_v, 1))
        P.set_mark(kTR, c.s_v)

    for h in range(2):
        hb = h * 64
        for qg in range(NQG):
            q0 = qg * QG
            for kt in range(16):
                pr = kt % 2
                P.wait_mark("tensor", kSC + f"_{pr}")
                P.op("tensor", "matmul", p_sc[:, pr * 512:(pr + 1) * 512],
                     KH[hb:hb + 64, kt * 128:(kt + 1) * 128],
                     QH[hb:hb + 64, q0:q0 + QG], start=True, stop=True,
                     inc=(c.s_t, 1))
                if kt == 0:
                    P.wait_mark("scalar", kRS if kind == "softmax" else kES)
                    P.wait_mark("vector", kRS if kind == "softmax" else kES)
                if kind == "softmax":
                    if kt % 2 == 0:
                        P.wait("scalar", c.s_t)
                        P.op("scalar", "copy", RS[:, kt * 512:(kt + 1) * 512],
                             p_sc[:, pr * 512:(pr + 1) * 512], inc=(c.s_s, 1))
                        P.set_mark(kSC + f"_{pr}", c.s_s)
                    else:
                        P.wait("vector", c.s_t)
                        P.op("vector", "tensor_copy",
                             RS[:, kt * 512:(kt + 1) * 512],
                             p_sc[:, pr * 512:(pr + 1) * 512], inc=(c.s_v, 1))
                        P.set_mark(kSC + f"_{pr}", c.s_v)
                else:
                    P.wait("scalar", c.s_t)
                    P.op("scalar", "activation", ES[:, kt * 512:(kt + 1) * 512],
                         p_sc[:, pr * 512:(pr + 1) * 512], AF.Sigmoid,
                         bias=0.0, scale=proj_scale, inc=(c.s_s, 1))
                    P.set_mark(kSC + f"_{pr}", c.s_s)
            if kind == "softmax":
                P.wait("vector", c.s_s)
                P.wait("vector", c.s_v)
                P.op("vector", "tensor_reduce", MROW[:, 0:1], RS[:, :],
                     AX.X, ALU.max, inc=(c.s_v, 1))
                P.wait("tensor", c.s_v)
                P.wait_mark("tensor", kSM)
                P.op("tensor", "transpose", p_sm[0:1, 0:128], MROW[:, 0:1],
                     c.CID[:, :], inc=(c.s_t, 1))
                P.wait("vector", c.s_t)
                P.op("vector", "tensor_reduce", MROW[0:1, 1:2], p_sm[0:1, 0:128],
                     AX.X, ALU.max, negate=True, inc=(c.s_v, 1))
                P.wait("vector", c.s_v)
                P.op("vector", "tensor_scalar_mul", MROW[0:1, 2:3],
                     MROW[0:1, 1:2], proj_scale, inc=(c.s_v, 1))
                P.set_mark(kSM, c.s_v)
                P.wait("tensor", c.s_v)
                P.wait_mark("tensor", kBC)
                P.op("tensor", "matmul", p_bc[:, 0:1], c.ONER[0:1, :],
                     MROW[0:1, 2:3], start=True, stop=True, inc=(c.s_t, 1))
                P.wait("scalar", c.s_t)
                P.op("scalar", "copy", BIASM[:, 0:1], p_bc[:, 0:1],
                     inc=(c.s_s, 1))
                P.set_mark(kBC, c.s_s)
                P.wait("scalar", c.s_s)
                P.wait("scalar", c.s_v)
                P.op("scalar", "activation", ES[:, :], RS[:, :], AF.Exp,
                     bias=BIASM[:, 0:1], scale=proj_scale, inc=(c.s_s, 1))
                P.set_mark(kRS, c.s_s)
            P.wait("tensor", c.s_s)
            P.wait("tensor", c.s_v)
            P.wait_mark("tensor", kPV)
            for kt in range(16):
                P.op("tensor", "matmul", p_pv[0:65, :],
                     VT[:, kt * 130 + 65 * h:kt * 130 + 65 * h + 65],
                     ES[:, kt * 512:(kt + 1) * 512], start=(kt == 0),
                     stop=(kt == 15), inc=(c.s_t, 1) if kt == 15 else None)
            P.set_mark(kES, c.s_t)
            P.wait("vector", c.s_t)
            if kind == "sigmoid":
                P.op("vector", "tensor_scalar_add", RECR[64:65, :],
                     p_pv[64:65, :], EPS_RENORM, inc=(c.s_v, 1))
                P.wait("vector", c.s_v)
                P.op("vector", "reciprocal", RECR[64:65, :], RECR[64:65, :],
                     inc=(c.s_v, 1))
            else:
                P.op("vector", "reciprocal", RECR[64:65, :], p_pv[64:65, :],
                     inc=(c.s_v, 1))
            P.wait("tensor", c.s_v)
            P.wait_mark("tensor", kBC + "b")
            P.op("tensor", "matmul", p_bc[0:64, :], c.ONES[64:65, 0:64],
                 RECR[64:65, :], start=True, stop=True, inc=(c.s_t, 1))
            P.wait("scalar", c.s_t)
            P.op("scalar", "copy", PVN[0:64, :], p_pv[0:64, :], inc=(c.s_s, 1))
            P.set_mark(kPV, c.s_s)
            P.wait("vector", c.s_s)
            P.op("vector", "tensor_mul", PVN[0:64, :], PVN[0:64, :],
                 p_bc[0:64, :], inc=(c.s_v, 1))
            P.set_mark(kBC + "b", c.s_v)
            if out_hook is not None:
                out_hook(h, qg, PVN)
            elif h == 0:
                P.wait("vector", c.s_v)
                P.op("vector", "tensor_add", c.XR[0:64, fmc(pair, q0, QG)],
                     c.XR[0:64, fmc(pair, q0, QG)], PVN[0:64, :],
                     inc=(c.s_v, 1))
            else:
                P.wait("vector", c.s_v)
                P.op("vector", "tensor_copy", NB[0:64, :], PVN[0:64, :],
                     inc=(c.s_v, 1))
                P.wait("tensor", c.s_v)
                P.wait_mark("tensor", kMV)
                P.op("tensor", "matmul", p_mv[64:128, :], c.CIDB[0:64, 0:64],
                     NB[0:64, :], start=True, stop=True,
                     tile_position=(0, 64), inc=(c.s_t, 1))
                P.wait("vector", c.s_t)
                P.op("vector", "tensor_add", c.XR[64:128, fmc(pair, q0, QG)],
                     c.XR[64:128, fmc(pair, q0, QG)], p_mv[64:128, :],
                     inc=(c.s_v, 1))
                P.set_mark(kMV, c.s_v)


def _alloc_attn_scratch(c, es, softmax=True):
    P, nc = c.P, c.nc
    c.a_VT = es.enter_context(nc.sbuf_tensor(_u("aVT"), [128, 16 * 130], BF16))
    c.a_RS = es.enter_context(nc.sbuf_tensor(_u("aRS"), [128, 16 * 512], F32)) \
        if softmax else None
    c.a_ES = es.enter_context(nc.sbuf_tensor(_u("aES"), [128, 16 * 512], BF16))
    c.a_MROW = es.enter_context(nc.sbuf_tensor(_u("aMR"), [128, 8], F32))
    c.a_BIASM = es.enter_context(nc.sbuf_tensor(_u("aBM"), [128, 2], F32))
    c.a_PVN = es.enter_context(nc.sbuf_tensor(_u("aPV"), [128, 512], F32))
    c.a_RECR = c.a_PVN
    c.a_NB = es.enter_context(nc.sbuf_tensor(_u("aNB"), [128, 512], BF16))
    c.p_sc = es.enter_context(nc.psum_tensor(_u("psc"), [128, 1024], F32))
    c.p_pv = es.enter_context(nc.psum_tensor(_u("ppv"), [128, 512], F32))
    c.p_bc = es.enter_context(nc.psum_tensor(_u("pbc"), [128, 512], F32))
    c.p_mv = es.enter_context(nc.psum_tensor(_u("pmv"), [128, 512], F32))
    c.p_tr = es.enter_context(nc.psum_tensor(_u("ptr"), [128, 1024], BF16))
    c.p_sm = es.enter_context(nc.psum_tensor(_u("psm"), [128, 512], F32))
    P.wait("vector", c.s_v)
    vt = c.a_VT[:, :]
    P.op("vector", "memset",
         AP(vt.tensor, vt.offset + 64, [vt.ap[0], [130, 16], [1, 1]]), 1.0)
    P.op("vector", "memset",
         AP(vt.tensor, vt.offset + 129, [vt.ap[0], [130, 16], [1, 1]]), 1.0,
         inc=(c.s_v, 1))


def _emit_proj_pair(c, Wsl, dst, scale_col=None, bias_col_=None, s_w=None,
                    neg=False):
    P = c.P
    kSC = c.p_sc.name
    for tc in range(NTCH):
        pr = tc % 2
        t0 = tc * TCH
        if s_w is not None:
            P.wait("tensor", s_w)
        P.wait_mark("tensor", kSC + f"_{pr}")
        for d in range(DT):
            P.op("tensor", "matmul", c.p_sc[:, pr * 512:(pr + 1) * 512],
                 Wsl[:, d * 128:(d + 1) * 128], c.XT[:, fmc(d, t0, TCH)],
                 start=(d == 0), stop=(d == DT - 1),
                 inc=(c.s_t, 1) if d == DT - 1 else None)
        P.wait("scalar", c.s_t)
        P.op("scalar", "activation", dst[:, t0:t0 + TCH],
             c.p_sc[:, pr * 512:(pr + 1) * 512], AF.Identity,
             bias=bias_col_ if bias_col_ is not None else 0.0,
             scale=scale_col if scale_col is not None else 1.0,
             inc=(c.s_s, 1))
        P.set_mark(kSC + f"_{pr}", c.s_s)


def _emit_l0a(c):
    P, nc = c.P, c.nc
    _sublayer_barrier(c)
    es = ExitStack()
    QH = es.enter_context(nc.sbuf_tensor(_u("l0aQ"), [128, S], BF16))
    KH = es.enter_context(nc.sbuf_tensor(_u("l0aK"), [128, S], BF16))
    VH = es.enter_context(nc.sbuf_tensor(_u("l0aV"), [128, S], BF16))
    WP = es.enter_context(nc.sbuf_tensor(_u("l0aW"), [128, DT * 128], BF16))
    with es:
        _alloc_attn_scratch(c, es, softmax=True)
        s_w = P.sem("s_wl0a")
        for pair in range(NPAIR):
            c.tag = f"s{c.seq}l0a_p{pair}"
            for i, dst in enumerate((QH, KH, VH)):
                P.wait_mark("sync", "l0aW")
                ksrc = c.st["l0a_k"]
                in3 = AP(ksrc.tensor, ksrc.offset + i * D * D + pair * 128,
                         [[D, 128], [128 * D, DT], [1, 128]])
                P.dma("sync", WP[:, :].rearrange("p (t q) -> p t q", q=128),
                      in3, s_w)
                _emit_proj_pair(c, WP[:, :], dst, s_w=s_w)
                P.set_mark("l0aW", c.s_t)
            _emit_attn_core(c, QH, KH, VH, pair, "softmax", 0.125, es)


def _emit_oproj(c, AO, WO, bout_col, s_w):
    P, nc = c.P, c.nc
    es = ExitStack()
    TMO = es.enter_context(nc.sbuf_tensor(_u("oprT"), [128, TCH], F32))
    p_o = es.enter_context(nc.psum_tensor(_u("opr"), [128, 1024], F32))
    with es:
        for dcx in range(DT):
            for tc in range(NTCH):
                pr = tc % 2
                t0 = tc * TCH
                P.wait("tensor", s_w)
                P.wait_mark("tensor", p_o.name + f"_{pr}")
                for d in range(DT):
                    P.op("tensor", "matmul", p_o[:, pr * 512:(pr + 1) * 512],
                         WO[:, d * 1024 + dcx * 128:d * 1024 + (dcx + 1) * 128],
                         AO[:, fmc(d, t0, TCH)], start=(d == 0),
                         stop=(d == DT - 1),
                         inc=(c.s_t, 1) if d == DT - 1 else None)
                P.wait("scalar", c.s_t)
                P.wait_mark("scalar", TMO.name)
                P.op("scalar", "activation", TMO[:, :],
                     p_o[:, pr * 512:(pr + 1) * 512], AF.Identity,
                     bias=bias_col(c, bout_col + dcx), scale=1.0,
                     inc=(c.s_s, 1))
                P.set_mark(p_o.name + f"_{pr}", c.s_s)
                P.wait("vector", c.s_s)
                P.op("vector", "tensor_add", c.XR[:, fmc(dcx, t0, TCH)],
                     c.XR[:, fmc(dcx, t0, TCH)], TMO[:, :], inc=(c.s_v, 1))
                P.set_mark(TMO.name, c.s_v)


def _emit_l1b(c):
    P, nc = c.P, c.nc
    _sublayer_barrier(c)
    es = ExitStack()
    QH = es.enter_context(nc.sbuf_tensor(_u("l1bQ"), [128, S], BF16))
    KH = es.enter_context(nc.sbuf_tensor(_u("l1bK"), [128, S], BF16))
    VH = es.enter_context(nc.sbuf_tensor(_u("l1bV"), [128, S], BF16))
    WP = es.enter_context(nc.sbuf_tensor(_u("l1bW"), [128, DT * 128], BF16))
    AO = es.enter_context(nc.sbuf_tensor(_u("l1bAO"), [128, DT * S], BF16))
    with es:
        es2 = ExitStack()
        _alloc_attn_scratch(c, es2, softmax=True)
        s_w = P.sem("s_wl1b")

        def hook(h, qg, PVN):
            q0 = qg * 512
            pair = c.cur_pair
            P.wait("vector", c.s_v)
            if h == 0:
                P.op("vector", "tensor_copy", AO[0:64, fmc(pair, q0, 512)],
                     PVN[0:64, :], inc=(c.s_v, 1))
            else:
                P.op("vector", "tensor_copy", c.a_NB[0:64, :], PVN[0:64, :],
                     inc=(c.s_v, 1))
                P.wait("tensor", c.s_v)
                P.wait_mark("tensor", c.p_mv.name)
                P.op("tensor", "matmul", c.p_mv[64:128, :], c.CIDB[0:64, 0:64],
                     c.a_NB[0:64, :], start=True, stop=True,
                     tile_position=(0, 64), inc=(c.s_t, 1))
                P.wait("vector", c.s_t)
                P.op("vector", "tensor_copy", AO[64:128, fmc(pair, q0, 512)],
                     c.p_mv[64:128, :], inc=(c.s_v, 1))
                P.set_mark(c.p_mv.name, c.s_v)

        with es2:
            for pair in range(NPAIR):
                c.cur_pair = pair
                c.tag = f"s{c.seq}l1b_p{pair}"
                for i, (dst, bc) in enumerate(((QH, 120), (KH, 128),
                                               (VH, 136))):
                    P.wait_mark("sync", "l1bW")
                    _ld3(P, WP[:, :], 128, c.st["l1b_win"], 3 * D, 0, DT,
                         i * D + pair * 128, s_w)
                    _emit_proj_pair(c, WP[:, :], dst,
                                    bias_col_=bias_col(c, bc + pair), s_w=s_w)
                    P.set_mark("l1bW", c.s_t)
                _emit_attn_core(c, QH, KH, VH, pair, "softmax", 1.0, es2,
                                out_hook=hook)
        _sublayer_barrier(c)
        WO = es.enter_context(nc.sbuf_tensor(_u("l1bWO"), [128, DT * 1024], BF16))
        s_wo = P.sem("s_wl1bo")
        P.wait_mark("sync", "l1bWO")
        _ld3(P, WO[:, :], 1024, c.st["l1b_wout"], D, 0, DT, 0, s_wo)
        _emit_oproj(c, AO, WO, 144, s_wo)
        P.set_mark("l1bWO", c.s_t)


def _emit_l0b(c):
    P, nc = c.P, c.nc
    _sublayer_barrier(c)
    es = ExitStack()
    QH = es.enter_context(nc.sbuf_tensor(_u("l0bQ"), [128, S], BF16))
    KH = es.enter_context(nc.sbuf_tensor(_u("l0bK"), [128, S], BF16))
    VH = es.enter_context(nc.sbuf_tensor(_u("l0bV"), [128, S], BF16))
    QF = es.enter_context(nc.sbuf_tensor(_u("l0bQF"), [128, S], F32))
    CS = es.enter_context(nc.sbuf_tensor(_u("l0bCS"), [128, 2 * S], BF16))
    WP = es.enter_context(nc.sbuf_tensor(_u("l0bW"), [128, DT * 128], BF16))
    AO = es.enter_context(nc.sbuf_tensor(_u("l0bAO"), [128, DT * S], BF16))
    SQ = es.enter_context(nc.sbuf_tensor(_u("l0bSQ"), [128, 2 * TCH], F32))
    T1 = es.enter_context(nc.sbuf_tensor(_u("l0bT1"), [128, TCH], F32))
    T2 = es.enter_context(nc.sbuf_tensor(_u("l0bT2"), [128, TCH], F32))
    RRROW = es.enter_context(nc.sbuf_tensor(_u("l0bRw"), [1, S], F32))
    s_cs = P.sem("s_l0bcs")
    s_w = P.sem("s_wl0b")
    with es:
        es2 = ExitStack()
        _alloc_attn_scratch(c, es2, softmax=False)
        p_st = c.p_bc
        p_rr = c.p_mv
        p_rp = c.p_pv
        P.dma("gpsimd", CS[:, 0:S], c.consts["c_cos"][:, :], s_cs)
        P.dma("gpsimd", CS[:, S:2 * S], c.consts["c_sin"][:, :], s_cs)
        for tc in range(NTCH):
            t0 = tc * TCH
            for d in range(DT):
                pr = d % 2
                P.wait_mark("scalar", SQ.name + f"_{pr}")
                P.op("scalar", "activation", SQ[:, pr * TCH:(pr + 1) * TCH],
                     c.XT[:, fmc(d, t0, TCH)], AF.Square, inc=(c.s_s, 1))
                P.wait("tensor", c.s_s)
                if d == 0:
                    P.wait_mark("tensor", p_st.name)
                P.op("tensor", "matmul", p_st[0:1, 0:512], c.ONES[:, 0:1],
                     SQ[:, pr * TCH:(pr + 1) * TCH], start=(d == 0),
                     stop=(d == DT - 1), inc=(c.s_t, 1))
                P.set_mark(SQ.name + f"_{pr}", c.s_t)
            P.wait("vector", c.s_t)
            P.op("vector", "tensor_scalar_mul", RRROW[0:1, t0:t0 + TCH],
                 p_st[0:1, 0:512], 1.0 / D, inc=(c.s_v, 1))
            P.set_mark(p_st.name, c.s_v)
            P.wait("scalar", c.s_v)
            P.op("scalar", "activation", RRROW[0:1, t0:t0 + TCH],
                 RRROW[0:1, t0:t0 + TCH], AF.Sqrt, bias=c.CNST[0:1, 1:2],
                 scale=1.0, inc=(c.s_s, 1))
            P.wait("vector", c.s_s)
            P.op("vector", "reciprocal", RRROW[0:1, t0:t0 + TCH],
                 RRROW[0:1, t0:t0 + TCH], inc=(c.s_v, 1))
        rr_done = c.s_v.n
        P.wait("vector", s_cs)

        with es2:
            for pair in range(NPAIR):
                c.cur_pair = pair
                c.tag = f"s{c.seq}l0b_p{pair}"
                for side, dst, bcol in ((0, QH, 0), (1, KH, 8)):
                    wname = "qk"[side]
                    P.wait_mark("sync", "l0bW")
                    _ld3(P, WP[:, :], 128, c.st["l0b_" + wname + "T"], D, 0,
                         DT, pair * 128, s_w)
                    P.wait("scalar", s_w)
                    colbase = 32 if side == 0 else 40
                    for d in range(DT):
                        P.op("scalar", "activation",
                             WP[:, d * 128:(d + 1) * 128],
                             WP[:, d * 128:(d + 1) * 128],
                             AF.Identity, bias=0.0,
                             scale=bias_col(c, colbase + d), inc=(c.s_s, 1))
                    P.wait("tensor", c.s_s)
                    _emit_proj_pair(c, WP[:, :], QF)
                    P.set_mark("l0bW", c.s_t)
                    for tc in range(NTCH):
                        t0 = tc * TCH
                        P.wait("tensor", c.s_v, rr_done)
                        P.wait("tensor", c.s_s)
                        P.wait_mark("tensor", p_rr.name)
                        P.op("tensor", "matmul", p_rr[:, 0:512], c.ONER[0:1, :],
                             RRROW[0:1, t0:t0 + TCH], start=True, stop=True,
                             inc=(c.s_t, 1))
                        P.wait("vector", c.s_t)
                        P.wait_mark("vector", T1.name)
                        P.op("vector", "tensor_mul", T1[:, :],
                             QF[:, t0:t0 + TCH], p_rr[:, 0:512], inc=(c.s_v, 1))
                        P.set_mark(p_rr.name, c.s_v)
                        P.wait("vector", c.s_v)
                        P.op("vector", "tensor_scalar_add", T1[:, :], T1[:, :],
                             bias_col(c, bcol + pair), inc=(c.s_v, 1))
                        P.wait("tensor", c.s_v)
                        P.wait_mark("tensor", p_rp.name)
                        P.op("tensor", "matmul", p_rp[:, 0:512], c.CPERM[:, :],
                             T1[:, :], start=True, stop=True, inc=(c.s_t, 1))
                        P.wait("vector", c.s_t)
                        P.op("vector", "tensor_mul", T2[:, :], p_rp[:, 0:512],
                             CS[:, S + t0:S + t0 + TCH], inc=(c.s_v, 1))
                        P.set_mark(p_rp.name, c.s_v)
                        P.op("vector", "tensor_mul", T1[:, :], T1[:, :],
                             CS[:, t0:t0 + TCH], inc=(c.s_v, 1))
                        P.wait("vector", c.s_v)
                        P.op("vector", "tensor_add", dst[:, t0:t0 + TCH],
                             T1[:, :], T2[:, :], inc=(c.s_v, 1))
                        P.set_mark(T1.name, c.s_v)
                P.wait_mark("sync", "l0bW")
                _ld3(P, WP[:, :], 128, c.st["l0b_vT"], D, 0, DT, pair * 128,
                     s_w)
                _emit_proj_pair(c, WP[:, :], VH,
                                bias_col_=bias_col(c, 16 + pair), s_w=s_w)
                P.set_mark("l0bW", c.s_t)
                _emit_attn_core(c, QH, KH, VH, pair, "sigmoid", 0.125, es2,
                                out_hook=_mk_ao_hook(c, AO))
        _sublayer_barrier(c)
        WO = es.enter_context(nc.sbuf_tensor(_u("l0bWO"), [128, DT * 1024], BF16))
        s_wo = P.sem("s_wl0bo")
        P.wait_mark("sync", "l0bWO")
        _ld3(P, WO[:, :], 1024, c.st["l0b_oT"], D, 0, DT, 0, s_wo)
        _emit_oproj(c, AO, WO, 24, s_wo)
        P.set_mark("l0bWO", c.s_t)


def _mk_ao_hook(c, AO):
    P = c.P

    def hook(h, qg, PVN):
        q0 = qg * 512
        pair = c.cur_pair
        P.wait("vector", c.s_v)
        if h == 0:
            P.op("vector", "tensor_copy", AO[0:64, fmc(pair, q0, 512)],
                 PVN[0:64, :], inc=(c.s_v, 1))
        else:
            P.op("vector", "tensor_copy", c.a_NB[0:64, :], PVN[0:64, :],
                 inc=(c.s_v, 1))
            P.wait("tensor", c.s_v)
            P.wait_mark("tensor", c.p_mv.name)
            P.op("tensor", "matmul", c.p_mv[64:128, :], c.CIDB[0:64, 0:64],
                 c.a_NB[0:64, :], start=True, stop=True,
                 tile_position=(0, 64), inc=(c.s_t, 1))
            P.wait("vector", c.s_t)
            P.op("vector", "tensor_copy", AO[64:128, fmc(pair, q0, 512)],
                 c.p_mv[64:128, :], inc=(c.s_v, 1))
            P.set_mark(c.p_mv.name, c.s_v)
    return hook


def _emit_l1a(c):
    P, nc = c.P, c.nc
    _sublayer_barrier(c)
    es = ExitStack()
    XI = es.enter_context(nc.sbuf_tensor(_u("aXI"), [128, DT * S], F32))
    with es:
        _emit_ln(c, src=c.XR, dst=XI, barrier=False)
        _emit_l1a_ttt(c, XI)
        _emit_l1a_mlp(c, XI)
        _sublayer_barrier(c)
        for d in range(DT):
            for tc in range(NTCH):
                P.wait("vector", c.s_v)
                P.op("vector", "tensor_add",
                     c.XR[:, fmc(d, tc * TCH, TCH)],
                     c.XR[:, fmc(d, tc * TCH, TCH)],
                     XI[:, fmc(d, tc * TCH, TCH)], inc=(c.s_v, 1))


def _emit_l1a_ttt(c, XI):
    P, nc = c.P, c.nc
    _sublayer_barrier(c)
    GT = 512
    XT = c.XT
    QG0, KG0, VG0 = 4096, 8192, 12288
    KE0 = 0
    GM0 = 1024
    VT0 = 1536
    KT0 = 1792
    TO0 = 2048
    es = ExitStack()
    W16 = es.enter_context(nc.sbuf_tensor(_u("aW16"), [128, 8192], BF16))
    WO = es.enter_context(nc.sbuf_tensor(_u("aWO"), [128, 4 * 1024], BF16))
    U = es.enter_context(nc.sbuf_tensor(_u("aU"), [128, 1024], F32))
    UB = es.enter_context(nc.sbuf_tensor(_u("aUB"), [128, 1024], BF16))
    TML = es.enter_context(nc.sbuf_tensor(_u("aTML"), [128, TCH], F32))
    W0T = es.enter_context(nc.sbuf_tensor(_u("aW0T"), [128, 512], F32))
    W0B = es.enter_context(nc.sbuf_tensor(_u("aW0B"), [128, 1024], BF16))
    pA = es.enter_context(nc.psum_tensor(_u("apA"), [128, 1024], F32))
    pB = es.enter_context(nc.psum_tensor(_u("apB"), [128, 512], F32))
    pBT = es.enter_context(nc.psum_tensor(_u("apBT"), [128, 512], F32))
    pU = es.enter_context(nc.psum_tensor(_u("apU"), [128, 1024], F32))
    pT = es.enter_context(nc.psum_tensor(_u("apT"), [128, 2048], BF16))
    s_w = P.sem("s_wl1a")
    with es:
        P.wait("gpsimd", c.s_v)
        P.wait("gpsimd", c.s_s)
        P.op("gpsimd", "memset", W0T[:, :], 0.0)
        P.op("gpsimd", "memset", W0B[:, :], 0.0, inc=(c.s_gp, 1))
        P.wait("tensor", c.s_gp)
        P.wait("scalar", c.s_gp)
        P.wait("vector", c.s_gp)

        for g in range(S // GT):
            t0g = g * GT
            P.wait("scalar", c.s_v)
            P.wait("scalar", c.s_t)
            P.wait("vector", c.s_t)
            P.wait("vector", c.s_s)
            c.ln_bufs = (U, TML, pU[0:1, 0:1024], pA[:, 0:1024])
            _emit_ln(c, src=XI, w_col=48, b_col=56,
                     dst_of=lambda d, t0, n: XT[:, d * GT + (t0 - t0g):
                                                d * GT + (t0 - t0g) + n],
                     t_lo=t0g, t_hi=t0g + GT, barrier=False)
            c.ln_bufs = None
            for wname, dst0 in (("wq", QG0), ("wk", KG0), ("wv", VG0)):
                P.wait_mark("sync", "aW16")
                _ld3(P, W16[:, :], 1024, c.st["l1a_" + wname + "T"], D, 0,
                     DT, 0, s_w)
                P.wait("tensor", s_w)
                P.wait("tensor", c.s_s)
                for jt in range(DT):
                    prj = jt % 2
                    P.wait_mark("tensor", pA.name + str(prj))
                    for d in range(DT):
                        P.op("tensor", "matmul",
                             pA[:, prj * 512:(prj + 1) * 512],
                             W16[:, d * 1024 + jt * 128:d * 1024 + (jt + 1) * 128],
                             XT[:, d * GT:(d + 1) * GT], start=(d == 0),
                             stop=(d == DT - 1),
                             inc=(c.s_t, 1) if d == DT - 1 else None)
                    P.wait("scalar", c.s_t)
                    P.op("scalar", "copy", XT[:, dst0 + jt * GT:dst0 + (jt + 1) * GT],
                         pA[:, prj * 512:(prj + 1) * 512], inc=(c.s_s, 1))
                    P.set_mark(pA.name + str(prj), c.s_s)
                P.set_mark("aW16", c.s_t)
            for cc in range(GT // 128):
                gt0 = cc * 128
                _emit_ttt_chunk(c, XI, XT, W16, WO, U, UB, W0T, W0B,
                                pA, pB, pBT, pU, pT,
                                QG0, KG0, VG0, KE0, GM0, VT0, KT0, TO0,
                                g, gt0)


def _emit_ttt_chunk(c, XI, XT, W16, WO, U, UB, W0T, W0B,
                    pA, pB, pBT, pU, pT, QG0, KG0, VG0, KE0, GM0, VT0, KT0,
                    TO0, g, gt0):
    P = c.P
    GT = 512
    chunk_t0 = g * GT + gt0
    KZ0 = {0: 0, 1: 1024}
    GM = 2048
    VT = 2560
    KT = 2816
    TO = 3072
    P.wait("scalar", c.s_s)
    P.wait("scalar", c.s_t)
    P.wait("scalar", c.s_v)
    P.wait_mark("scalar", "aKZ")
    for par in range(2):
        for p in range(8):
            P.op("scalar", "activation",
                 XT[:, KZ0[par] + p * 128:KZ0[par] + (p + 1) * 128],
                 XT[:, KG0 + p * GT + gt0:KG0 + p * GT + gt0 + 128],
                 AF.Identity, bias=0.0, scale=bias_col(c, 184 + par),
                 inc=(c.s_s, 1))
    for hg in range(4):
        bk = hg % 2
        heads = [4 * hg + i for i in range(4)]
        P.wait("tensor", c.s_s)
        P.wait_mark("tensor", pA.name + str(bk))
        for i, h in enumerate(heads):
            par = h % 2
            p = h // 2
            P.op("tensor", "matmul",
                 pA[:, bk * 512 + i * 128:bk * 512 + (i + 1) * 128],
                 XT[:, KZ0[par] + p * 128:KZ0[par] + (p + 1) * 128],
                 XT[:, KG0 + p * GT + gt0:KG0 + p * GT + gt0 + 128],
                 start=True, stop=True, inc=(c.s_t, 1) if i == 3 else None)
        P.wait("vector", c.s_t)
        P.wait_mark("vector", "aW16L")
        for i, h in enumerate(heads):
            asl = pA[:, bk * 512 + i * 128:bk * 512 + (i + 1) * 128]
            P.op("vector", "scalar_tensor_tensor",
                 W16[:, i * 128:(i + 1) * 128], asl, bias_col(c, 152 + h),
                 c.MASKL[:, :], ALU.mult, ALU.mult)
            P.op("vector", "scalar_tensor_tensor",
                 W16[:, 512 + i * 128:512 + (i + 1) * 128], asl,
                 bias_col(c, 152 + h), c.MASKU[:, :], ALU.mult, ALU.mult,
                 inc=(c.s_v, 1) if i == 3 else None)
        P.set_mark(pA.name + str(bk), c.s_v)
        for L in range(1, 7):
            P.wait("tensor", c.s_v)
            P.wait_mark("tensor", pB.name)
            P.wait_mark("tensor", pBT.name)
            for i in range(4):
                Bp = W16[:, (L - 1) * 1024 + i * 128:(L - 1) * 1024 + (i + 1) * 128]
                BTp = W16[:, (L - 1) * 1024 + 512 + i * 128:(L - 1) * 1024 + 512 + (i + 1) * 128]
                P.op("tensor", "matmul", pB[:, i * 128:(i + 1) * 128], BTp, Bp,
                     start=True, stop=True)
                P.op("tensor", "matmul", pBT[:, i * 128:(i + 1) * 128], Bp, BTp,
                     start=True, stop=True,
                     inc=(c.s_t, 1) if i == 3 else None)
            P.wait("vector", c.s_t)
            P.op("vector", "tensor_copy", W16[:, L * 1024:L * 1024 + 512],
                 pB[:, :], inc=(c.s_v, 1))
            P.op("vector", "tensor_copy", W16[:, L * 1024 + 512:(L + 1) * 1024],
                 pBT[:, :], inc=(c.s_v, 1))
            P.set_mark(pB.name, c.s_v)
            P.set_mark(pBT.name, c.s_v)
        P.set_mark("aW16L", c.s_v)
        for ip, p in enumerate((2 * hg, 2 * hg + 1)):
            prt = ip % 2
            P.wait_mark("tensor", pT.name + str(prt))
            P.op("tensor", "transpose", pT[:, prt * 1024:prt * 1024 + 128],
                 XT[:, VG0 + p * GT + gt0:VG0 + p * GT + gt0 + 128],
                 c.CIDB[:, :], inc=(c.s_t, 1))
            P.op("tensor", "transpose", pT[:, prt * 1024 + 128:prt * 1024 + 256],
                 XT[:, KG0 + p * GT + gt0:KG0 + p * GT + gt0 + 128],
                 c.CIDB[:, :], inc=(c.s_t, 1))
            P.wait("vector", c.s_t)
            P.op("vector", "tensor_copy",
                 XT[:, VT + ip * 128:VT + (ip + 1) * 128],
                 pT[:, prt * 1024:prt * 1024 + 128])
            P.op("vector", "tensor_copy",
                 XT[:, KT + ip * 128:KT + (ip + 1) * 128],
                 pT[:, prt * 1024 + 128:prt * 1024 + 256], inc=(c.s_v, 1))
            P.set_mark(pT.name + str(prt), c.s_v)
        P.wait("scalar", c.s_v)
        P.wait("scalar", c.s_s)
        for i, h in enumerate(heads):
            par = h % 2
            p = h // 2
            P.op("scalar", "activation", W0B[:, h * 64:(h + 1) * 64],
                 W0T[:, p * 64:(p + 1) * 64], AF.Identity, bias=0.0,
                 scale=bias_col(c, 184 + par), inc=(c.s_s, 1))
        P.wait("tensor", c.s_s)
        P.wait_mark("tensor", pU.name + str(bk))
        for i, h in enumerate(heads):
            par = h % 2
            p = h // 2
            P.op("tensor", "matmul",
                 pU[:, bk * 512 + i * 64:bk * 512 + (i + 1) * 64],
                 XT[:, KZ0[par] + p * 128:KZ0[par] + (p + 1) * 128],
                 W0B[:, h * 64:(h + 1) * 64],
                 start=True, stop=True, inc=(c.s_t, 1) if i == 3 else None)
        P.wait("vector", c.s_t)
        P.wait("vector", c.s_v)
        P.op("vector", "tensor_sub", U[:, hg * 256:(hg + 1) * 256],
             pU[:, bk * 512:bk * 512 + 256], XT[:, VT:VT + 256],
             inc=(c.s_v, 1))
        P.set_mark(pU.name + str(bk), c.s_v)
        P.wait("scalar", c.s_v)
        P.op("scalar", "copy", UB[:, hg * 256:(hg + 1) * 256],
             U[:, hg * 256:(hg + 1) * 256], inc=(c.s_s, 1))
        for L in range(6, -1, -1):
            P.wait("tensor", c.s_s)
            P.wait_mark("tensor", pU.name + str(1 - bk))
            for i in range(4):
                BTp = W16[:, L * 1024 + 512 + i * 128:L * 1024 + 512 + (i + 1) * 128]
                P.op("tensor", "matmul",
                     pU[:, (1 - bk) * 512 + i * 64:(1 - bk) * 512 + (i + 1) * 64],
                     BTp, UB[:, (hg * 4 + i) * 64:(hg * 4 + i + 1) * 64],
                     start=True, stop=True, inc=(c.s_t, 1) if i == 3 else None)
            P.wait("vector", c.s_t)
            P.op("vector", "tensor_tensor", U[:, hg * 256:(hg + 1) * 256],
                 U[:, hg * 256:(hg + 1) * 256],
                 pU[:, (1 - bk) * 512:(1 - bk) * 512 + 256],
                 ALU.add if L > 0 else ALU.subtract, inc=(c.s_v, 1))
            P.set_mark(pU.name + str(1 - bk), c.s_v)
            P.wait("scalar", c.s_v)
            P.op("scalar", "copy", UB[:, hg * 256:(hg + 1) * 256],
                 U[:, hg * 256:(hg + 1) * 256], inc=(c.s_s, 1))
        P.wait("scalar", c.s_s)
        for i, h in enumerate(heads):
            P.op("scalar", "activation",
                 UB[:, (hg * 4 + i) * 64:(hg * 4 + i + 1) * 64],
                 U[:, (hg * 4 + i) * 64:(hg * 4 + i + 1) * 64], AF.Identity,
                 bias=0.0, scale=bias_col(c, 168 + h), inc=(c.s_s, 1))
        P.wait("tensor", c.s_s)
        P.wait_mark("tensor", pA.name + str(bk))
        for i, h in enumerate(heads):
            par = h % 2
            p = h // 2
            P.op("tensor", "matmul",
                 pA[:, bk * 512 + i * 128:bk * 512 + (i + 1) * 128],
                 XT[:, KZ0[par] + p * 128:KZ0[par] + (p + 1) * 128],
                 XT[:, QG0 + p * GT + gt0:QG0 + p * GT + gt0 + 128],
                 start=True, stop=True, inc=(c.s_t, 1) if i == 3 else None)
        P.wait("vector", c.s_t)
        av = pA[:, bk * 512:(bk + 1) * 512]
        a3 = AP(av.tensor, av.offset, [av.ap[0], [128, 4], [1, 128]])
        mu = c.MASKU[:, :]
        mu3 = AP(mu.tensor, mu.offset, [mu.ap[0], [0, 4], [1, 128]])
        gm = XT[:, GM:GM + 512]
        gm3 = AP(gm.tensor, gm.offset, [gm.ap[0], [128, 4], [1, 128]])
        P.op("vector", "tensor_tensor", gm3, a3, mu3, ALU.mult, inc=(c.s_v, 1))
        P.set_mark(pA.name + str(bk), c.s_v)
        P.wait("tensor", c.s_v)
        P.wait("tensor", c.s_s)
        P.wait_mark("tensor", pB.name)
        for i, h in enumerate(heads):
            rb = (h % 2) * 64
            p = h // 2
            ip = p - 2 * hg
            P.op("tensor", "matmul", pB[rb:rb + 64, ip * 128:(ip + 1) * 128],
                 W0B[:, h * 64:(h + 1) * 64],
                 XT[:, QG0 + p * GT + gt0:QG0 + p * GT + gt0 + 128],
                 start=True, stop=False, tile_position=(0, rb))
            P.op("tensor", "matmul", pB[rb:rb + 64, ip * 128:(ip + 1) * 128],
                 UB[:, (hg * 4 + i) * 64:(hg * 4 + i + 1) * 64],
                 XT[:, GM + i * 128:GM + (i + 1) * 128],
                 start=False, stop=True, tile_position=(0, rb),
                 inc=(c.s_t, 1) if i == 3 else None)
        P.wait("vector", c.s_t)
        P.op("vector", "tensor_copy", XT[:, TO + hg * 256:TO + (hg + 1) * 256],
             pB[:, 0:256], inc=(c.s_v, 1))
        P.set_mark(pB.name, c.s_v)
        P.wait("tensor", c.s_v)
        P.wait_mark("tensor", pBT.name)
        for i, h in enumerate(heads):
            rb = (h % 2) * 64
            p = h // 2
            ip = p - 2 * hg
            P.op("tensor", "matmul", pBT[rb:rb + 64, ip * 64:(ip + 1) * 64],
                 XT[:, KT + ip * 128 + rb:KT + ip * 128 + rb + 64],
                 UB[:, (hg * 4 + i) * 64:(hg * 4 + i + 1) * 64],
                 start=True, stop=True, tile_position=(0, rb),
                 inc=(c.s_t, 1) if i == 3 else None)
        P.wait("vector", c.s_t)
        P.op("vector", "tensor_add", W0T[:, hg * 128:(hg + 1) * 128],
             W0T[:, hg * 128:(hg + 1) * 128], pBT[:, 0:128], inc=(c.s_v, 1))
        P.set_mark(pBT.name, c.s_v)
    P.set_mark("aKZ", c.s_t)
    if TTT_STAGES < 11:
        return
    P.wait("tensor", c.s_v)
    P.wait_mark("tensor", pU.name + "o0")
    P.wait_mark("tensor", pU.name + "o1")
    s_wo = c.P.sem("s_wl1ao")
    for half in range(2):
        c.P.wait_mark("sync", "aWOh")
        _ld3(c.P, WO[:, :], 1024, c.st["l1a_woT"], D, half * 4 * 128, 4, 0,
             s_wo)
        P.wait("tensor", s_wo)
        for jt in range(DT):
            prj = jt % 2
            P.wait_mark("tensor", pU.name + "o" + str(prj))
            for p in range(4):
                P.op("tensor", "matmul",
                     pU[:, prj * 512:prj * 512 + 128],
                     WO[:, p * 1024 + jt * 128:p * 1024 + (jt + 1) * 128],
                     XT[:, TO + (half * 4 + p) * 128:TO + (half * 4 + p + 1) * 128],
                     start=(p == 0), stop=(p == 3),
                     inc=(c.s_t, 1) if p == 3 else None)
            P.wait("vector", c.s_t)
            P.op("vector", "tensor_add",
                 XI[:, fmc(jt, chunk_t0, 128)], XI[:, fmc(jt, chunk_t0, 128)],
                 pU[:, prj * 512:prj * 512 + 128], inc=(c.s_v, 1))
            P.set_mark(pU.name + "o" + str(prj), c.s_v)
        P.set_mark("aWOh", c.s_t)


def _emit_l1a_mlp(c, XI):
    P, nc = c.P, c.nc
    _sublayer_barrier(c)
    _emit_ln(c, src=XI, dst=c.XT, w_col=64, b_col=72, barrier=False)
    NHT = H1A // 128
    GRP = 4
    es = ExitStack()
    W1 = es.enter_context(nc.sbuf_tensor(_u("m1W1"), [128, GRP * 1024], BF16))
    W2 = es.enter_context(nc.sbuf_tensor(_u("m1W2"), [128, GRP * 1024], BF16))
    HGB = es.enter_context(nc.sbuf_tensor(_u("m1HG"), [128, GRP * S], BF16))
    PSG = es.enter_context(nc.psum_tensor(_u("m1psg"), [128, 1024], F32))
    PSD = es.enter_context(nc.psum_tensor(_u("m1psd"), [128, 1024], F32))
    s_fw = P.sem("s_wl1am")
    with es:
        P.wait("sync", c.s_s)
        P.wait("sync", c.s_v)
        P.wait("sync", c.s_t)
        for grp in range(NHT // GRP):
            P.wait_mark("sync", "m1W")
            P.wait("sync", c.s_stage)
            gw = GRP * 128
            _ld3(P, W1[:, :], gw, c.st["l1a_w1T"], H1A, 0, DT, grp * gw, s_fw)
            _ld3(P, W2[:, :], 1024, c.st["l1a_w2T"], D, grp * GRP * 128, GRP,
                 0, s_fw)
            w_ready = s_fw
            for hh in range(GRP):
                ht = grp * GRP + hh
                for tc in range(NTCH):
                    prx = tc % 2
                    t0 = tc * TCH
                    P.wait("tensor", w_ready)
                    P.wait_mark("tensor", PSG.name + str(prx))
                    for d in range(DT):
                        P.op("tensor", "matmul", PSG[:, prx * 512:(prx + 1) * 512],
                             W1[:, d * gw + hh * 128:d * gw + (hh + 1) * 128],
                             c.XT[:, fmc(d, t0, TCH)], start=(d == 0),
                             stop=(d == DT - 1),
                             inc=(c.s_t, 1) if d == DT - 1 else None)
                    P.wait("scalar", c.s_t)
                    P.op("scalar", "activation",
                         HGB[:, hh * S + t0:hh * S + t0 + TCH],
                         PSG[:, prx * 512:(prx + 1) * 512], AF.Gelu,
                         bias=bias_col(c, 80 + ht), scale=1.0, inc=(c.s_s, 1))
                    P.set_mark(PSG.name + str(prx), c.s_s)
            h_ready = c.s_s
            hn = c.s_s.n
            for dcx in range(DT):
                for tc in range(NTCH):
                    prx = tc % 2
                    t0 = tc * TCH
                    P.wait("tensor", h_ready, hn)
                    P.wait_mark("tensor", PSD.name + str(prx))
                    for hh in range(GRP):
                        P.op("tensor", "matmul", PSD[:, prx * 512:(prx + 1) * 512],
                             W2[:, hh * 1024 + dcx * 128:hh * 1024 + (dcx + 1) * 128],
                             HGB[:, hh * S + t0:hh * S + t0 + TCH],
                             start=(hh == 0), stop=(hh == GRP - 1),
                             inc=(c.s_t, 1) if hh == GRP - 1 else None)
                    P.wait("vector", c.s_t)
                    if grp == NHT // GRP - 1:
                        P.op("vector", "scalar_tensor_tensor",
                             XI[:, fmc(dcx, t0, TCH)],
                             PSD[:, prx * 512:(prx + 1) * 512],
                             bias_col(c, 112 + dcx), XI[:, fmc(dcx, t0, TCH)],
                             ALU.add, ALU.add, inc=(c.s_v, 1))
                    else:
                        P.op("vector", "tensor_add", XI[:, fmc(dcx, t0, TCH)],
                             XI[:, fmc(dcx, t0, TCH)],
                             PSD[:, prx * 512:(prx + 1) * 512], inc=(c.s_v, 1))
                    P.set_mark(PSD.name + str(prx), c.s_v)
            P.set_mark("m1W", c.s_t)


def _alloc_quant_scratch(c, es, n):
    nc = c.nc
    q = {}
    q["yn"] = es.enter_context(nc.sbuf_tensor(_u("qyn"), [128, n], F32))
    q["c1"] = es.enter_context(nc.sbuf_tensor(_u("qc1"), [128, n], F32))
    q["c2"] = es.enter_context(nc.sbuf_tensor(_u("qc2"), [128, n], F32))
    q["sg"] = es.enter_context(nc.sbuf_tensor(_u("qsg"), [128, n], BF16))
    q["bm"] = es.enter_context(nc.sbuf_tensor(_u("qbm"), [128, n // 16], F32))
    q["rs"] = es.enter_context(nc.sbuf_tensor(_u("qrs"), [128, n // 16], F32))
    return q


def _win16(ap_2d):
    a = ap_2d
    n = a.ap[-1][1]
    return AP(a.tensor, a.offset, [a.ap[0], [16, n // 16], [1, 16]])


def _bc16(ap_small, n):
    a = ap_small
    return AP(a.tensor, a.offset, [a.ap[0], [1, n // 16], [0, 16]])


def _emit_quant(c, q, src, dst, n, m=128):
    P = c.P
    V, SV = "vector", c.s_v
    yn, c1, c2, sg = (q[k][0:m, 0:n] for k in ("yn", "c1", "c2", "sg"))
    bm, rs = q["bm"][0:m, 0:n // 16], q["rs"][0:m, 0:n // 16]
    P.op(V, "tensor_reduce", bm, _win16(src), AX.X, ALU.max,
         apply_absolute_value=True, inc=(SV, 1))
    P.wait("scalar", c.s_s)
    P.wait("scalar", c.s_v)
    P.op("scalar", "activation", sg, src, AF.Sign, inc=(c.s_s, 1))
    P.wait(V, SV)
    P.op(V, "tensor_scalar", bm, bm, 1.0 / 6.0, 1e-12, ALU.mult, ALU.max,
         inc=(SV, 1))
    P.op(V, "scalar_tensor_tensor", yn, src, -1.0, src, ALU.mult, ALU.max)
    P.wait(V, SV)
    P.op(V, "reciprocal", rs, bm, inc=(SV, 1))
    P.wait(V, SV)
    P.op(V, "tensor_tensor", _win16(yn), _win16(yn), _bc16(rs, n), ALU.mult,
         inc=(SV, 1))
    P.wait(V, SV)
    P.op(V, "tensor_single_scalar", c1, yn, 0.25, ALU.is_gt)
    P.op(V, "tensor_single_scalar", c2, yn, 2.5, ALU.is_gt, inc=(SV, 1))
    P.wait(V, SV)
    P.op(V, "scalar_tensor_tensor", c1, yn, 0.75, c1, ALU.is_gt, ALU.add)
    P.op(V, "scalar_tensor_tensor", c2, yn, 3.5, c2, ALU.is_gt, ALU.add,
         inc=(SV, 1))
    P.wait(V, SV)
    P.op(V, "scalar_tensor_tensor", c1, yn, 1.25, c1, ALU.is_gt, ALU.add)
    P.op(V, "scalar_tensor_tensor", c2, yn, 5.0, c2, ALU.is_gt, ALU.add,
         inc=(SV, 1))
    P.wait(V, SV)
    P.op(V, "scalar_tensor_tensor", c1, yn, 1.75, c1, ALU.is_gt, ALU.add)
    P.op(V, "scalar_tensor_tensor", c2, yn, 5.0, c2, ALU.is_gt, ALU.add,
         inc=(SV, 1))
    P.wait(V, SV)
    P.op(V, "scalar_tensor_tensor", c2, c1, 0.5, c2, ALU.mult, ALU.add,
         inc=(SV, 1))
    P.wait(V, c.s_s)
    P.wait(V, SV)
    P.op(V, "tensor_mul", c1, c2, sg, inc=(SV, 1))
    P.wait(V, SV)
    P.op(V, "tensor_tensor", _win16(c1), _win16(c1), _bc16(bm, n), ALU.mult,
         inc=(SV, 1))
    P.wait(V, SV)
    P.op(V, "tensor_copy", dst, c1, inc=(SV, 1))


def _ld3(P, dst_ap, ncol, srcT, rowstride, row0, nrow_t, col0, sem):
    out3 = dst_ap.rearrange("p (t q) -> p t q", q=ncol)
    in3 = AP(srcT.tensor, srcT.offset + row0 * rowstride + col0,
             [[rowstride, 128], [128 * rowstride, nrow_t], [1, ncol]])
    P.dma("sync", out3, in3, sem)


def _emit_pretranspose_weights(c, active):
    P, nc = c.P, c.nc
    jobs = []
    if "l0b" in active:
        for n in "qkvo":
            jobs.append((c.st["l0b_" + n], c.st["l0b_" + n + "T"], D, D))
    if "l1a" in active:
        for n in ("wq", "wk", "wv", "wo"):
            jobs.append((c.st["l1a_" + n], c.st["l1a_" + n + "T"], D, D))
        jobs.append((c.st["l1a_w1"], c.st["l1a_w1T"], H1A, D))
        jobs.append((c.st["l1a_w2"], c.st["l1a_w2T"], D, H1A))
    if ("l0f1" in active) or ("l0f2" in active):
        jobs.append((c.st["l0f_g"], c.st["l0f_gT"], H0, D))
        jobs.append((c.st["l0f_u"], c.st["l0f_uT"], H0, D))
        jobs.append((c.st["l0f_d"], c.st["l0f_dT"], D, H0))
    if not jobs:
        return
    _sublayer_barrier(c)
    es = ExitStack()
    LT = es.enter_context(nc.sbuf_tensor(_u("ptL"), [128, 2 * 4096], BF16))
    TT = es.enter_context(nc.sbuf_tensor(_u("ptT"), [128, 2 * 4096], BF16))
    pX = es.enter_context(nc.psum_tensor(_u("ptP"), [128, 2048], BF16))
    s_l = [P.sem("s_pt0"), P.sem("s_pt1")]
    s_s2 = [P.sem("s_pt2"), P.sem("s_pt3")]
    with es:
        P.wait("sync", c.s_stage)
        for wsrc, wdst, O, I in jobs:
            for ot in range(O // 128):
                pr = ot % 2
                P.wait_mark("sync", f"ptL{pr}")
                P.dma("sync", LT[:, pr * 4096:pr * 4096 + I],
                      wsrc[ot * 128:(ot + 1) * 128, :], s_l[pr])
                P.wait("tensor", s_l[pr])
                P.wait_mark("tensor", f"ptT{pr}")
                for itc in range(I // 1024):
                    pp = itc % 2
                    P.wait_mark("tensor", pX.name + str(pp))
                    for j in range(8):
                        it = itc * 8 + j
                        P.op("tensor", "transpose",
                             pX[:, pp * 1024 + j * 128:pp * 1024 + (j + 1) * 128],
                             LT[:, pr * 4096 + it * 128:pr * 4096 + (it + 1) * 128],
                             c.CIDB[:, :],
                             inc=(c.s_t, 1) if j == 7 else None)
                    P.wait("vector", c.s_t)
                    P.op("vector", "tensor_copy",
                         TT[:, pr * 4096 + itc * 1024:pr * 4096 + (itc + 1) * 1024],
                         pX[:, pp * 1024:(pp + 1) * 1024], inc=(c.s_v, 1))
                    P.set_mark(pX.name + str(pp), c.s_v)
                P.set_mark(f"ptL{pr}", c.s_t)
                P.wait("sync", c.s_v)
                tv = TT[:, pr * 4096:pr * 4096 + I]
                src3 = AP(tv.tensor, tv.offset, [tv.ap[0], [128, I // 128], [1, 128]])
                dst3 = AP(wdst.tensor, wdst.offset + ot * 128,
                          [[O, 128], [128 * O, I // 128], [1, 128]])
                P.dma("sync", dst3, src3, s_s2[pr])
                P.set_mark(f"ptT{pr}", s_s2[pr])


def _emit_l1f_wquant(c):
    P, nc = c.P, c.nc
    _sublayer_barrier(c)
    es = ExitStack()
    with es:
        WF = es.enter_context(nc.sbuf_tensor(_u("wqF"), [128, 2 * 2816], F32))
        WQ = es.enter_context(nc.sbuf_tensor(_u("wqQ"), [128, 2 * 2816], BF16))
        WT = es.enter_context(nc.sbuf_tensor(_u("wqT"), [128, 2816], BF16))
        p_tq = es.enter_context(nc.psum_tensor(_u("wqp"), [128, 2048], BF16))
        q = _alloc_quant_scratch(c, es, 2816)
        q1k = _alloc_quant_scratch(c, es, 1024)
        jobs = []
        for wsrc, wdst, ntile, width in (
            (c.l1f["w_gate"], c.st["l1f_gT"], 22, 1024),
            (c.l1f["w_up"], c.st["l1f_uT"], 22, 1024),
            (c.l1f["w_down"], c.st["l1f_dT"], 8, 2816),
        ):
            for i in range(ntile):
                jobs.append((wsrc, wdst, i, ntile, width))
        for idx, (wsrc, wdst, i, ntile, width) in enumerate(jobs):
            pr = idx % 2
            s_lw = P.sem(f"s_l1fq_ld{pr}")
            s_sw = P.sem(f"s_l1fq_st{pr}")
            qq = q1k if width == 1024 else q
            P.wait_mark("sync", f"wqF{pr}")
            P.dma("sync", WF[:, pr * 2816:pr * 2816 + width],
                  wsrc[i * 128:(i + 1) * 128, :], s_lw)
            P.wait("vector", s_lw)
            P.wait("scalar", s_lw)
            P.wait_mark("vector", f"wqQ{pr}")
            _emit_quant(c, qq, WF[:, pr * 2816:pr * 2816 + width],
                        WQ[:, pr * 2816:pr * 2816 + width], width)
            P.set_mark(f"wqF{pr}", c.s_v)
            nblk = width // 128
            ow = ntile * 128
            P.wait("tensor", c.s_v)
            P.wait_mark("vector", "wqT")
            for j in range(nblk):
                pj = j % 2
                P.wait_mark("tensor", p_tq.name + str(pj))
                P.op("tensor", "transpose", p_tq[:, pj * 1024:pj * 1024 + 128],
                     WQ[:, pr * 2816 + j * 128:pr * 2816 + (j + 1) * 128],
                     c.CIDB[:, :], inc=(c.s_t, 1))
                P.wait("vector", c.s_t)
                P.op("vector", "tensor_copy", WT[:, j * 128:(j + 1) * 128],
                     p_tq[:, pj * 1024:pj * 1024 + 128], inc=(c.s_v, 1))
                P.set_mark(p_tq.name + str(pj), c.s_v)
            P.set_mark(f"wqQ{pr}", c.s_v)
            P.wait("sync", c.s_v)
            wtv = WT[:, :]
            src3 = AP(wtv.tensor, wtv.offset, [wtv.ap[0], [128, nblk], [1, 128]])
            dst3 = AP(wdst.tensor, wdst.offset + i * 128,
                      [[ow, 128], [128 * ow, nblk], [1, 128]])
            P.dma("sync", dst3, src3, s_sw)
            P.set_mark("wqT", s_sw)
    c.l1f_wq_done = True


def _emit_l1ffn(c):
    P, nc = c.P, c.nc
    _sublayer_barrier(c)
    HCH = [(0, 512), (512, 512), (1024, 512), (1536, 512), (2048, 512),
           (2560, 256)]
    es = ExitStack()
    XTM = es.enter_context(nc.sbuf_tensor(_u("f1XTM"), [128, D], F32))
    XQT = es.enter_context(nc.sbuf_tensor(_u("f1XQT"), [128, D], BF16))
    HQF = es.enter_context(nc.sbuf_tensor(_u("f1HQF"), [128, 22 * TCH], BF16))
    HTM = es.enter_context(nc.sbuf_tensor(_u("f1HTM"), [128, 2048], F32))
    HQT = es.enter_context(nc.sbuf_tensor(_u("f1HQT"), [128, 2048], BF16))
    SGT = es.enter_context(nc.sbuf_tensor(_u("f1SG"), [128, TCH], F32))
    WB = es.enter_context(nc.sbuf_tensor(_u("f1WB"), [128, 22 * 1024], BF16))
    p_a = es.enter_context(nc.psum_tensor(_u("f1pa"), [128, 1024], F32))
    p_b = es.enter_context(nc.psum_tensor(_u("f1pb"), [128, 1024], F32))
    p_t = es.enter_context(nc.psum_tensor(_u("f1pt"), [128, 2048], BF16))
    p_d = es.enter_context(nc.psum_tensor(_u("f1pd"), [128, 1024], F32))
    s_w = P.sem("s_wl1f")
    s_wd = P.sem("s_wl1fd")
    with es:
        q1k = _alloc_quant_scratch(c, es, 1024)
        ROWS = es.enter_context(nc.sbuf_tensor(_u("f1RO"), [128, 8], F32))
        for tt in range(S // 128):
            P.wait_mark("tensor", XTM.name)
            for d in range(DT):
                pr2 = d % 2
                P.wait_mark("tensor", p_a.name + "x" + str(pr2))
                P.op("tensor", "transpose", p_a[:, pr2 * 512:pr2 * 512 + 128],
                     c.XR[:, fmc(d, tt * 128, 128)], c.CID[:, :],
                     inc=(c.s_t, 1))
                P.wait("scalar", c.s_t)
                P.op("scalar", "copy", XTM[:, d * 128:(d + 1) * 128],
                     p_a[:, pr2 * 512:pr2 * 512 + 128], inc=(c.s_s, 1))
                P.set_mark(p_a.name + "x" + str(pr2), c.s_s)
            P.wait("vector", c.s_s)
            P.op("vector", "tensor_reduce", ROWS[:, 0:1], XTM[:, :], AX.X,
                 ALU.add, negate=True, inc=(c.s_v, 1))
            P.wait("scalar", c.s_v)
            P.op("scalar", "activation", q1k["c1"][:, :], XTM[:, :],
                 AF.Square, inc=(c.s_s, 1))
            P.wait("vector", c.s_s)
            P.op("vector", "tensor_reduce", ROWS[:, 1:2], q1k["c1"][:, :],
                 AX.X, ALU.add, inc=(c.s_v, 1))
            P.wait("vector", c.s_v)
            P.op("vector", "tensor_scalar_mul", ROWS[:, 2:3], ROWS[:, 0:1],
                 1.0 / D, inc=(c.s_v, 1))
            P.wait("vector", c.s_v)
            P.op("vector", "tensor_mul", ROWS[:, 3:4], ROWS[:, 2:3],
                 ROWS[:, 2:3], inc=(c.s_v, 1))
            P.wait("vector", c.s_v)
            P.op("vector", "scalar_tensor_tensor", ROWS[:, 4:5], ROWS[:, 1:2],
                 1.0 / D, ROWS[:, 3:4], ALU.mult, ALU.subtract,
                 inc=(c.s_v, 1))
            P.wait("scalar", c.s_v)
            P.op("scalar", "activation", ROWS[:, 4:5], ROWS[:, 4:5], AF.Sqrt,
                 bias=c.CNST[:, 0:1], scale=1.0, inc=(c.s_s, 1))
            P.wait("vector", c.s_s)
            P.op("vector", "reciprocal", ROWS[:, 5:6], ROWS[:, 4:5],
                 inc=(c.s_v, 1))
            P.wait("vector", c.s_v)
            P.op("vector", "tensor_scalar", XTM[:, :], XTM[:, :],
                 ROWS[:, 2:3], ROWS[:, 5:6], ALU.add, ALU.mult,
                 inc=(c.s_v, 1))
            P.wait("vector", c.s_v)
            _emit_quant(c, q1k, XTM[:, :], XQT[:, :], 1024)
            P.set_mark(XTM.name, c.s_v)
            P.wait("tensor", c.s_v)
            for d in range(DT):
                pr2 = d % 2
                P.wait_mark("tensor", p_t.name + str(pr2))
                P.op("tensor", "transpose", p_t[:, pr2 * 1024:pr2 * 1024 + 128],
                     XQT[:, d * 128:(d + 1) * 128], c.CIDB[:, :],
                     inc=(c.s_t, 1))
                P.wait("vector", c.s_t)
                P.op("vector", "tensor_copy", c.XT[:, fmc(d, tt * 128, 128)],
                     p_t[:, pr2 * 1024:pr2 * 1024 + 128], inc=(c.s_v, 1))
                P.set_mark(p_t.name + str(pr2), c.s_v)
        for tc in range(NTCH):
            t0 = tc * TCH
            for hci, (h0, hn) in enumerate(HCH):
                P.wait_mark("sync", WB.name)
                for buf0, wt in ((0, c.st["l1f_gT"]), (4096, c.st["l1f_uT"])):
                    bv = WB[:, buf0:buf0 + DT * 512]
                    out3 = AP(bv.tensor, bv.offset,
                              [bv.ap[0], [512, DT], [1, hn]])
                    in3 = AP(wt.tensor, wt.offset + h0,
                             [[H1, 128], [128 * H1, DT], [1, hn]])
                    P.dma("sync", out3, in3, s_w)
                P.wait("tensor", s_w)
                for ttl in range(4):
                    tt0 = t0 + ttl * 128
                    prx = ttl % 2
                    P.wait_mark("tensor", p_a.name + str(prx))
                    for d in range(DT):
                        P.op("tensor", "matmul", p_a[:, prx * 512:prx * 512 + hn],
                             c.XT[:, fmc(d, tt0, 128)],
                             WB[:, d * 512:d * 512 + hn],
                             start=(d == 0), stop=(d == DT - 1),
                             inc=(c.s_t, 1) if d == DT - 1 else None)
                    P.wait_mark("tensor", p_b.name + str(prx))
                    for d in range(DT):
                        P.op("tensor", "matmul", p_b[:, prx * 512:prx * 512 + hn],
                             c.XT[:, fmc(d, tt0, 128)],
                             WB[:, 4096 + d * 512:4096 + d * 512 + hn],
                             start=(d == 0), stop=(d == DT - 1),
                             inc=(c.s_t, 1) if d == DT - 1 else None)
                    P.wait("scalar", c.s_t, c.s_t.n - 1)
                    P.wait_mark("scalar", SGT.name)
                    P.op("scalar", "activation", SGT[:, 0:hn],
                         p_a[:, prx * 512:prx * 512 + hn], AF.Silu, bias=0.0,
                         scale=1.0, inc=(c.s_s, 1))
                    P.set_mark(p_a.name + str(prx), c.s_s)
                    P.wait("vector", c.s_s)
                    P.wait("vector", c.s_t)
                    P.wait_mark("vector", HTM.name + str(ttl))
                    P.op("vector", "tensor_mul",
                         HTM[:, ttl * 512:ttl * 512 + hn], SGT[:, 0:hn],
                         p_b[:, prx * 512:prx * 512 + hn], inc=(c.s_v, 1))
                    P.set_mark(p_b.name + str(prx), c.s_v)
                    P.set_mark(SGT.name, c.s_v)
                P.set_mark(WB.name, c.s_t)
                P.wait("vector", c.s_v)
                for ttl in range(4):
                    _emit_quant(c, q1k, HTM[:, ttl * 512:ttl * 512 + hn],
                                HQT[:, ttl * 512:ttl * 512 + hn], hn)
                    P.set_mark(HTM.name + str(ttl), c.s_v)
                P.wait("tensor", c.s_v)
                nht = hn // 128
                for ttl in range(4):
                    for hh in range(nht):
                        pr2 = hh % 2
                        P.wait_mark("tensor", p_t.name + str(pr2))
                        P.op("tensor", "transpose", p_t[:, pr2 * 1024:pr2 * 1024 + 128],
                             HQT[:, ttl * 512 + hh * 128:ttl * 512 + (hh + 1) * 128],
                             c.CIDB[:, :], inc=(c.s_t, 1))
                        P.wait("vector", c.s_t)
                        P.op("vector", "tensor_copy",
                             HQF[:, (h0 // 128 + hh) * TCH + ttl * 128:
                                 (h0 // 128 + hh) * TCH + (ttl + 1) * 128],
                             p_t[:, pr2 * 1024:pr2 * 1024 + 128],
                             inc=(c.s_v, 1))
                        P.set_mark(p_t.name + str(pr2), c.s_v)
            P.set_mark(WB.name, c.s_t)
            P.wait_mark("sync", WB.name)
            _ld3(P, WB[:, :], 1024, c.st["l1f_dT"], D, 0, 22, 0, s_wd)
            P.wait("tensor", s_wd)
            P.wait("tensor", c.s_v)
            for dcx in range(DT):
                prx = dcx % 2
                P.wait_mark("tensor", p_d.name + str(prx))
                for ht in range(22):
                    P.op("tensor", "matmul",
                         p_d[:, prx * 512:(prx + 1) * 512],
                         WB[:, ht * 1024 + dcx * 128:ht * 1024 + (dcx + 1) * 128],
                         HQF[:, ht * TCH:(ht + 1) * TCH],
                         start=(ht == 0), stop=(ht == 21),
                         inc=(c.s_t, 1) if ht == 21 else None)
                P.wait("vector", c.s_t)
                P.op("vector", "tensor_add", c.XR[:, fmc(dcx, t0, TCH)],
                     c.XR[:, fmc(dcx, t0, TCH)],
                     p_d[:, prx * 512:(prx + 1) * 512], inc=(c.s_v, 1))
                P.set_mark(p_d.name + str(prx), c.s_v)
            P.set_mark(WB.name, c.s_t)


def kernel(**inputs):
    active = SUBLAYERS
    c = build_program(active)
    consts = c.consts_np
    in_maps = []
    x_full = np.ascontiguousarray(inputs["L0_attn_a_x"])
    for core in range(NCORE):
        m = {}
        for k, v in inputs.items():
            if k == "L0_attn_a_x":
                m[k] = np.ascontiguousarray(
                    x_full[core * SEQ_PER_CORE:(core + 1) * SEQ_PER_CORE])
            else:
                m[k] = np.ascontiguousarray(v)
        for k, v in consts.items():
            m[k] = v
        in_maps.append(m)
    res = run_bass_kernel_spmd(c.nc, in_maps, core_ids=list(range(NCORE)))
    return np.concatenate([r["out"] for r in res.results], axis=0)


# version 08
# speedup vs baseline: 1.3311x; 1.0943x over previous
import numpy as np
from contextlib import ExitStack
import concourse.bass as bass
import concourse.mybir as mybir
from concourse.bass_utils import run_bass_kernel_spmd

F32 = mybir.dt.float32
BF16 = mybir.dt.bfloat16
I32 = mybir.dt.int32
AF = mybir.ActivationFunctionType
ALU = mybir.AluOpType
AX = mybir.AxisListType
AP = bass.AP

B, S, D = 16, 2048, 1024
NCORE = 8
SEQ_PER_CORE = B // NCORE
DT = D // 128
TCH = 512
NTCH = S // TCH
NH, HD = 16, 64
NPAIR = NH // 2
EPS_LN = 1e-5
EPS_RMS = 1e-8
EPS_RENORM = 1e-8
INNER_LR = 0.01
QD = 1.0 / 127.0
H0, H1, H1A = 3072, 2816, 4096

ENGINES = ("tensor", "vector", "scalar", "gpsimd", "sync")
SUBLAYERS = ["l0a", "l0f1", "l0b", "l0f2", "l1a", "l1f1", "l1b", "l1f2"]
TTT_STAGES = 99
DBG = None


class Sem:
    __slots__ = ("h", "n", "name", "is_dma")

    def __init__(self, handle, name):
        self.h = handle
        self.n = 0
        self.name = name
        self.is_dma = False


class Prog:
    def __init__(self, nc):
        self.nc = nc
        self.q = {e: [] for e in ENGINES}
        self.es = ExitStack()
        self.mark = {}

    def sem(self, name):
        if not hasattr(self, "_sems"):
            self._sems = {}
        if name not in self._sems:
            self._sems[name] = Sem(
                self.es.enter_context(self.nc.semaphore(name)), name)
        return self._sems[name]

    def sbuf(self, name, shape, dtype):
        return self.es.enter_context(self.nc.sbuf_tensor(name, shape, dtype))

    def psum(self, name, shape, dtype=F32):
        return self.es.enter_context(self.nc.psum_tensor(name, shape, dtype))

    def emit(self, eng, fn):
        self.q[eng].append(fn)

    def op(self, eng, method, *args, inc=None, **kw):
        if inc is not None:
            inc[0].n += inc[1]

        def fn(e, method=method, args=args, kw=kw, inc=inc):
            if method == "_recip_nr":
                out, in_, scratch = args
                import concourse.dve_ops as dve_ops
                inst = e._custom_dve(dve_ops.RECIPROCAL_APPROX_NR, out=out,
                                     in0=in_, in1=scratch, s0=2.0)
            else:
                inst = getattr(e, method)(*args, **kw)
            if inc is not None:
                inst.then_inc(inc[0].h, inc[1])
        self.emit(eng, fn)

    def wait(self, eng, sem, val=None):
        v = sem.n if val is None else val
        if v <= 0:
            return
        self.emit(eng, lambda e, s=sem.h, v=v: e.wait_ge(s, v))

    def dma(self, qname, out, in_, sem, **kw):
        sem.is_dma = True
        self.op(qname, "dma_start", out, in_, inc=(sem, 16), **kw)

    def set_mark(self, key, sem):
        self.mark[key] = (sem, sem.n)

    def wait_mark(self, eng, key):
        if key in self.mark:
            sem, n = self.mark[key]
            self.wait(eng, sem, n)

    def finalize(self, block):
        for eng in ENGINES:
            fns = self.q[eng]
            if not fns:
                continue

            def body(e, fns=fns):
                for fn in fns:
                    fn(e)
            getattr(block, eng)(body)


def _sublayer_barrier(c):
    P = c.P
    for eng in ("tensor", "vector", "scalar", "sync", "gpsimd"):
        for s in (c.s_t, c.s_v, c.s_s):
            P.wait(eng, s)
        for s in list(P._sems.values()):
            if s.is_dma:
                P.wait(eng, s)



_UNIQ = [0]


def _u(name):
    _UNIQ[0] += 1
    return f"{name}_{_UNIQ[0]}"


def host_constants():
    ident = np.eye(128, dtype=np.float32)
    inv_freq = 1.0 / (10000.0 ** (np.arange(0, HD, 2, dtype=np.float64) / HD))
    t = np.arange(S, dtype=np.float64)
    emb = t[:, None] * np.concatenate([inv_freq, inv_freq])[None, :]
    cos = np.cos(emb).astype(np.float32)
    sin = np.sin(emb).astype(np.float32)
    sgn = np.where(np.arange(HD) % 2 == 0, -1.0, 1.0).astype(np.float32)
    sin_eff = sin * sgn[None, :]
    cos_fm = np.concatenate([cos.T, cos.T], axis=0).copy()
    sin_fm = np.concatenate([sin_eff.T, sin_eff.T], axis=0).copy()
    perm = np.zeros((128, 128), dtype=np.float32)
    for i in range(128):
        perm[i ^ 1, i] = 1.0
    mask_u = np.triu(np.ones((128, 128), dtype=np.float32), 1)
    mask_l = np.tril(np.ones((128, 128), dtype=np.float32), -1)
    return {"c_ident": ident, "c_cos": cos_fm, "c_sin": sin_fm,
            "c_perm": perm, "c_mask_u": mask_u, "c_mask_l": mask_l}


def fmc(d, t0, n):
    return slice(d * S + t0, d * S + t0 + n)


class Ctx:
    pass


def build_program(active=None):
    if active is None:
        active = SUBLAYERS
    nc = bass.Bass("TRN2", target_bir_lowering=False, debug=False)
    P = Prog(nc)
    c = Ctx()
    c.nc, c.P = nc, P

    def din(name, shape, dtype=F32):
        return nc.dram_tensor(name, list(shape), dtype, kind="ExternalInput").ap()

    c.x_in = din("L0_attn_a_x", (SEQ_PER_CORE, S, D))
    c.k0 = din("L0_attn_a_kernel", (3, D, D))
    c.l0b_w = {n: din("L0_attn_b_" + n + "_w", (D, D)) for n in "qkvo"}
    c.l0b_b = {n: din("L0_attn_b_" + n + "_b", (D,)) for n in "qkvo"}
    c.l0b_qn = din("L0_attn_b_qn_scale", (D,))
    c.l0b_kn = din("L0_attn_b_kn_scale", (D,))
    c.l0f = {"w_gate": din("L0_ffn_w_gate", (H0, D), I32),
             "w_up": din("L0_ffn_w_up", (H0, D), I32),
             "w_down": din("L0_ffn_w_down", (D, H0), I32)}
    c.l1a_w = {n: din("L1_attn_a_" + n, (D, D)) for n in ("wq", "wk", "wv", "wo")}
    c.l1a_w1 = din("L1_attn_a_w1", (H1A, D))
    c.l1a_w2 = din("L1_attn_a_w2", (D, H1A))
    c.l1a_v = {n: din("L1_attn_a_" + n, (D,)) for n in
               ("ln1_w", "ln1_b", "ln2_w", "ln2_b", "b2")}
    c.l1a_b1 = din("L1_attn_a_b1", (H1A,))
    c.l1a_lr = din("L1_attn_a_lr_scale", (NH, 1, 1))
    c.l1b_win = din("L1_attn_b_w_in", (D, 3 * D))
    c.l1b_bin = din("L1_attn_b_b_in", (3 * D,))
    c.l1b_wout = din("L1_attn_b_w_out", (D, D))
    c.l1b_bout = din("L1_attn_b_b_out", (D,))
    c.l1f = {"w_gate": din("L1_ffn_w_gate", (H1, D)),
             "w_up": din("L1_ffn_w_up", (H1, D)),
             "w_down": din("L1_ffn_w_down", (D, H1))}
    c.consts_np = host_constants()
    c.consts = {n: din(n, v.shape) for n, v in c.consts_np.items()}
    c.out = nc.dram_tensor("out", [SEQ_PER_CORE, S, D], F32,
                           kind="ExternalOutput").ap()
    c.dbg = nc.dram_tensor("dbg", [128, DT * S], F32,
                           kind="ExternalOutput").ap() if DBG else None

    def dsc(name, shape, dtype=BF16):
        return nc.dram_tensor(name, list(shape), dtype, kind="Internal").ap()

    c.st = {}
    for n in "qkvo":
        c.st["l0b_" + n] = dsc("st_l0b_" + n, (D, D))
        c.st["l0b_" + n + "T"] = dsc("st_l0b_" + n + "T", (D, D))
    for n in ("wq", "wk", "wv", "wo"):
        c.st["l1a_" + n] = dsc("st_l1a_" + n, (D, D))
        c.st["l1a_" + n + "T"] = dsc("st_l1a_" + n + "T", (D, D))
    c.st["l1a_w1"] = dsc("st_l1a_w1", (H1A, D))
    c.st["l1a_w1T"] = dsc("st_l1a_w1T", (D, H1A))
    c.st["l1a_w2"] = dsc("st_l1a_w2", (D, H1A))
    c.st["l1a_w2T"] = dsc("st_l1a_w2T", (H1A, D))
    c.st["l0f_g"] = dsc("st_l0f_g", (H0, D))
    c.st["l0f_gT"] = dsc("st_l0f_gT", (D, H0))
    c.st["l0f_u"] = dsc("st_l0f_u", (H0, D))
    c.st["l0f_uT"] = dsc("st_l0f_uT", (D, H0))
    c.st["l0f_d"] = dsc("st_l0f_d", (D, H0))
    c.st["l0f_dT"] = dsc("st_l0f_dT", (H0, D))
    c.st["l0a_k"] = dsc("st_l0a_k", (3, D, D))
    c.st["l1b_win"] = dsc("st_l1b_win", (D, 3 * D))
    c.st["l1b_wout"] = dsc("st_l1b_wout", (D, D))
    c.st["l1f_gT"] = dsc("st_l1f_gT", (D, H1))
    c.st["l1f_uT"] = dsc("st_l1f_uT", (D, H1))
    c.st["l1f_dT"] = dsc("st_l1f_dT", (H1, D))

    c.s_t = P.sem("s_t")
    c.s_v = P.sem("s_v")
    c.s_s = P.sem("s_s")
    c.s_stage = P.sem("s_stage")
    c.s_const = P.sem("s_const")
    c.s_gp = P.sem("s_gp")

    c.XR = P.sbuf("XR", [128, DT * S], F32)
    c.XT = P.sbuf("XT", [128, DT * S], BF16)
    c.CID = P.sbuf("CID", [128, 128], F32)
    c.CIDB = P.sbuf("CIDB", [128, 128], BF16)
    c.CPERM = P.sbuf("CPERM", [128, 128], F32)
    c.ONES = P.sbuf("ONES", [128, 64], F32)
    c.ONER = P.sbuf("ONER", [1, 128], F32)
    c.BIAS = P.sbuf("BIAS", [128, 192], F32)
    c.MASKU = P.sbuf("MASKU", [128, 128], F32)
    c.MASKL = P.sbuf("MASKL", [128, 128], F32)
    c.CNST = P.sbuf("CNST", [128, 4], F32)
    c.ROWA = P.sbuf("ROWA", [1, TCH], F32)
    c.ROWB = P.sbuf("ROWB", [1, TCH], F32)
    c.ln_bufs = None

    for src, dst in (
        (c.l0b_w["q"], c.st["l0b_q"]), (c.l0b_w["k"], c.st["l0b_k"]),
        (c.l0b_w["v"], c.st["l0b_v"]), (c.l0b_w["o"], c.st["l0b_o"]),
        (c.l1a_w["wq"], c.st["l1a_wq"]), (c.l1a_w["wk"], c.st["l1a_wk"]),
        (c.l1a_w["wv"], c.st["l1a_wv"]), (c.l1a_w["wo"], c.st["l1a_wo"]),
        (c.l1a_w1, c.st["l1a_w1"]), (c.l1a_w2, c.st["l1a_w2"]),
        (c.l0f["w_gate"], c.st["l0f_g"]), (c.l0f["w_up"], c.st["l0f_u"]),
        (c.l0f["w_down"], c.st["l0f_d"]),
        (c.k0, c.st["l0a_k"]),
        (c.l1b_win, c.st["l1b_win"]), (c.l1b_wout, c.st["l1b_wout"]),
    ):
        P.dma("gpsimd", dst[...], src[...], c.s_stage)

    P.dma("sync", c.CID[:, :], c.consts["c_ident"][:, :], c.s_const)
    P.dma("sync", c.CPERM[:, :], c.consts["c_perm"][:, :], c.s_const)
    P.dma("sync", c.MASKU[:, :], c.consts["c_mask_u"][:, :], c.s_const)
    P.dma("sync", c.MASKL[:, :], c.consts["c_mask_l"][:, :], c.s_const)
    P.op("gpsimd", "memset", c.CNST[:, 0:1], EPS_LN)
    P.op("gpsimd", "memset", c.CNST[:, 1:2], EPS_RMS)
    P.op("gpsimd", "memset", c.CNST[:, 2:3], EPS_RENORM)
    P.op("gpsimd", "memset", c.ONES[:, :], 1.0)
    P.op("gpsimd", "memset", c.ONER[:, :], 1.0, inc=(c.s_gp, 1))
    P.wait("vector", c.s_const)
    P.op("vector", "tensor_copy", c.CIDB[:, :], c.CID[:, :], inc=(c.s_v, 1))

    _emit_bias_pack(c)

    _emit_pretranspose_weights(c, active)
    if ("l1f1" in active) or ("l1f2" in active):
        _emit_l1f_wquant(c)

    for eng in ("tensor", "vector", "scalar"):
        P.wait(eng, c.s_gp)
        P.wait(eng, c.s_const)

    for seq in range(SEQ_PER_CORE):
        c.seq = seq
        _emit_load_x(c)
        if DBG and seq == 0 and DBG[1] == -1:
            _emit_dbg_dump(c, DBG[0])
        for idx_, name in enumerate(active):
            c.tag = f"s{seq}{name}"
            if name in ("l0f1", "l0f2"):
                _emit_ln(c)
                _emit_l0ffn(c)
            elif name == "l0a":
                _emit_ln(c)
                _emit_l0a(c)
            elif name == "l0b":
                _emit_ln(c)
                _emit_l0b(c)
            elif name == "l1b":
                _emit_ln(c)
                _emit_l1b(c)
            elif name in ("l1f1", "l1f2"):
                _emit_l1ffn(c)
            elif name == "l1a":
                _emit_l1a(c)
            if DBG and seq == 0 and DBG[1] == idx_:
                _emit_dbg_dump(c, DBG[0])
        _emit_store_x(c)

    for s in P._sems.values():
        if s.name.startswith(("s_o", "s_x", "s_vstg", "s_const", "s_stage",
                              "s_w")):
            P.wait("sync", s)
    with P.es:
        blk = P.es.enter_context(nc.Block())
        P.finalize(blk)
    return c


def _emit_bias_pack(c):
    P, nc = c.P, c.nc
    vec_srcs = [
        (c.l0b_b["q"], 0, 8), (c.l0b_b["k"], 8, 8), (c.l0b_b["v"], 16, 8),
        (c.l0b_b["o"], 24, 8), (c.l0b_qn, 32, 8), (c.l0b_kn, 40, 8),
        (c.l1a_v["ln1_w"], 48, 8), (c.l1a_v["ln1_b"], 56, 8),
        (c.l1a_v["ln2_w"], 64, 8), (c.l1a_v["ln2_b"], 72, 8),
        (c.l1a_b1, 80, 32), (c.l1a_v["b2"], 112, 8),
        (c.l1b_bin, 120, 24), (c.l1b_bout, 144, 8),
    ]
    P.wait("tensor", c.s_gp)
    P.wait("tensor", c.s_const)
    es = ExitStack()
    VS = es.enter_context(nc.sbuf_tensor(_u("VSTG"), [32, 128], F32))
    LR = es.enter_context(nc.sbuf_tensor(_u("LRROW"), [1, 16], F32))
    PSV = es.enter_context(nc.psum_tensor(_u("PS_V"), [128, 512], F32))
    with es:
        s_vstg = P.sem("s_vstg")
        for src, col0, n in vec_srcs:
            P.wait_mark("sync", "VSTG")
            P.dma("sync", VS[0:n, :], src.rearrange("(a p) -> a p", p=128), s_vstg)
            P.wait("tensor", s_vstg)
            P.wait_mark("tensor", "PSV")
            P.op("tensor", "transpose", PSV[:, 0:n], VS[0:n, :],
                 c.CID[0:n, 0:n], inc=(c.s_t, 1))
            P.set_mark("VSTG", c.s_t)
            P.wait("scalar", c.s_t)
            P.op("scalar", "copy", c.BIAS[:, col0:col0 + n], PSV[:, 0:n],
                 inc=(c.s_s, 1))
            P.set_mark("PSV", c.s_s)
        P.wait_mark("sync", "VSTG")
        P.dma("sync", LR[0:1, :], c.l1a_lr.rearrange("h a b -> (a b) h"), s_vstg)
        P.wait("tensor", s_vstg)
        P.wait_mark("tensor", "PSV")
        P.op("tensor", "matmul", PSV[:, 16:32], c.ONER[0:1, :], LR[0:1, :],
             start=True, stop=True)
        P.op("tensor", "matmul", PSV[0:64, 32:40], c.ONER[0:1, 0:64],
             LR[0:1, 0:8], start=True, stop=True, inc=(c.s_t, 1))
        P.op("gpsimd", "memset", c.BIAS[0:64, 184:185], 1.0)
        P.op("gpsimd", "memset", c.BIAS[64:128, 184:185], 0.0)
        P.op("gpsimd", "memset", c.BIAS[0:64, 185:186], 0.0)
        P.op("gpsimd", "memset", c.BIAS[64:128, 185:186], 1.0,
             inc=(c.s_gp, 1))
        P.wait("scalar", c.s_t)
        P.op("scalar", "mul", c.BIAS[:, 152:168], PSV[:, 16:32], INNER_LR,
             inc=(c.s_s, 1))
        P.wait("scalar", c.s_s)
        P.op("scalar", "mul", c.BIAS[:, 168:184], PSV[:, 16:32], -INNER_LR,
             inc=(c.s_s, 1))
        P.set_mark("PSV", c.s_s)


def bias_col(c, col):
    return c.BIAS[:, col:col + 1]


def _emit_dbg_dump(c, which):
    P, nc = c.P, c.nc
    _sublayer_barrier(c)
    buf = {"xt": c.XT, "xr": c.XR}.get(which)
    es = ExitStack()
    TMP = es.enter_context(nc.sbuf_tensor(_u("dbgt"), [128, 2048], F32))
    with es:
        for d in range(DT):
            P.wait_mark("vector", "dbgtmp")
            P.op("vector", "tensor_copy", TMP[:, :], buf[:, fmc(d, 0, S)],
                 inc=(c.s_v, 1))
            s_od = P.sem("s_odbg")
            P.wait("sync", c.s_v)
            P.dma("sync", c.dbg[:, d * S:(d + 1) * S], TMP[:, :], s_od)
            P.set_mark("dbgtmp", s_od)


def _emit_load_x(c):
    P, nc = c.P, c.nc
    _sublayer_barrier(c)
    es = ExitStack()
    TMP = es.enter_context(nc.sbuf_tensor(_u(f"ldx{c.seq}"), [128, 2 * D], F32))
    PS = es.enter_context(nc.psum_tensor(_u(f"ldxp{c.seq}"), [128, 2048], F32))
    with es:
        for tt in range(S // 128):
            pr = tt % 2
            s_x = P.sem(f"s_x{pr}")
            P.wait_mark("sync", f"ldx{pr}")
            P.dma("sync", TMP[:, pr * D:(pr + 1) * D],
                  c.x_in[c.seq, tt * 128:(tt + 1) * 128, :], s_x)
            P.wait("tensor", s_x)
            P.wait_mark("tensor", f"ldxp{pr}")
            for d in range(DT):
                P.op("tensor", "transpose", PS[:, (pr * 8 + d) * 128:(pr * 8 + d + 1) * 128],
                     TMP[:, (pr * D) + d * 128:(pr * D) + (d + 1) * 128], c.CID[:, :],
                     inc=(c.s_t, 1) if d == DT - 1 else None)
            P.set_mark(f"ldx{pr}", c.s_t)
            P.wait("vector", c.s_t)
            for d in range(DT):
                P.op("vector", "tensor_copy", c.XR[:, fmc(d, tt * 128, 128)],
                     PS[:, (pr * 8 + d) * 128:(pr * 8 + d + 1) * 128],
                     inc=(c.s_v, 1) if d == DT - 1 else None)
            P.set_mark(f"ldxp{pr}", c.s_v)


def _emit_store_x(c):
    P, nc = c.P, c.nc
    _sublayer_barrier(c)
    es = ExitStack()
    TMP = es.enter_context(nc.sbuf_tensor(_u(f"stx{c.seq}"), [128, 2 * D], F32))
    PS = es.enter_context(nc.psum_tensor(_u(f"stxp{c.seq}"), [128, 2048], F32))
    with es:
        for tt in range(S // 128):
            pr = tt % 2
            P.wait_mark("tensor", f"stxp{pr}")
            for d in range(DT):
                P.op("tensor", "transpose", PS[:, (pr * 8 + d) * 128:(pr * 8 + d + 1) * 128],
                     c.XR[:, fmc(d, tt * 128, 128)], c.CID[:, :],
                     inc=(c.s_t, 1) if d == DT - 1 else None)
            P.wait("vector", c.s_t)
            P.wait_mark("vector", f"stx{pr}")
            for d in range(DT):
                P.op("vector", "tensor_copy", TMP[:, pr * D + d * 128:pr * D + (d + 1) * 128],
                     PS[:, (pr * 8 + d) * 128:(pr * 8 + d + 1) * 128],
                     inc=(c.s_v, 1) if d == DT - 1 else None)
            P.set_mark(f"stxp{pr}", c.s_v)
            s_o = P.sem(f"s_o{pr}")
            P.wait("sync", c.s_v)
            P.dma("sync", c.out[c.seq, tt * 128:(tt + 1) * 128, :],
                  TMP[:, pr * D:(pr + 1) * D], s_o)
            P.set_mark(f"stx{pr}", s_o)


def _emit_ln(c, src=None, dst=None, w_col=None, b_col=None, dst_of=None,
             t_lo=0, t_hi=S, barrier=True):
    P, nc = c.P, c.nc
    if barrier:
        _sublayer_barrier(c)
    if src is None:
        src = c.XR
    if dst is None:
        dst = c.XT
    if dst_of is None:
        dst_of = lambda d, t0, n: dst[:, fmc(d, t0, n)]
    es = ExitStack()
    ext = getattr(c, "ln_bufs", None)
    if ext is None:
        SQ = es.enter_context(nc.sbuf_tensor(_u("ln_sq"), [128, 2 * TCH], F32))
        TM = es.enter_context(nc.sbuf_tensor(_u("ln_tm"), [128, TCH], F32))
        PSS = es.enter_context(nc.psum_tensor(_u("ln_pss"), [1, 1024], F32))
        PSB = es.enter_context(nc.psum_tensor(_u("ln_psb"), [128, 1024], F32))
    else:
        SQ, TM, PSS, PSB = ext
    RA = c.ROWA
    RB = c.ROWB
    with es:
        for t0 in range(t_lo, t_hi, TCH):
            for d in range(DT):
                pr = d % 2
                P.wait_mark("scalar", f"lnsq{pr}")
                P.op("scalar", "activation", SQ[:, pr * TCH:(pr + 1) * TCH],
                     src[:, fmc(d, t0, TCH)], AF.Square, inc=(c.s_s, 1))
                P.wait("tensor", c.s_s)
                if d == 0:
                    P.wait_mark("tensor", "lnpss")
                P.op("tensor", "matmul", PSS[0:1, 0:TCH], c.ONES[:, 0:1],
                     src[:, fmc(d, t0, TCH)], start=(d == 0), stop=(d == DT - 1))
                P.op("tensor", "matmul", PSS[0:1, TCH:2 * TCH], c.ONES[:, 0:1],
                     SQ[:, pr * TCH:(pr + 1) * TCH], start=(d == 0),
                     stop=(d == DT - 1), inc=(c.s_t, 1))
                P.set_mark(f"lnsq{pr}", c.s_t)
            P.wait("vector", c.s_t)
            P.op("vector", "tensor_scalar_mul", RA[0:1, :], PSS[0:1, 0:TCH],
                 1.0 / D, inc=(c.s_v, 1))
            P.wait("vector", c.s_v)
            P.op("vector", "scalar_tensor_tensor", RB[0:1, :], RA[0:1, :], 1.0,
                 RA[0:1, :], ALU.mult, ALU.mult, inc=(c.s_v, 1))
            P.wait("vector", c.s_v)
            P.op("vector", "scalar_tensor_tensor", RB[0:1, :],
                 PSS[0:1, TCH:2 * TCH], 1.0 / D, RB[0:1, :],
                 ALU.mult, ALU.subtract, inc=(c.s_v, 1))
            P.set_mark("lnpss", c.s_v)
            P.wait("scalar", c.s_v)
            P.op("scalar", "activation", RB[0:1, :], RB[0:1, :], AF.Sqrt,
                 bias=c.CNST[0:1, 0:1], scale=1.0, inc=(c.s_s, 1))
            P.wait("vector", c.s_s)
            P.op("vector", "reciprocal", RB[0:1, :], RB[0:1, :], inc=(c.s_v, 1))
            P.wait("vector", c.s_v)
            P.op("vector", "scalar_tensor_tensor", RA[0:1, :], RA[0:1, :], -1.0,
                 RB[0:1, :], ALU.mult, ALU.mult, inc=(c.s_v, 1))
            P.wait("tensor", c.s_v)
            P.wait_mark("tensor", "lnpsb")
            P.op("tensor", "matmul", PSB[:, 0:TCH], c.ONER[0:1, :], RB[0:1, :],
                 start=True, stop=True)
            P.op("tensor", "matmul", PSB[:, TCH:2 * TCH], c.ONER[0:1, :],
                 RA[0:1, :], start=True, stop=True, inc=(c.s_t, 1))
            P.wait("vector", c.s_t)
            for d in range(DT):
                P.wait("vector", c.s_v)
                P.op("vector", "tensor_mul", TM[:, :],
                     src[:, fmc(d, t0, TCH)], PSB[:, 0:TCH], inc=(c.s_v, 1))
                P.wait("vector", c.s_v)
                if w_col is None:
                    P.op("vector", "tensor_add", dst_of(d, t0, TCH),
                         TM[:, :], PSB[:, TCH:2 * TCH], inc=(c.s_v, 1))
                else:
                    P.op("vector", "tensor_add", TM[:, :], TM[:, :],
                         PSB[:, TCH:2 * TCH], inc=(c.s_v, 1))
                    P.wait("scalar", c.s_v)
                    P.op("scalar", "activation", dst_of(d, t0, TCH),
                         TM[:, :], AF.Identity, bias=bias_col(c, b_col + d),
                         scale=bias_col(c, w_col + d), inc=(c.s_s, 1))
                if w_col is not None:
                    P.wait("vector", c.s_s)
            if w_col is None:
                P.set_mark("lnpsb", c.s_v)
            else:
                P.set_mark("lnpsb", c.s_s)


def _emit_l0ffn(c):
    P, nc = c.P, c.nc
    _sublayer_barrier(c)
    NHT = H0 // 128
    GRP = 8
    NGRP = NHT // GRP
    es = ExitStack()
    WG = es.enter_context(nc.sbuf_tensor(_u("f0_wg"), [128, GRP * 1024], BF16))
    WU = es.enter_context(nc.sbuf_tensor(_u("f0_wu"), [128, GRP * 1024], BF16))
    WD = es.enter_context(nc.sbuf_tensor(_u("f0_wd"), [128, GRP * 1024], BF16))
    HGB = es.enter_context(nc.sbuf_tensor(_u("f0_hg"), [128, GRP * S], BF16))
    SGT = es.enter_context(nc.sbuf_tensor(_u("f0_sg"), [128, 2 * TCH], BF16))
    PSG = es.enter_context(nc.psum_tensor(_u("f0_psg"), [128, 1024], F32))
    PSU = es.enter_context(nc.psum_tensor(_u("f0_psu"), [128, 1024], F32))
    PSD = es.enter_context(nc.psum_tensor(_u("f0_psd"), [128, 1024], F32))
    with es:
        for g in range(NGRP):
            s_fw = P.sem("s_w0f")
            P.wait_mark("sync", "f0_w")
            _ld3(P, WG[:, :], 1024, c.st["l0f_gT"], H0, 0, DT, g * 1024, s_fw)
            _ld3(P, WU[:, :], 1024, c.st["l0f_uT"], H0, 0, DT, g * 1024, s_fw)
            _ld3(P, WD[:, :], 1024, c.st["l0f_dT"], D, g * GRP * 128, GRP, 0,
                 s_fw)
            w_ready = s_fw
            for h in range(GRP):
                for tc in range(NTCH):
                    pr = tc % 2
                    t0 = tc * TCH
                    P.wait("tensor", w_ready)
                    P.wait_mark("tensor", f"f0_psg{pr}")
                    for d in range(DT):
                        P.op("tensor", "matmul", PSG[:, pr * TCH:(pr + 1) * TCH],
                             WG[:, d * 1024 + h * 128:d * 1024 + (h + 1) * 128],
                             c.XT[:, fmc(d, t0, TCH)], start=(d == 0),
                             stop=(d == DT - 1),
                             inc=(c.s_t, 1) if d == DT - 1 else None)
                    P.wait_mark("tensor", f"f0_psu{pr}")
                    for d in range(DT):
                        P.op("tensor", "matmul", PSU[:, pr * TCH:(pr + 1) * TCH],
                             WU[:, d * 1024 + h * 128:d * 1024 + (h + 1) * 128],
                             c.XT[:, fmc(d, t0, TCH)], start=(d == 0),
                             stop=(d == DT - 1),
                             inc=(c.s_t, 1) if d == DT - 1 else None)
                    P.wait("scalar", c.s_t, c.s_t.n - 1)
                    P.wait_mark("scalar", f"f0_sg{pr}")
                    P.op("scalar", "activation", SGT[:, pr * TCH:(pr + 1) * TCH],
                         PSG[:, pr * TCH:(pr + 1) * TCH], AF.Silu, bias=0.0,
                         scale=QD, inc=(c.s_s, 1))
                    P.set_mark(f"f0_psg{pr}", c.s_s)
                    P.wait("vector", c.s_s)
                    P.wait("vector", c.s_t)
                    P.op("vector", "scalar_tensor_tensor",
                         HGB[:, h * S + t0:h * S + t0 + TCH],
                         SGT[:, pr * TCH:(pr + 1) * TCH], QD,
                         PSU[:, pr * TCH:(pr + 1) * TCH], ALU.mult, ALU.mult,
                         inc=(c.s_v, 1))
                    P.set_mark(f"f0_psu{pr}", c.s_v)
                    P.set_mark(f"f0_sg{pr}", c.s_v)
            h_ready = c.s_v.n
            for dcx in range(DT):
                for tc in range(NTCH):
                    pr = tc % 2
                    t0 = tc * TCH
                    P.wait("tensor", c.s_v, h_ready)
                    P.wait_mark("tensor", f"f0_psd{pr}")
                    for h in range(GRP):
                        P.op("tensor", "matmul", PSD[:, pr * TCH:(pr + 1) * TCH],
                             WD[:, h * 1024 + dcx * 128:h * 1024 + (dcx + 1) * 128],
                             HGB[:, h * S + t0:h * S + t0 + TCH], start=(h == 0),
                             stop=(h == GRP - 1),
                             inc=(c.s_t, 1) if h == GRP - 1 else None)
                    P.wait("vector", c.s_t)
                    P.op("vector", "scalar_tensor_tensor",
                         c.XR[:, fmc(dcx, t0, TCH)],
                         PSD[:, pr * TCH:(pr + 1) * TCH], QD,
                         c.XR[:, fmc(dcx, t0, TCH)], ALU.mult, ALU.add,
                         inc=(c.s_v, 1))
                    P.set_mark(f"f0_psd{pr}", c.s_v)
            P.set_mark("f0_w", c.s_t)


def _attn_tail(c, pair, kind, out_hook, h, qg, par, pv_cnt):
    P = c.P
    q0 = qg * 512
    po = par * 512
    p_pv, p_bc = c.p_pv, c.p_bc
    PVN, NB, RECR = c.a_PVN, c.a_NB, c.a_RECR
    kPV = p_pv.name
    kBC0 = p_bc.name + "0"
    kBC1 = p_bc.name + "1"
    P.wait("vector", c.s_t, pv_cnt)
    if kind == "sigmoid":
        P.op("vector", "tensor_scalar_add", RECR[64:65, :],
             p_pv[64:65, po:po + 512], EPS_RENORM, inc=(c.s_v, 1))
        P.wait("vector", c.s_v)
        P.op("vector", "reciprocal", RECR[64:65, :], RECR[64:65, :],
             inc=(c.s_v, 1))
    else:
        P.op("vector", "reciprocal", RECR[64:65, :], p_pv[64:65, po:po + 512],
             inc=(c.s_v, 1))
    P.set_mark(kPV + f"_{par}r", c.s_v)
    P.wait("tensor", c.s_v)
    P.wait_mark("tensor", kBC0)
    P.op("tensor", "matmul", p_bc[0:64, :], c.ONES[64:65, 0:64],
         RECR[64:65, :], start=True, stop=True, inc=(c.s_t, 1))
    P.wait("scalar", c.s_t)
    P.wait_mark("scalar", PVN.name)
    P.op("scalar", "copy", PVN[0:64, :], p_pv[0:64, po:po + 512],
         inc=(c.s_s, 1))
    P.set_mark(kPV + f"_{par}", c.s_s)
    P.wait("vector", c.s_s)
    P.op("vector", "tensor_mul", PVN[0:64, :], PVN[0:64, :],
         p_bc[0:64, :], inc=(c.s_v, 1))
    P.set_mark(kBC0, c.s_v)
    if out_hook is not None:
        out_hook(h, qg, PVN)
        P.set_mark(PVN.name, c.s_v)
    elif h == 0:
        P.wait("vector", c.s_v)
        P.op("vector", "tensor_add", c.XR[0:64, fmc(pair, q0, 512)],
             c.XR[0:64, fmc(pair, q0, 512)], PVN[0:64, :], inc=(c.s_v, 1))
        P.set_mark(PVN.name, c.s_v)
    else:
        P.wait("vector", c.s_v)
        P.wait_mark("vector", NB.name)
        P.op("vector", "tensor_copy", NB[0:64, :], PVN[0:64, :],
             inc=(c.s_v, 1))
        P.set_mark(PVN.name, c.s_v)
        P.wait("tensor", c.s_v)
        P.wait_mark("tensor", kBC1)
        P.op("tensor", "matmul", p_bc[64:128, :], c.CIDB[0:64, 0:64],
             NB[0:64, :], start=True, stop=True, tile_position=(0, 64),
             inc=(c.s_t, 1))
        P.set_mark(NB.name, c.s_t)
        P.wait("vector", c.s_t)
        P.op("vector", "tensor_add", c.XR[64:128, fmc(pair, q0, 512)],
             c.XR[64:128, fmc(pair, q0, 512)], p_bc[64:128, :],
             inc=(c.s_v, 1))
        P.set_mark(kBC1, c.s_v)


def _emit_attn_core_gm(c, QH, KH, VH, pair, proj_scale, out_hook=None):
    P, nc = c.P, c.nc
    QG = 512
    NQG = S // QG
    VT = c.a_VT
    RS = c.a_RS2
    ES = c.a_ES
    MROW = c.a_MR2
    BIASM = c.a_BIASM
    RECR = c.a_RECR
    PVN = c.a_PVN
    p_sc = c.p_sc
    p_pv = c.p_pv
    p_bc = c.p_bc
    p_tr = c.p_tr
    p_sm = c.p_sm
    kTR = p_tr.name
    kSC = p_sc.name
    kSM = p_sm.name
    kSMb = p_sm.name + "b"
    kBC0 = p_bc.name + "0"
    kPV = p_pv.name
    kRS = RS.name
    kES = ES.name
    P.wait("tensor", c.s_s)
    for kt in range(16):
        P.wait_mark("tensor", kTR)
        P.op("tensor", "transpose", p_tr[:, 0:128],
             VH[:, kt * 128:(kt + 1) * 128], c.CIDB[:, :], inc=(c.s_t, 1))
        P.wait("vector", c.s_t)
        P.op("vector", "tensor_copy", VT[:, kt * 130:kt * 130 + 64],
             p_tr[:, 0:64])
        P.op("vector", "tensor_copy", VT[:, kt * 130 + 65:kt * 130 + 129],
             p_tr[:, 64:128], inc=(c.s_v, 1))
        P.set_mark(kTR, c.s_v)
    for h in range(2):
        hb = h * 64
        for qg in range(NQG):
            q0 = qg * QG
            for kt in range(16):
                pr = kt % 2
                P.wait_mark("tensor", kSC + f"_{pr}")
                P.op("tensor", "matmul", p_sc[:, pr * 512:(pr + 1) * 512],
                     KH[hb:hb + 64, kt * 128:(kt + 1) * 128],
                     QH[hb:hb + 64, q0:q0 + QG], start=True, stop=True,
                     inc=(c.s_t, 1))
                if kt == 0:
                    P.wait_mark("scalar", kRS)
                    P.wait_mark("vector", kRS)
                if kt % 2 == 0:
                    P.wait("scalar", c.s_t)
                    P.op("scalar", "copy", RS[:, kt * 512:(kt + 1) * 512],
                         p_sc[:, pr * 512:(pr + 1) * 512], inc=(c.s_s, 1))
                    P.set_mark(kSC + f"_{pr}", c.s_s)
                else:
                    P.wait("vector", c.s_t)
                    P.op("vector", "tensor_copy",
                         RS[:, kt * 512:(kt + 1) * 512],
                         p_sc[:, pr * 512:(pr + 1) * 512], inc=(c.s_v, 1))
                    P.set_mark(kSC + f"_{pr}", c.s_v)
            P.wait("vector", c.s_s)
            P.wait("vector", c.s_v)
            P.op("vector", "tensor_reduce", MROW[:, 0:1], RS[:, :],
                 AX.X, ALU.max, inc=(c.s_v, 1))
            P.wait("tensor", c.s_v)
            P.wait_mark("tensor", kSM)
            P.op("tensor", "transpose", p_sm[0:1, 0:128], MROW[:, 0:1],
                 c.CID[:, :], inc=(c.s_t, 1))
            P.wait("vector", c.s_t)
            P.op("vector", "tensor_reduce", MROW[0:1, 1:2], p_sm[0:1, 0:128],
                 AX.X, ALU.max, negate=True, inc=(c.s_v, 1))
            P.wait("vector", c.s_v)
            P.op("vector", "tensor_scalar_mul", MROW[0:1, 2:3],
                 MROW[0:1, 1:2], proj_scale, inc=(c.s_v, 1))
            P.set_mark(kSM, c.s_v)
            P.wait("tensor", c.s_v)
            P.wait_mark("tensor", kSMb)
            P.op("tensor", "matmul", p_sm[:, 256:257], c.ONER[0:1, :],
                 MROW[0:1, 2:3], start=True, stop=True, inc=(c.s_t, 1))
            P.wait("scalar", c.s_t)
            P.op("scalar", "copy", BIASM[:, 0:1], p_sm[:, 256:257],
                 inc=(c.s_s, 1))
            P.set_mark(kSMb, c.s_s)
            P.wait("scalar", c.s_s)
            P.wait("scalar", c.s_v)
            P.wait_mark("scalar", kES)
            P.op("scalar", "activation", ES[:, :], RS[:, :], AF.Exp,
                 bias=BIASM[:, 0:1], scale=proj_scale, inc=(c.s_s, 1))
            P.set_mark(kRS, c.s_s)
            P.wait("tensor", c.s_s)
            P.wait("tensor", c.s_v)
            P.wait_mark("tensor", kPV + "_0")
            P.wait_mark("tensor", kPV + "_0r")
            for kt in range(16):
                P.op("tensor", "matmul", p_pv[0:65, 0:512],
                     VT[:, kt * 130 + 65 * h:kt * 130 + 65 * h + 65],
                     ES[:, kt * 512:(kt + 1) * 512], start=(kt == 0),
                     stop=(kt == 15), inc=(c.s_t, 1) if kt == 15 else None)
            P.set_mark(kES, c.s_t)
            _attn_tail(c, pair, "softmax", out_hook, h, qg, 0, c.s_t.n)


def _emit_attn_core(c, QH, KH, VH, pair, kind, proj_scale, es_parent,
                    out_hook=None):
    P, nc = c.P, c.nc
    QG = 512
    NQG = S // QG
    VT = c.a_VT
    ES = c.a_ES
    BIASM = c.a_BIASM
    p_sc = c.p_sc
    p_pv = c.p_pv
    p_tr = c.p_tr
    p_sm = c.p_sm
    kTR = p_tr.name
    kSC = p_sc.name
    kSM = p_sm.name
    kPV = p_pv.name
    kES = ES.name
    P.wait("tensor", c.s_s)
    for kt in range(16):
        P.wait_mark("tensor", kTR)
        P.op("tensor", "transpose", p_tr[:, 0:128],
             VH[:, kt * 128:(kt + 1) * 128], c.CIDB[:, :], inc=(c.s_t, 1))
        P.wait("vector", c.s_t)
        P.op("vector", "tensor_copy", VT[:, kt * 130:kt * 130 + 64],
             p_tr[:, 0:64])
        P.op("vector", "tensor_copy", VT[:, kt * 130 + 65:kt * 130 + 129],
             p_tr[:, 64:128], inc=(c.s_v, 1))
        P.set_mark(kTR, c.s_v)
    vt_done = c.s_v.n

    if kind == "softmax":
        SQB = c.a_SQB
        QN = c.a_QN
        qn_mm = None
        for which, SRC in ((0, QH), (1, KH)):
            P.wait("vector", c.s_s)
            if qn_mm is not None:
                P.wait("vector", c.s_t, qn_mm)
            P.wait_mark("vector", SQB.name)
            P.op("vector", "tensor_mul", SQB[:, :], SRC[:, :], SRC[:, :],
                 inc=(c.s_v, 1))
            sq_done = c.s_v.n
            for h in range(2):
                for j in range(NQG):
                    P.wait("tensor", c.s_v, sq_done)
                    P.wait_mark("tensor", kSM)
                    P.op("tensor", "matmul", p_sm[0:1, 0:512],
                         c.a_ONESH[:, h:h + 1], SQB[:, j * 512:(j + 1) * 512],
                         start=True, stop=True, inc=(c.s_t, 1))
                    P.wait("vector", c.s_t)
                    col = which * 8 + h * 4 + j
                    P.op("vector", "tensor_reduce", QN[0:1, col:col + 1],
                         p_sm[0:1, 0:512], AX.X, ALU.max, inc=(c.s_v, 1))
                    P.set_mark(kSM, c.s_v)
            qn_mm = c.s_t.n
        P.set_mark(SQB.name, c.s_t)
        P.wait("vector", c.s_v)
        P.wait_mark("vector", QN.name)
        for r in range(4):
            P.op("vector", "tensor_reduce", QN[0:1, 16 + r:17 + r],
                 QN[0:1, r * 4:r * 4 + 4], AX.X, ALU.max,
                 inc=(c.s_v, 1) if r == 3 else None)
        P.wait("scalar", c.s_v)
        P.op("scalar", "activation", QN[0:1, 20:24], QN[0:1, 16:20],
             AF.Sqrt, bias=0.0, scale=1.0, inc=(c.s_s, 1))
        P.wait("vector", c.s_s)
        P.op("vector", "tensor_mul", QN[0:1, 24:26], QN[0:1, 20:22],
             QN[0:1, 22:24], inc=(c.s_v, 1))
        P.wait("vector", c.s_v)
        P.op("vector", "tensor_scalar_mul", QN[0:1, 26:28], QN[0:1, 24:26],
             -proj_scale, inc=(c.s_v, 1))
        for h in range(2):
            P.wait("tensor", c.s_v)
            P.wait_mark("tensor", kSM)
            P.op("tensor", "matmul", p_sm[:, 0:1], c.ONER[0:1, :],
                 QN[0:1, 26 + h:27 + h], start=True, stop=True,
                 inc=(c.s_t, 1))
            P.wait("scalar", c.s_t)
            P.wait("scalar", c.s_s)
            P.op("scalar", "copy", BIASM[:, h:h + 1], p_sm[:, 0:1],
                 inc=(c.s_s, 1))
            P.set_mark(kSM, c.s_s)
        P.set_mark(QN.name, c.s_t)

    bk = c.attn_bk
    pv_info = None
    for it in range(2 * NQG):
        h, qg = divmod(it, NQG)
        par = it % 2
        hb = h * 64
        q0 = qg * QG
        eb = 0
        P.wait_mark("scalar", kES)
        for kt in range(16):
            P.wait_mark("tensor", kSC + f"_{bk}")
            P.op("tensor", "matmul", p_sc[:, bk * 512:(bk + 1) * 512],
                 KH[hb:hb + 64, kt * 128:(kt + 1) * 128],
                 QH[hb:hb + 64, q0:q0 + QG], start=True, stop=True,
                 inc=(c.s_t, 1))
            P.wait("scalar", c.s_t)
            if kind == "softmax":
                P.op("scalar", "activation",
                     ES[:, eb + kt * 512:eb + (kt + 1) * 512],
                     p_sc[:, bk * 512:(bk + 1) * 512], AF.Exp,
                     bias=BIASM[:, h:h + 1], scale=proj_scale,
                     inc=(c.s_s, 1))
            else:
                P.op("scalar", "activation",
                     ES[:, eb + kt * 512:eb + (kt + 1) * 512],
                     p_sc[:, bk * 512:(bk + 1) * 512], AF.Sigmoid,
                     bias=0.0, scale=proj_scale, inc=(c.s_s, 1))
            P.set_mark(kSC + f"_{bk}", c.s_s)
            bk = (bk + 1) % 3
        exp_done = c.s_s.n
        if pv_info is not None:
            _attn_tail(c, pair, kind, out_hook, *pv_info)
        P.wait("tensor", c.s_s, exp_done)
        P.wait("tensor", c.s_v, vt_done)
        P.wait_mark("tensor", kPV + f"_{par}")
        P.wait_mark("tensor", kPV + f"_{par}r")
        for kt in range(16):
            P.op("tensor", "matmul", p_pv[0:65, par * 512:(par + 1) * 512],
                 VT[:, kt * 130 + 65 * h:kt * 130 + 65 * h + 65],
                 ES[:, eb + kt * 512:eb + (kt + 1) * 512], start=(kt == 0),
                 stop=(kt == 15), inc=(c.s_t, 1) if kt == 15 else None)
        P.set_mark(kES, c.s_t)
        pv_info = (h, qg, par, c.s_t.n)
    _attn_tail(c, pair, kind, out_hook, *pv_info)
    c.attn_bk = bk


def _alloc_attn_scratch(c, es, softmax=True, bound=None, gm=False):
    P, nc = c.P, c.nc
    if bound is None:
        bound = softmax
    c.a_VT = es.enter_context(nc.sbuf_tensor(_u("aVT"), [128, 16 * 130], BF16))
    c.a_ES = es.enter_context(nc.sbuf_tensor(_u("aES"), [128, 16 * 512],
                                             BF16))
    if gm:
        c.a_RS2 = es.enter_context(nc.sbuf_tensor(_u("aRS"), [128, 16 * 512],
                                                  F32))
        c.a_MR2 = es.enter_context(nc.sbuf_tensor(_u("aMR"), [128, 8], F32))
    c.a_BIASM = es.enter_context(nc.sbuf_tensor(_u("aBM"), [128, 4], F32))
    c.a_PVN = es.enter_context(nc.sbuf_tensor(_u("aPV"), [128, 512], F32))
    c.a_RECR = c.a_PVN
    c.a_NB = es.enter_context(nc.sbuf_tensor(_u("aNB"), [128, 512], BF16))
    c.p_sc = es.enter_context(nc.psum_tensor(_u("psc"), [128, 1536], F32))
    c.p_pv = es.enter_context(nc.psum_tensor(_u("ppv"), [128, 1024], F32))
    c.p_bc = es.enter_context(nc.psum_tensor(_u("pbc"), [128, 512], F32))
    c.p_tr = es.enter_context(nc.psum_tensor(_u("ptr"), [128, 1024], BF16))
    c.p_sm = es.enter_context(nc.psum_tensor(_u("psm"), [128, 512], F32))
    c.attn_bk = 0
    if bound:
        c.a_SQB = es.enter_context(nc.sbuf_tensor(_u("aSQ"), [128, S], BF16))
        c.a_QN = es.enter_context(nc.sbuf_tensor(_u("aQN"), [1, 32], F32))
        c.a_ONESH = es.enter_context(nc.sbuf_tensor(_u("aOH"), [128, 2], BF16))
        P.wait("vector", c.s_v)
        P.op("vector", "memset", c.a_ONESH[:, :], 0.0, inc=(c.s_v, 1))
        P.wait("vector", c.s_v)
        P.op("vector", "memset", c.a_ONESH[0:64, 0:1], 1.0)
        P.op("vector", "memset", c.a_ONESH[64:128, 1:2], 1.0, inc=(c.s_v, 1))
    P.wait("vector", c.s_v)
    vt = c.a_VT[:, :]
    P.op("vector", "memset",
         AP(vt.tensor, vt.offset + 64, [vt.ap[0], [130, 16], [1, 1]]), 1.0)
    P.op("vector", "memset",
         AP(vt.tensor, vt.offset + 129, [vt.ap[0], [130, 16], [1, 1]]), 1.0,
         inc=(c.s_v, 1))


def _emit_proj_pair(c, Wsl, dst, scale_col=None, bias_col_=None, s_w=None,
                    neg=False):
    P = c.P
    kSC = c.p_sc.name
    for tc in range(NTCH):
        pr = tc % 2
        t0 = tc * TCH
        if s_w is not None:
            P.wait("tensor", s_w)
        P.wait_mark("tensor", kSC + f"_{pr}")
        for d in range(DT):
            P.op("tensor", "matmul", c.p_sc[:, pr * 512:(pr + 1) * 512],
                 Wsl[:, d * 128:(d + 1) * 128], c.XT[:, fmc(d, t0, TCH)],
                 start=(d == 0), stop=(d == DT - 1),
                 inc=(c.s_t, 1) if d == DT - 1 else None)
        P.wait("scalar", c.s_t)
        P.op("scalar", "activation", dst[:, t0:t0 + TCH],
             c.p_sc[:, pr * 512:(pr + 1) * 512], AF.Identity,
             bias=bias_col_ if bias_col_ is not None else 0.0,
             scale=scale_col if scale_col is not None else 1.0,
             inc=(c.s_s, 1))
        P.set_mark(kSC + f"_{pr}", c.s_s)


def _emit_l0a(c):
    P, nc = c.P, c.nc
    _sublayer_barrier(c)
    es = ExitStack()
    QH = es.enter_context(nc.sbuf_tensor(_u("l0aQ"), [128, S], BF16))
    KH = es.enter_context(nc.sbuf_tensor(_u("l0aK"), [128, S], BF16))
    VH = es.enter_context(nc.sbuf_tensor(_u("l0aV"), [128, S], BF16))
    WP = es.enter_context(nc.sbuf_tensor(_u("l0aW"), [128, DT * 128], BF16))
    with es:
        _alloc_attn_scratch(c, es, softmax=True)
        s_w = P.sem("s_wl0a")
        for pair in range(NPAIR):
            c.tag = f"s{c.seq}l0a_p{pair}"
            for i, dst in enumerate((QH, KH, VH)):
                P.wait_mark("sync", "l0aW")
                ksrc = c.st["l0a_k"]
                in3 = AP(ksrc.tensor, ksrc.offset + i * D * D + pair * 128,
                         [[D, 128], [128 * D, DT], [1, 128]])
                P.dma("sync", WP[:, :].rearrange("p (t q) -> p t q", q=128),
                      in3, s_w)
                _emit_proj_pair(c, WP[:, :], dst, s_w=s_w)
                P.set_mark("l0aW", c.s_t)
            _emit_attn_core(c, QH, KH, VH, pair, "softmax", 0.125, es)


def _emit_oproj(c, AO, WO, bout_col, s_w):
    P, nc = c.P, c.nc
    es = ExitStack()
    TMO = es.enter_context(nc.sbuf_tensor(_u("oprT"), [128, TCH], F32))
    p_o = es.enter_context(nc.psum_tensor(_u("opr"), [128, 1024], F32))
    with es:
        for dcx in range(DT):
            for tc in range(NTCH):
                pr = tc % 2
                t0 = tc * TCH
                P.wait("tensor", s_w)
                P.wait_mark("tensor", p_o.name + f"_{pr}")
                for d in range(DT):
                    P.op("tensor", "matmul", p_o[:, pr * 512:(pr + 1) * 512],
                         WO[:, d * 1024 + dcx * 128:d * 1024 + (dcx + 1) * 128],
                         AO[:, fmc(d, t0, TCH)], start=(d == 0),
                         stop=(d == DT - 1),
                         inc=(c.s_t, 1) if d == DT - 1 else None)
                P.wait("scalar", c.s_t)
                P.wait_mark("scalar", TMO.name)
                P.op("scalar", "activation", TMO[:, :],
                     p_o[:, pr * 512:(pr + 1) * 512], AF.Identity,
                     bias=bias_col(c, bout_col + dcx), scale=1.0,
                     inc=(c.s_s, 1))
                P.set_mark(p_o.name + f"_{pr}", c.s_s)
                P.wait("vector", c.s_s)
                P.op("vector", "tensor_add", c.XR[:, fmc(dcx, t0, TCH)],
                     c.XR[:, fmc(dcx, t0, TCH)], TMO[:, :], inc=(c.s_v, 1))
                P.set_mark(TMO.name, c.s_v)


def _emit_l1b(c):
    P, nc = c.P, c.nc
    _sublayer_barrier(c)
    es = ExitStack()
    QH = es.enter_context(nc.sbuf_tensor(_u("l1bQ"), [128, S], BF16))
    KH = es.enter_context(nc.sbuf_tensor(_u("l1bK"), [128, S], BF16))
    VH = es.enter_context(nc.sbuf_tensor(_u("l1bV"), [128, S], BF16))
    WP = es.enter_context(nc.sbuf_tensor(_u("l1bW"), [128, DT * 128], BF16))
    AO = es.enter_context(nc.sbuf_tensor(_u("l1bAO"), [128, DT * S], BF16))
    with es:
        es2 = ExitStack()
        _alloc_attn_scratch(c, es2, softmax=True, bound=False, gm=True)
        s_w = P.sem("s_wl1b")

        def hook(h, qg, PVN):
            q0 = qg * 512
            pair = c.cur_pair
            P.wait("vector", c.s_v)
            if h == 0:
                P.op("vector", "tensor_copy", AO[0:64, fmc(pair, q0, 512)],
                     PVN[0:64, :], inc=(c.s_v, 1))
            else:
                P.wait_mark("vector", c.a_NB.name)
                P.op("vector", "tensor_copy", c.a_NB[0:64, :], PVN[0:64, :],
                     inc=(c.s_v, 1))
                P.wait("tensor", c.s_v)
                P.wait_mark("tensor", c.p_bc.name + "1")
                P.op("tensor", "matmul", c.p_bc[64:128, :], c.CIDB[0:64, 0:64],
                     c.a_NB[0:64, :], start=True, stop=True,
                     tile_position=(0, 64), inc=(c.s_t, 1))
                P.set_mark(c.a_NB.name, c.s_t)
                P.wait("vector", c.s_t)
                P.op("vector", "tensor_copy", AO[64:128, fmc(pair, q0, 512)],
                     c.p_bc[64:128, :], inc=(c.s_v, 1))
                P.set_mark(c.p_bc.name + "1", c.s_v)

        with es2:
            for pair in range(NPAIR):
                c.cur_pair = pair
                c.tag = f"s{c.seq}l1b_p{pair}"
                for i, (dst, bc) in enumerate(((QH, 120), (KH, 128),
                                               (VH, 136))):
                    P.wait_mark("sync", "l1bW")
                    _ld3(P, WP[:, :], 128, c.st["l1b_win"], 3 * D, 0, DT,
                         i * D + pair * 128, s_w)
                    _emit_proj_pair(c, WP[:, :], dst,
                                    bias_col_=bias_col(c, bc + pair), s_w=s_w)
                    P.set_mark("l1bW", c.s_t)
                _emit_attn_core_gm(c, QH, KH, VH, pair, 1.0, out_hook=hook)
        _sublayer_barrier(c)
        WO = es.enter_context(nc.sbuf_tensor(_u("l1bWO"), [128, DT * 1024], BF16))
        s_wo = P.sem("s_wl1bo")
        P.wait_mark("sync", "l1bWO")
        _ld3(P, WO[:, :], 1024, c.st["l1b_wout"], D, 0, DT, 0, s_wo)
        _emit_oproj(c, AO, WO, 144, s_wo)
        P.set_mark("l1bWO", c.s_t)


def _emit_l0b(c):
    P, nc = c.P, c.nc
    _sublayer_barrier(c)
    es = ExitStack()
    QH = es.enter_context(nc.sbuf_tensor(_u("l0bQ"), [128, S], BF16))
    KH = es.enter_context(nc.sbuf_tensor(_u("l0bK"), [128, S], BF16))
    VH = es.enter_context(nc.sbuf_tensor(_u("l0bV"), [128, S], BF16))
    QF = es.enter_context(nc.sbuf_tensor(_u("l0bQF"), [128, S], F32))
    CS = es.enter_context(nc.sbuf_tensor(_u("l0bCS"), [128, 2 * S], BF16))
    WP = es.enter_context(nc.sbuf_tensor(_u("l0bW"), [128, DT * 128], BF16))
    AO = es.enter_context(nc.sbuf_tensor(_u("l0bAO"), [128, DT * S], BF16))
    SQ = es.enter_context(nc.sbuf_tensor(_u("l0bSQ"), [128, 2 * TCH], F32))
    T1 = es.enter_context(nc.sbuf_tensor(_u("l0bT1"), [128, TCH], F32))
    T2 = es.enter_context(nc.sbuf_tensor(_u("l0bT2"), [128, TCH], F32))
    RRROW = es.enter_context(nc.sbuf_tensor(_u("l0bRw"), [1, S], F32))
    s_cs = P.sem("s_l0bcs")
    s_w = P.sem("s_wl0b")
    with es:
        es2 = ExitStack()
        _alloc_attn_scratch(c, es2, softmax=False)
        p_st = c.p_bc
        p_rr = c.p_sm
        p_rp = c.p_pv
        P.dma("gpsimd", CS[:, 0:S], c.consts["c_cos"][:, :], s_cs)
        P.dma("gpsimd", CS[:, S:2 * S], c.consts["c_sin"][:, :], s_cs)
        for tc in range(NTCH):
            t0 = tc * TCH
            for d in range(DT):
                pr = d % 2
                P.wait_mark("scalar", SQ.name + f"_{pr}")
                P.op("scalar", "activation", SQ[:, pr * TCH:(pr + 1) * TCH],
                     c.XT[:, fmc(d, t0, TCH)], AF.Square, inc=(c.s_s, 1))
                P.wait("tensor", c.s_s)
                if d == 0:
                    P.wait_mark("tensor", p_st.name)
                P.op("tensor", "matmul", p_st[0:1, 0:512], c.ONES[:, 0:1],
                     SQ[:, pr * TCH:(pr + 1) * TCH], start=(d == 0),
                     stop=(d == DT - 1), inc=(c.s_t, 1))
                P.set_mark(SQ.name + f"_{pr}", c.s_t)
            P.wait("vector", c.s_t)
            P.op("vector", "tensor_scalar_mul", RRROW[0:1, t0:t0 + TCH],
                 p_st[0:1, 0:512], 1.0 / D, inc=(c.s_v, 1))
            P.set_mark(p_st.name, c.s_v)
            P.wait("scalar", c.s_v)
            P.op("scalar", "activation", RRROW[0:1, t0:t0 + TCH],
                 RRROW[0:1, t0:t0 + TCH], AF.Sqrt, bias=c.CNST[0:1, 1:2],
                 scale=1.0, inc=(c.s_s, 1))
            P.wait("vector", c.s_s)
            P.op("vector", "reciprocal", RRROW[0:1, t0:t0 + TCH],
                 RRROW[0:1, t0:t0 + TCH], inc=(c.s_v, 1))
        rr_done = c.s_v.n
        P.wait("vector", s_cs)

        with es2:
            for pair in range(NPAIR):
                c.cur_pair = pair
                c.tag = f"s{c.seq}l0b_p{pair}"
                for side, dst, bcol in ((0, QH, 0), (1, KH, 8)):
                    wname = "qk"[side]
                    P.wait_mark("sync", "l0bW")
                    _ld3(P, WP[:, :], 128, c.st["l0b_" + wname + "T"], D, 0,
                         DT, pair * 128, s_w)
                    P.wait("scalar", s_w)
                    colbase = 32 if side == 0 else 40
                    for d in range(DT):
                        P.op("scalar", "activation",
                             WP[:, d * 128:(d + 1) * 128],
                             WP[:, d * 128:(d + 1) * 128],
                             AF.Identity, bias=0.0,
                             scale=bias_col(c, colbase + d), inc=(c.s_s, 1))
                    P.wait("tensor", c.s_s)
                    _emit_proj_pair(c, WP[:, :], QF)
                    P.set_mark("l0bW", c.s_t)
                    for tc in range(NTCH):
                        t0 = tc * TCH
                        P.wait("tensor", c.s_v, rr_done)
                        P.wait("tensor", c.s_s)
                        P.wait_mark("tensor", p_rr.name)
                        P.op("tensor", "matmul", p_rr[:, 0:512], c.ONER[0:1, :],
                             RRROW[0:1, t0:t0 + TCH], start=True, stop=True,
                             inc=(c.s_t, 1))
                        P.wait("vector", c.s_t)
                        P.wait_mark("vector", T1.name)
                        P.op("vector", "tensor_mul", T1[:, :],
                             QF[:, t0:t0 + TCH], p_rr[:, 0:512], inc=(c.s_v, 1))
                        P.set_mark(p_rr.name, c.s_v)
                        P.wait("vector", c.s_v)
                        P.op("vector", "tensor_scalar_add", T1[:, :], T1[:, :],
                             bias_col(c, bcol + pair), inc=(c.s_v, 1))
                        P.wait("tensor", c.s_v)
                        P.wait_mark("tensor", p_rp.name + "_0")
                        P.wait_mark("tensor", p_rp.name + "_0r")
                        P.op("tensor", "matmul", p_rp[:, 0:512], c.CPERM[:, :],
                             T1[:, :], start=True, stop=True, inc=(c.s_t, 1))
                        P.wait("vector", c.s_t)
                        P.op("vector", "tensor_mul", T2[:, :], p_rp[:, 0:512],
                             CS[:, S + t0:S + t0 + TCH], inc=(c.s_v, 1))
                        P.set_mark(p_rp.name + "_0", c.s_v)
                        P.set_mark(p_rp.name + "_0r", c.s_v)
                        P.op("vector", "tensor_mul", T1[:, :], T1[:, :],
                             CS[:, t0:t0 + TCH], inc=(c.s_v, 1))
                        P.wait("vector", c.s_v)
                        P.op("vector", "tensor_add", dst[:, t0:t0 + TCH],
                             T1[:, :], T2[:, :], inc=(c.s_v, 1))
                        P.set_mark(T1.name, c.s_v)
                P.wait_mark("sync", "l0bW")
                _ld3(P, WP[:, :], 128, c.st["l0b_vT"], D, 0, DT, pair * 128,
                     s_w)
                _emit_proj_pair(c, WP[:, :], VH,
                                bias_col_=bias_col(c, 16 + pair), s_w=s_w)
                P.set_mark("l0bW", c.s_t)
                _emit_attn_core(c, QH, KH, VH, pair, "sigmoid", 0.125, es2,
                                out_hook=_mk_ao_hook(c, AO))
        _sublayer_barrier(c)
        WO = es.enter_context(nc.sbuf_tensor(_u("l0bWO"), [128, DT * 1024], BF16))
        s_wo = P.sem("s_wl0bo")
        P.wait_mark("sync", "l0bWO")
        _ld3(P, WO[:, :], 1024, c.st["l0b_oT"], D, 0, DT, 0, s_wo)
        _emit_oproj(c, AO, WO, 24, s_wo)
        P.set_mark("l0bWO", c.s_t)


def _mk_ao_hook(c, AO):
    P = c.P

    def hook(h, qg, PVN):
        q0 = qg * 512
        pair = c.cur_pair
        P.wait("vector", c.s_v)
        if h == 0:
            P.op("vector", "tensor_copy", AO[0:64, fmc(pair, q0, 512)],
                 PVN[0:64, :], inc=(c.s_v, 1))
        else:
            P.wait_mark("vector", c.a_NB.name)
            P.op("vector", "tensor_copy", c.a_NB[0:64, :], PVN[0:64, :],
                 inc=(c.s_v, 1))
            P.wait("tensor", c.s_v)
            P.wait_mark("tensor", c.p_bc.name + "1")
            P.op("tensor", "matmul", c.p_bc[64:128, :], c.CIDB[0:64, 0:64],
                 c.a_NB[0:64, :], start=True, stop=True,
                 tile_position=(0, 64), inc=(c.s_t, 1))
            P.set_mark(c.a_NB.name, c.s_t)
            P.wait("vector", c.s_t)
            P.op("vector", "tensor_copy", AO[64:128, fmc(pair, q0, 512)],
                 c.p_bc[64:128, :], inc=(c.s_v, 1))
            P.set_mark(c.p_bc.name + "1", c.s_v)
    return hook


def _emit_l1a(c):
    P, nc = c.P, c.nc
    _sublayer_barrier(c)
    es = ExitStack()
    XI = es.enter_context(nc.sbuf_tensor(_u("aXI"), [128, DT * S], F32))
    with es:
        _emit_ln(c, src=c.XR, dst=XI, barrier=False)
        _emit_l1a_ttt(c, XI)
        _emit_l1a_mlp(c, XI)
        _sublayer_barrier(c)
        for d in range(DT):
            for tc in range(NTCH):
                P.wait("vector", c.s_v)
                P.op("vector", "tensor_add",
                     c.XR[:, fmc(d, tc * TCH, TCH)],
                     c.XR[:, fmc(d, tc * TCH, TCH)],
                     XI[:, fmc(d, tc * TCH, TCH)], inc=(c.s_v, 1))


def _emit_l1a_ttt(c, XI):
    P, nc = c.P, c.nc
    _sublayer_barrier(c)
    GT = 512
    XT = c.XT
    QG0, KG0, VG0 = 4096, 8192, 12288
    KE0 = 0
    GM0 = 1024
    VT0 = 1536
    KT0 = 1792
    TO0 = 2048
    es = ExitStack()
    W16 = es.enter_context(nc.sbuf_tensor(_u("aW16"), [128, 8192], BF16))
    WO = es.enter_context(nc.sbuf_tensor(_u("aWO"), [128, 4 * 1024], BF16))
    U = es.enter_context(nc.sbuf_tensor(_u("aU"), [128, 1024], F32))
    UB = es.enter_context(nc.sbuf_tensor(_u("aUB"), [128, 1024], BF16))
    TML = es.enter_context(nc.sbuf_tensor(_u("aTML"), [128, TCH], F32))
    W0T = es.enter_context(nc.sbuf_tensor(_u("aW0T"), [128, 512], F32))
    W0B = es.enter_context(nc.sbuf_tensor(_u("aW0B"), [128, 1024], BF16))
    pA = es.enter_context(nc.psum_tensor(_u("apA"), [128, 1024], F32))
    pB = es.enter_context(nc.psum_tensor(_u("apB"), [128, 512], F32))
    pBT = es.enter_context(nc.psum_tensor(_u("apBT"), [128, 512], F32))
    pU = es.enter_context(nc.psum_tensor(_u("apU"), [128, 1024], F32))
    pT = es.enter_context(nc.psum_tensor(_u("apT"), [128, 2048], BF16))
    s_w = P.sem("s_wl1a")
    with es:
        P.wait("gpsimd", c.s_v)
        P.wait("gpsimd", c.s_s)
        P.op("gpsimd", "memset", W0T[:, :], 0.0)
        P.op("gpsimd", "memset", W0B[:, :], 0.0, inc=(c.s_gp, 1))
        P.wait("tensor", c.s_gp)
        P.wait("scalar", c.s_gp)
        P.wait("vector", c.s_gp)

        for g in range(S // GT):
            t0g = g * GT
            P.wait("scalar", c.s_v)
            P.wait("scalar", c.s_t)
            P.wait("vector", c.s_t)
            P.wait("vector", c.s_s)
            c.ln_bufs = (U, TML, pU[0:1, 0:1024], pA[:, 0:1024])
            _emit_ln(c, src=XI, w_col=48, b_col=56,
                     dst_of=lambda d, t0, n: XT[:, d * GT + (t0 - t0g):
                                                d * GT + (t0 - t0g) + n],
                     t_lo=t0g, t_hi=t0g + GT, barrier=False)
            c.ln_bufs = None
            for wname, dst0 in (("wq", QG0), ("wk", KG0), ("wv", VG0)):
                P.wait_mark("sync", "aW16")
                _ld3(P, W16[:, :], 1024, c.st["l1a_" + wname + "T"], D, 0,
                     DT, 0, s_w)
                P.wait("tensor", s_w)
                P.wait("tensor", c.s_s)
                for jt in range(DT):
                    prj = jt % 2
                    P.wait_mark("tensor", pA.name + str(prj))
                    for d in range(DT):
                        P.op("tensor", "matmul",
                             pA[:, prj * 512:(prj + 1) * 512],
                             W16[:, d * 1024 + jt * 128:d * 1024 + (jt + 1) * 128],
                             XT[:, d * GT:(d + 1) * GT], start=(d == 0),
                             stop=(d == DT - 1),
                             inc=(c.s_t, 1) if d == DT - 1 else None)
                    P.wait("scalar", c.s_t)
                    P.op("scalar", "copy", XT[:, dst0 + jt * GT:dst0 + (jt + 1) * GT],
                         pA[:, prj * 512:(prj + 1) * 512], inc=(c.s_s, 1))
                    P.set_mark(pA.name + str(prj), c.s_s)
                P.set_mark("aW16", c.s_t)
            for cc in range(GT // 128):
                gt0 = cc * 128
                _emit_ttt_chunk(c, XI, XT, W16, WO, U, UB, W0T, W0B,
                                pA, pB, pBT, pU, pT,
                                QG0, KG0, VG0, KE0, GM0, VT0, KT0, TO0,
                                g, gt0)


def _emit_ttt_chunk(c, XI, XT, W16, WO, U, UB, W0T, W0B,
                    pA, pB, pBT, pU, pT, QG0, KG0, VG0, KE0, GM0, VT0, KT0,
                    TO0, g, gt0):
    P = c.P
    GT = 512
    chunk_t0 = g * GT + gt0
    KZ0 = {0: 0, 1: 1024}
    GM = 2048
    VT = 2560
    KT = 2816
    TO = 3072
    P.wait("scalar", c.s_s)
    P.wait("scalar", c.s_t)
    P.wait("scalar", c.s_v)
    P.wait_mark("scalar", "aKZ")
    for par in range(2):
        for p in range(8):
            P.op("scalar", "activation",
                 XT[:, KZ0[par] + p * 128:KZ0[par] + (p + 1) * 128],
                 XT[:, KG0 + p * GT + gt0:KG0 + p * GT + gt0 + 128],
                 AF.Identity, bias=0.0, scale=bias_col(c, 184 + par),
                 inc=(c.s_s, 1))
    for hg in range(4):
        bk = hg % 2
        heads = [4 * hg + i for i in range(4)]
        P.wait("tensor", c.s_s)
        P.wait_mark("tensor", pA.name + str(bk))
        for i, h in enumerate(heads):
            par = h % 2
            p = h // 2
            P.op("tensor", "matmul",
                 pA[:, bk * 512 + i * 128:bk * 512 + (i + 1) * 128],
                 XT[:, KZ0[par] + p * 128:KZ0[par] + (p + 1) * 128],
                 XT[:, KG0 + p * GT + gt0:KG0 + p * GT + gt0 + 128],
                 start=True, stop=True, inc=(c.s_t, 1) if i == 3 else None)
        P.wait("vector", c.s_t)
        P.wait_mark("vector", "aW16L")
        for i, h in enumerate(heads):
            asl = pA[:, bk * 512 + i * 128:bk * 512 + (i + 1) * 128]
            P.op("vector", "scalar_tensor_tensor",
                 W16[:, i * 128:(i + 1) * 128], asl, bias_col(c, 152 + h),
                 c.MASKL[:, :], ALU.mult, ALU.mult)
            P.op("vector", "scalar_tensor_tensor",
                 W16[:, 512 + i * 128:512 + (i + 1) * 128], asl,
                 bias_col(c, 152 + h), c.MASKU[:, :], ALU.mult, ALU.mult,
                 inc=(c.s_v, 1) if i == 3 else None)
        P.set_mark(pA.name + str(bk), c.s_v)
        for L in range(1, 7):
            P.wait("tensor", c.s_v)
            P.wait_mark("tensor", pB.name)
            P.wait_mark("tensor", pBT.name)
            for i in range(4):
                Bp = W16[:, (L - 1) * 1024 + i * 128:(L - 1) * 1024 + (i + 1) * 128]
                BTp = W16[:, (L - 1) * 1024 + 512 + i * 128:(L - 1) * 1024 + 512 + (i + 1) * 128]
                P.op("tensor", "matmul", pB[:, i * 128:(i + 1) * 128], BTp, Bp,
                     start=True, stop=True)
                P.op("tensor", "matmul", pBT[:, i * 128:(i + 1) * 128], Bp, BTp,
                     start=True, stop=True,
                     inc=(c.s_t, 1) if i == 3 else None)
            P.wait("vector", c.s_t)
            P.op("vector", "tensor_copy", W16[:, L * 1024:L * 1024 + 512],
                 pB[:, :], inc=(c.s_v, 1))
            P.op("vector", "tensor_copy", W16[:, L * 1024 + 512:(L + 1) * 1024],
                 pBT[:, :], inc=(c.s_v, 1))
            P.set_mark(pB.name, c.s_v)
            P.set_mark(pBT.name, c.s_v)
        P.set_mark("aW16L", c.s_v)
        for ip, p in enumerate((2 * hg, 2 * hg + 1)):
            prt = ip % 2
            P.wait_mark("tensor", pT.name + str(prt))
            P.op("tensor", "transpose", pT[:, prt * 1024:prt * 1024 + 128],
                 XT[:, VG0 + p * GT + gt0:VG0 + p * GT + gt0 + 128],
                 c.CIDB[:, :], inc=(c.s_t, 1))
            P.op("tensor", "transpose", pT[:, prt * 1024 + 128:prt * 1024 + 256],
                 XT[:, KG0 + p * GT + gt0:KG0 + p * GT + gt0 + 128],
                 c.CIDB[:, :], inc=(c.s_t, 1))
            P.wait("vector", c.s_t)
            P.op("vector", "tensor_copy",
                 XT[:, VT + ip * 128:VT + (ip + 1) * 128],
                 pT[:, prt * 1024:prt * 1024 + 128])
            P.op("vector", "tensor_copy",
                 XT[:, KT + ip * 128:KT + (ip + 1) * 128],
                 pT[:, prt * 1024 + 128:prt * 1024 + 256], inc=(c.s_v, 1))
            P.set_mark(pT.name + str(prt), c.s_v)
        P.wait("scalar", c.s_v)
        P.wait("scalar", c.s_s)
        for i, h in enumerate(heads):
            par = h % 2
            p = h // 2
            P.op("scalar", "activation", W0B[:, h * 64:(h + 1) * 64],
                 W0T[:, p * 64:(p + 1) * 64], AF.Identity, bias=0.0,
                 scale=bias_col(c, 184 + par), inc=(c.s_s, 1))
        P.wait("tensor", c.s_s)
        P.wait_mark("tensor", pU.name + str(bk))
        for i, h in enumerate(heads):
            par = h % 2
            p = h // 2
            P.op("tensor", "matmul",
                 pU[:, bk * 512 + i * 64:bk * 512 + (i + 1) * 64],
                 XT[:, KZ0[par] + p * 128:KZ0[par] + (p + 1) * 128],
                 W0B[:, h * 64:(h + 1) * 64],
                 start=True, stop=True, inc=(c.s_t, 1) if i == 3 else None)
        P.wait("vector", c.s_t)
        P.wait("vector", c.s_v)
        P.op("vector", "tensor_sub", U[:, hg * 256:(hg + 1) * 256],
             pU[:, bk * 512:bk * 512 + 256], XT[:, VT:VT + 256],
             inc=(c.s_v, 1))
        P.set_mark(pU.name + str(bk), c.s_v)
        P.wait("scalar", c.s_v)
        P.op("scalar", "copy", UB[:, hg * 256:(hg + 1) * 256],
             U[:, hg * 256:(hg + 1) * 256], inc=(c.s_s, 1))
        for L in range(6, -1, -1):
            P.wait("tensor", c.s_s)
            P.wait_mark("tensor", pU.name + str(1 - bk))
            for i in range(4):
                BTp = W16[:, L * 1024 + 512 + i * 128:L * 1024 + 512 + (i + 1) * 128]
                P.op("tensor", "matmul",
                     pU[:, (1 - bk) * 512 + i * 64:(1 - bk) * 512 + (i + 1) * 64],
                     BTp, UB[:, (hg * 4 + i) * 64:(hg * 4 + i + 1) * 64],
                     start=True, stop=True, inc=(c.s_t, 1) if i == 3 else None)
            P.wait("vector", c.s_t)
            P.op("vector", "tensor_tensor", U[:, hg * 256:(hg + 1) * 256],
                 U[:, hg * 256:(hg + 1) * 256],
                 pU[:, (1 - bk) * 512:(1 - bk) * 512 + 256],
                 ALU.add if L > 0 else ALU.subtract, inc=(c.s_v, 1))
            P.set_mark(pU.name + str(1 - bk), c.s_v)
            P.wait("scalar", c.s_v)
            P.op("scalar", "copy", UB[:, hg * 256:(hg + 1) * 256],
                 U[:, hg * 256:(hg + 1) * 256], inc=(c.s_s, 1))
        P.wait("scalar", c.s_s)
        for i, h in enumerate(heads):
            P.op("scalar", "activation",
                 UB[:, (hg * 4 + i) * 64:(hg * 4 + i + 1) * 64],
                 U[:, (hg * 4 + i) * 64:(hg * 4 + i + 1) * 64], AF.Identity,
                 bias=0.0, scale=bias_col(c, 168 + h), inc=(c.s_s, 1))
        P.wait("tensor", c.s_s)
        P.wait_mark("tensor", pA.name + str(bk))
        for i, h in enumerate(heads):
            par = h % 2
            p = h // 2
            P.op("tensor", "matmul",
                 pA[:, bk * 512 + i * 128:bk * 512 + (i + 1) * 128],
                 XT[:, KZ0[par] + p * 128:KZ0[par] + (p + 1) * 128],
                 XT[:, QG0 + p * GT + gt0:QG0 + p * GT + gt0 + 128],
                 start=True, stop=True, inc=(c.s_t, 1) if i == 3 else None)
        P.wait("vector", c.s_t)
        av = pA[:, bk * 512:(bk + 1) * 512]
        a3 = AP(av.tensor, av.offset, [av.ap[0], [128, 4], [1, 128]])
        mu = c.MASKU[:, :]
        mu3 = AP(mu.tensor, mu.offset, [mu.ap[0], [0, 4], [1, 128]])
        gm = XT[:, GM:GM + 512]
        gm3 = AP(gm.tensor, gm.offset, [gm.ap[0], [128, 4], [1, 128]])
        P.op("vector", "tensor_tensor", gm3, a3, mu3, ALU.mult, inc=(c.s_v, 1))
        P.set_mark(pA.name + str(bk), c.s_v)
        P.wait("tensor", c.s_v)
        P.wait("tensor", c.s_s)
        P.wait_mark("tensor", pB.name)
        for i, h in enumerate(heads):
            rb = (h % 2) * 64
            p = h // 2
            ip = p - 2 * hg
            P.op("tensor", "matmul", pB[rb:rb + 64, ip * 128:(ip + 1) * 128],
                 W0B[:, h * 64:(h + 1) * 64],
                 XT[:, QG0 + p * GT + gt0:QG0 + p * GT + gt0 + 128],
                 start=True, stop=False, tile_position=(0, rb))
            P.op("tensor", "matmul", pB[rb:rb + 64, ip * 128:(ip + 1) * 128],
                 UB[:, (hg * 4 + i) * 64:(hg * 4 + i + 1) * 64],
                 XT[:, GM + i * 128:GM + (i + 1) * 128],
                 start=False, stop=True, tile_position=(0, rb),
                 inc=(c.s_t, 1) if i == 3 else None)
        P.wait("vector", c.s_t)
        P.op("vector", "tensor_copy", XT[:, TO + hg * 256:TO + (hg + 1) * 256],
             pB[:, 0:256], inc=(c.s_v, 1))
        P.set_mark(pB.name, c.s_v)
        P.wait("tensor", c.s_v)
        P.wait_mark("tensor", pBT.name)
        for i, h in enumerate(heads):
            rb = (h % 2) * 64
            p = h // 2
            ip = p - 2 * hg
            P.op("tensor", "matmul", pBT[rb:rb + 64, ip * 64:(ip + 1) * 64],
                 XT[:, KT + ip * 128 + rb:KT + ip * 128 + rb + 64],
                 UB[:, (hg * 4 + i) * 64:(hg * 4 + i + 1) * 64],
                 start=True, stop=True, tile_position=(0, rb),
                 inc=(c.s_t, 1) if i == 3 else None)
        P.wait("vector", c.s_t)
        P.op("vector", "tensor_add", W0T[:, hg * 128:(hg + 1) * 128],
             W0T[:, hg * 128:(hg + 1) * 128], pBT[:, 0:128], inc=(c.s_v, 1))
        P.set_mark(pBT.name, c.s_v)
    P.set_mark("aKZ", c.s_t)
    if TTT_STAGES < 11:
        return
    P.wait("tensor", c.s_v)
    P.wait_mark("tensor", pU.name + "o0")
    P.wait_mark("tensor", pU.name + "o1")
    s_wo = c.P.sem("s_wl1ao")
    for half in range(2):
        c.P.wait_mark("sync", "aWOh")
        _ld3(c.P, WO[:, :], 1024, c.st["l1a_woT"], D, half * 4 * 128, 4, 0,
             s_wo)
        P.wait("tensor", s_wo)
        for jt in range(DT):
            prj = jt % 2
            P.wait_mark("tensor", pU.name + "o" + str(prj))
            for p in range(4):
                P.op("tensor", "matmul",
                     pU[:, prj * 512:prj * 512 + 128],
                     WO[:, p * 1024 + jt * 128:p * 1024 + (jt + 1) * 128],
                     XT[:, TO + (half * 4 + p) * 128:TO + (half * 4 + p + 1) * 128],
                     start=(p == 0), stop=(p == 3),
                     inc=(c.s_t, 1) if p == 3 else None)
            P.wait("vector", c.s_t)
            P.op("vector", "tensor_add",
                 XI[:, fmc(jt, chunk_t0, 128)], XI[:, fmc(jt, chunk_t0, 128)],
                 pU[:, prj * 512:prj * 512 + 128], inc=(c.s_v, 1))
            P.set_mark(pU.name + "o" + str(prj), c.s_v)
        P.set_mark("aWOh", c.s_t)


def _emit_l1a_mlp(c, XI):
    P, nc = c.P, c.nc
    _sublayer_barrier(c)
    _emit_ln(c, src=XI, dst=c.XT, w_col=64, b_col=72, barrier=False)
    NHT = H1A // 128
    GRP = 4
    es = ExitStack()
    W1 = es.enter_context(nc.sbuf_tensor(_u("m1W1"), [128, GRP * 1024], BF16))
    W2 = es.enter_context(nc.sbuf_tensor(_u("m1W2"), [128, GRP * 1024], BF16))
    HGB = es.enter_context(nc.sbuf_tensor(_u("m1HG"), [128, GRP * S], BF16))
    PSG = es.enter_context(nc.psum_tensor(_u("m1psg"), [128, 1024], F32))
    PSD = es.enter_context(nc.psum_tensor(_u("m1psd"), [128, 1024], F32))
    s_fw = P.sem("s_wl1am")
    with es:
        P.wait("sync", c.s_s)
        P.wait("sync", c.s_v)
        P.wait("sync", c.s_t)
        for grp in range(NHT // GRP):
            P.wait_mark("sync", "m1W")
            P.wait("sync", c.s_stage)
            gw = GRP * 128
            _ld3(P, W1[:, :], gw, c.st["l1a_w1T"], H1A, 0, DT, grp * gw, s_fw)
            _ld3(P, W2[:, :], 1024, c.st["l1a_w2T"], D, grp * GRP * 128, GRP,
                 0, s_fw)
            w_ready = s_fw
            for hh in range(GRP):
                ht = grp * GRP + hh
                for tc in range(NTCH):
                    prx = tc % 2
                    t0 = tc * TCH
                    P.wait("tensor", w_ready)
                    P.wait_mark("tensor", PSG.name + str(prx))
                    for d in range(DT):
                        P.op("tensor", "matmul", PSG[:, prx * 512:(prx + 1) * 512],
                             W1[:, d * gw + hh * 128:d * gw + (hh + 1) * 128],
                             c.XT[:, fmc(d, t0, TCH)], start=(d == 0),
                             stop=(d == DT - 1),
                             inc=(c.s_t, 1) if d == DT - 1 else None)
                    P.wait("scalar", c.s_t)
                    P.op("scalar", "activation",
                         HGB[:, hh * S + t0:hh * S + t0 + TCH],
                         PSG[:, prx * 512:(prx + 1) * 512], AF.Gelu,
                         bias=bias_col(c, 80 + ht), scale=1.0, inc=(c.s_s, 1))
                    P.set_mark(PSG.name + str(prx), c.s_s)
            h_ready = c.s_s
            hn = c.s_s.n
            for dcx in range(DT):
                for tc in range(NTCH):
                    prx = tc % 2
                    t0 = tc * TCH
                    P.wait("tensor", h_ready, hn)
                    P.wait_mark("tensor", PSD.name + str(prx))
                    for hh in range(GRP):
                        P.op("tensor", "matmul", PSD[:, prx * 512:(prx + 1) * 512],
                             W2[:, hh * 1024 + dcx * 128:hh * 1024 + (dcx + 1) * 128],
                             HGB[:, hh * S + t0:hh * S + t0 + TCH],
                             start=(hh == 0), stop=(hh == GRP - 1),
                             inc=(c.s_t, 1) if hh == GRP - 1 else None)
                    P.wait("vector", c.s_t)
                    if grp == NHT // GRP - 1:
                        P.op("vector", "scalar_tensor_tensor",
                             XI[:, fmc(dcx, t0, TCH)],
                             PSD[:, prx * 512:(prx + 1) * 512],
                             bias_col(c, 112 + dcx), XI[:, fmc(dcx, t0, TCH)],
                             ALU.add, ALU.add, inc=(c.s_v, 1))
                    else:
                        P.op("vector", "tensor_add", XI[:, fmc(dcx, t0, TCH)],
                             XI[:, fmc(dcx, t0, TCH)],
                             PSD[:, prx * 512:(prx + 1) * 512], inc=(c.s_v, 1))
                    P.set_mark(PSD.name + str(prx), c.s_v)
            P.set_mark("m1W", c.s_t)


def _alloc_quant_scratch(c, es, n):
    nc = c.nc
    q = {}
    q["yn"] = es.enter_context(nc.sbuf_tensor(_u("qyn"), [128, n], F32))
    q["c1"] = es.enter_context(nc.sbuf_tensor(_u("qc1"), [128, n], F32))
    q["c2"] = es.enter_context(nc.sbuf_tensor(_u("qc2"), [128, n], F32))
    q["sg"] = es.enter_context(nc.sbuf_tensor(_u("qsg"), [128, n], BF16))
    q["bm"] = es.enter_context(nc.sbuf_tensor(_u("qbm"), [128, n // 16], F32))
    q["rs"] = es.enter_context(nc.sbuf_tensor(_u("qrs"), [128, n // 16], F32))
    return q


def _win16(ap_2d):
    a = ap_2d
    n = a.ap[-1][1]
    return AP(a.tensor, a.offset, [a.ap[0], [16, n // 16], [1, 16]])


def _bc16(ap_small, n):
    a = ap_small
    return AP(a.tensor, a.offset, [a.ap[0], [1, n // 16], [0, 16]])


def _emit_quant(c, q, src, dst, n, m=128):
    P = c.P
    V, SV = "vector", c.s_v
    yn, c1, c2, sg = (q[k][0:m, 0:n] for k in ("yn", "c1", "c2", "sg"))
    bm, rs = q["bm"][0:m, 0:n // 16], q["rs"][0:m, 0:n // 16]
    P.op(V, "tensor_reduce", bm, _win16(src), AX.X, ALU.max,
         apply_absolute_value=True, inc=(SV, 1))
    P.wait("scalar", c.s_s)
    P.wait("scalar", c.s_v)
    P.op("scalar", "activation", sg, src, AF.Sign, inc=(c.s_s, 1))
    P.wait(V, SV)
    P.op(V, "tensor_scalar", bm, bm, 1.0 / 6.0, 1e-12, ALU.mult, ALU.max,
         inc=(SV, 1))
    P.op(V, "scalar_tensor_tensor", yn, src, -1.0, src, ALU.mult, ALU.max)
    P.wait(V, SV)
    P.op(V, "reciprocal", rs, bm, inc=(SV, 1))
    P.wait(V, SV)
    P.op(V, "tensor_tensor", _win16(yn), _win16(yn), _bc16(rs, n), ALU.mult,
         inc=(SV, 1))
    P.wait(V, SV)
    P.op(V, "tensor_single_scalar", c1, yn, 0.25, ALU.is_gt)
    P.op(V, "tensor_single_scalar", c2, yn, 2.5, ALU.is_gt, inc=(SV, 1))
    P.wait(V, SV)
    P.op(V, "scalar_tensor_tensor", c1, yn, 0.75, c1, ALU.is_gt, ALU.add)
    P.op(V, "scalar_tensor_tensor", c2, yn, 3.5, c2, ALU.is_gt, ALU.add,
         inc=(SV, 1))
    P.wait(V, SV)
    P.op(V, "scalar_tensor_tensor", c1, yn, 1.25, c1, ALU.is_gt, ALU.add)
    P.op(V, "scalar_tensor_tensor", c2, yn, 5.0, c2, ALU.is_gt, ALU.add,
         inc=(SV, 1))
    P.wait(V, SV)
    P.op(V, "scalar_tensor_tensor", c1, yn, 1.75, c1, ALU.is_gt, ALU.add)
    P.op(V, "scalar_tensor_tensor", c2, yn, 5.0, c2, ALU.is_gt, ALU.add,
         inc=(SV, 1))
    P.wait(V, SV)
    P.op(V, "scalar_tensor_tensor", c2, c1, 0.5, c2, ALU.mult, ALU.add,
         inc=(SV, 1))
    P.wait(V, c.s_s)
    P.wait(V, SV)
    P.op(V, "tensor_mul", c1, c2, sg, inc=(SV, 1))
    P.wait(V, SV)
    P.op(V, "tensor_tensor", _win16(c1), _win16(c1), _bc16(bm, n), ALU.mult,
         inc=(SV, 1))
    P.wait(V, SV)
    P.op(V, "tensor_copy", dst, c1, inc=(SV, 1))


def _ld3(P, dst_ap, ncol, srcT, rowstride, row0, nrow_t, col0, sem):
    out3 = dst_ap.rearrange("p (t q) -> p t q", q=ncol)
    in3 = AP(srcT.tensor, srcT.offset + row0 * rowstride + col0,
             [[rowstride, 128], [128 * rowstride, nrow_t], [1, ncol]])
    P.dma("sync", out3, in3, sem)


def _emit_pretranspose_weights(c, active):
    P, nc = c.P, c.nc
    jobs = []
    if "l0b" in active:
        for n in "qkvo":
            jobs.append((c.st["l0b_" + n], c.st["l0b_" + n + "T"], D, D))
    if "l1a" in active:
        for n in ("wq", "wk", "wv", "wo"):
            jobs.append((c.st["l1a_" + n], c.st["l1a_" + n + "T"], D, D))
        jobs.append((c.st["l1a_w1"], c.st["l1a_w1T"], H1A, D))
        jobs.append((c.st["l1a_w2"], c.st["l1a_w2T"], D, H1A))
    if ("l0f1" in active) or ("l0f2" in active):
        jobs.append((c.st["l0f_g"], c.st["l0f_gT"], H0, D))
        jobs.append((c.st["l0f_u"], c.st["l0f_uT"], H0, D))
        jobs.append((c.st["l0f_d"], c.st["l0f_dT"], D, H0))
    if not jobs:
        return
    _sublayer_barrier(c)
    es = ExitStack()
    LT = es.enter_context(nc.sbuf_tensor(_u("ptL"), [128, 2 * 4096], BF16))
    TT = es.enter_context(nc.sbuf_tensor(_u("ptT"), [128, 2 * 4096], BF16))
    pX = es.enter_context(nc.psum_tensor(_u("ptP"), [128, 2048], BF16))
    s_l = [P.sem("s_pt0"), P.sem("s_pt1")]
    s_s2 = [P.sem("s_pt2"), P.sem("s_pt3")]
    with es:
        P.wait("sync", c.s_stage)
        for wsrc, wdst, O, I in jobs:
            for ot in range(O // 128):
                pr = ot % 2
                P.wait_mark("sync", f"ptL{pr}")
                P.dma("sync", LT[:, pr * 4096:pr * 4096 + I],
                      wsrc[ot * 128:(ot + 1) * 128, :], s_l[pr])
                P.wait("tensor", s_l[pr])
                P.wait_mark("tensor", f"ptT{pr}")
                for itc in range(I // 1024):
                    pp = itc % 2
                    P.wait_mark("tensor", pX.name + str(pp))
                    for j in range(8):
                        it = itc * 8 + j
                        P.op("tensor", "transpose",
                             pX[:, pp * 1024 + j * 128:pp * 1024 + (j + 1) * 128],
                             LT[:, pr * 4096 + it * 128:pr * 4096 + (it + 1) * 128],
                             c.CIDB[:, :],
                             inc=(c.s_t, 1) if j == 7 else None)
                    P.wait("vector", c.s_t)
                    P.op("vector", "tensor_copy",
                         TT[:, pr * 4096 + itc * 1024:pr * 4096 + (itc + 1) * 1024],
                         pX[:, pp * 1024:(pp + 1) * 1024], inc=(c.s_v, 1))
                    P.set_mark(pX.name + str(pp), c.s_v)
                P.set_mark(f"ptL{pr}", c.s_t)
                P.wait("sync", c.s_v)
                tv = TT[:, pr * 4096:pr * 4096 + I]
                src3 = AP(tv.tensor, tv.offset, [tv.ap[0], [128, I // 128], [1, 128]])
                dst3 = AP(wdst.tensor, wdst.offset + ot * 128,
                          [[O, 128], [128 * O, I // 128], [1, 128]])
                P.dma("sync", dst3, src3, s_s2[pr])
                P.set_mark(f"ptT{pr}", s_s2[pr])


def _emit_l1f_wquant(c):
    P, nc = c.P, c.nc
    _sublayer_barrier(c)
    es = ExitStack()
    with es:
        WF = es.enter_context(nc.sbuf_tensor(_u("wqF"), [128, 2 * 2816], F32))
        WQ = es.enter_context(nc.sbuf_tensor(_u("wqQ"), [128, 2 * 2816], BF16))
        WT = es.enter_context(nc.sbuf_tensor(_u("wqT"), [128, 2816], BF16))
        p_tq = es.enter_context(nc.psum_tensor(_u("wqp"), [128, 2048], BF16))
        q = _alloc_quant_scratch(c, es, 2816)
        q1k = _alloc_quant_scratch(c, es, 1024)
        jobs = []
        for wsrc, wdst, ntile, width in (
            (c.l1f["w_gate"], c.st["l1f_gT"], 22, 1024),
            (c.l1f["w_up"], c.st["l1f_uT"], 22, 1024),
            (c.l1f["w_down"], c.st["l1f_dT"], 8, 2816),
        ):
            for i in range(ntile):
                jobs.append((wsrc, wdst, i, ntile, width))
        for idx, (wsrc, wdst, i, ntile, width) in enumerate(jobs):
            pr = idx % 2
            s_lw = P.sem(f"s_l1fq_ld{pr}")
            s_sw = P.sem(f"s_l1fq_st{pr}")
            qq = q1k if width == 1024 else q
            P.wait_mark("sync", f"wqF{pr}")
            P.dma("sync", WF[:, pr * 2816:pr * 2816 + width],
                  wsrc[i * 128:(i + 1) * 128, :], s_lw)
            P.wait("vector", s_lw)
            P.wait("scalar", s_lw)
            P.wait_mark("vector", f"wqQ{pr}")
            _emit_quant(c, qq, WF[:, pr * 2816:pr * 2816 + width],
                        WQ[:, pr * 2816:pr * 2816 + width], width)
            P.set_mark(f"wqF{pr}", c.s_v)
            nblk = width // 128
            ow = ntile * 128
            P.wait("tensor", c.s_v)
            P.wait_mark("vector", "wqT")
            for j in range(nblk):
                pj = j % 2
                P.wait_mark("tensor", p_tq.name + str(pj))
                P.op("tensor", "transpose", p_tq[:, pj * 1024:pj * 1024 + 128],
                     WQ[:, pr * 2816 + j * 128:pr * 2816 + (j + 1) * 128],
                     c.CIDB[:, :], inc=(c.s_t, 1))
                P.wait("vector", c.s_t)
                P.op("vector", "tensor_copy", WT[:, j * 128:(j + 1) * 128],
                     p_tq[:, pj * 1024:pj * 1024 + 128], inc=(c.s_v, 1))
                P.set_mark(p_tq.name + str(pj), c.s_v)
            P.set_mark(f"wqQ{pr}", c.s_v)
            P.wait("sync", c.s_v)
            wtv = WT[:, :]
            src3 = AP(wtv.tensor, wtv.offset, [wtv.ap[0], [128, nblk], [1, 128]])
            dst3 = AP(wdst.tensor, wdst.offset + i * 128,
                      [[ow, 128], [128 * ow, nblk], [1, 128]])
            P.dma("sync", dst3, src3, s_sw)
            P.set_mark("wqT", s_sw)
    c.l1f_wq_done = True


def _emit_l1ffn(c):
    P, nc = c.P, c.nc
    _sublayer_barrier(c)
    HCH = [(0, 512), (512, 512), (1024, 512), (1536, 512), (2048, 512),
           (2560, 256)]
    es = ExitStack()
    XTM = es.enter_context(nc.sbuf_tensor(_u("f1XTM"), [128, D], F32))
    XQT = es.enter_context(nc.sbuf_tensor(_u("f1XQT"), [128, D], BF16))
    HQF = es.enter_context(nc.sbuf_tensor(_u("f1HQF"), [128, 22 * TCH], BF16))
    HTM = es.enter_context(nc.sbuf_tensor(_u("f1HTM"), [128, 2048], F32))
    HQT = es.enter_context(nc.sbuf_tensor(_u("f1HQT"), [128, 2048], BF16))
    SGT = es.enter_context(nc.sbuf_tensor(_u("f1SG"), [128, TCH], F32))
    WB = es.enter_context(nc.sbuf_tensor(_u("f1WB"), [128, 22 * 1024], BF16))
    p_a = es.enter_context(nc.psum_tensor(_u("f1pa"), [128, 1024], F32))
    p_b = es.enter_context(nc.psum_tensor(_u("f1pb"), [128, 1024], F32))
    p_t = es.enter_context(nc.psum_tensor(_u("f1pt"), [128, 2048], BF16))
    p_d = es.enter_context(nc.psum_tensor(_u("f1pd"), [128, 1024], F32))
    s_w = P.sem("s_wl1f")
    s_wd = P.sem("s_wl1fd")
    with es:
        q1k = _alloc_quant_scratch(c, es, 1024)
        ROWS = es.enter_context(nc.sbuf_tensor(_u("f1RO"), [128, 8], F32))
        for tt in range(S // 128):
            P.wait_mark("tensor", XTM.name)
            for d in range(DT):
                pr2 = d % 2
                P.wait_mark("tensor", p_a.name + "x" + str(pr2))
                P.op("tensor", "transpose", p_a[:, pr2 * 512:pr2 * 512 + 128],
                     c.XR[:, fmc(d, tt * 128, 128)], c.CID[:, :],
                     inc=(c.s_t, 1))
                P.wait("scalar", c.s_t)
                P.op("scalar", "copy", XTM[:, d * 128:(d + 1) * 128],
                     p_a[:, pr2 * 512:pr2 * 512 + 128], inc=(c.s_s, 1))
                P.set_mark(p_a.name + "x" + str(pr2), c.s_s)
            P.wait("vector", c.s_s)
            P.op("vector", "tensor_reduce", ROWS[:, 0:1], XTM[:, :], AX.X,
                 ALU.add, negate=True, inc=(c.s_v, 1))
            P.wait("scalar", c.s_v)
            P.op("scalar", "activation", q1k["c1"][:, :], XTM[:, :],
                 AF.Square, inc=(c.s_s, 1))
            P.wait("vector", c.s_s)
            P.op("vector", "tensor_reduce", ROWS[:, 1:2], q1k["c1"][:, :],
                 AX.X, ALU.add, inc=(c.s_v, 1))
            P.wait("vector", c.s_v)
            P.op("vector", "tensor_scalar_mul", ROWS[:, 2:3], ROWS[:, 0:1],
                 1.0 / D, inc=(c.s_v, 1))
            P.wait("vector", c.s_v)
            P.op("vector", "tensor_mul", ROWS[:, 3:4], ROWS[:, 2:3],
                 ROWS[:, 2:3], inc=(c.s_v, 1))
            P.wait("vector", c.s_v)
            P.op("vector", "scalar_tensor_tensor", ROWS[:, 4:5], ROWS[:, 1:2],
                 1.0 / D, ROWS[:, 3:4], ALU.mult, ALU.subtract,
                 inc=(c.s_v, 1))
            P.wait("scalar", c.s_v)
            P.op("scalar", "activation", ROWS[:, 4:5], ROWS[:, 4:5], AF.Sqrt,
                 bias=c.CNST[:, 0:1], scale=1.0, inc=(c.s_s, 1))
            P.wait("vector", c.s_s)
            P.op("vector", "reciprocal", ROWS[:, 5:6], ROWS[:, 4:5],
                 inc=(c.s_v, 1))
            P.wait("vector", c.s_v)
            P.op("vector", "tensor_scalar", XTM[:, :], XTM[:, :],
                 ROWS[:, 2:3], ROWS[:, 5:6], ALU.add, ALU.mult,
                 inc=(c.s_v, 1))
            P.wait("vector", c.s_v)
            _emit_quant(c, q1k, XTM[:, :], XQT[:, :], 1024)
            P.set_mark(XTM.name, c.s_v)
            P.wait("tensor", c.s_v)
            for d in range(DT):
                pr2 = d % 2
                P.wait_mark("tensor", p_t.name + str(pr2))
                P.op("tensor", "transpose", p_t[:, pr2 * 1024:pr2 * 1024 + 128],
                     XQT[:, d * 128:(d + 1) * 128], c.CIDB[:, :],
                     inc=(c.s_t, 1))
                P.wait("vector", c.s_t)
                P.op("vector", "tensor_copy", c.XT[:, fmc(d, tt * 128, 128)],
                     p_t[:, pr2 * 1024:pr2 * 1024 + 128], inc=(c.s_v, 1))
                P.set_mark(p_t.name + str(pr2), c.s_v)
        for tc in range(NTCH):
            t0 = tc * TCH
            for hci, (h0, hn) in enumerate(HCH):
                P.wait_mark("sync", WB.name)
                for buf0, wt in ((0, c.st["l1f_gT"]), (4096, c.st["l1f_uT"])):
                    bv = WB[:, buf0:buf0 + DT * 512]
                    out3 = AP(bv.tensor, bv.offset,
                              [bv.ap[0], [512, DT], [1, hn]])
                    in3 = AP(wt.tensor, wt.offset + h0,
                             [[H1, 128], [128 * H1, DT], [1, hn]])
                    P.dma("sync", out3, in3, s_w)
                P.wait("tensor", s_w)
                for ttl in range(4):
                    tt0 = t0 + ttl * 128
                    prx = ttl % 2
                    P.wait_mark("tensor", p_a.name + str(prx))
                    for d in range(DT):
                        P.op("tensor", "matmul", p_a[:, prx * 512:prx * 512 + hn],
                             c.XT[:, fmc(d, tt0, 128)],
                             WB[:, d * 512:d * 512 + hn],
                             start=(d == 0), stop=(d == DT - 1),
                             inc=(c.s_t, 1) if d == DT - 1 else None)
                    P.wait_mark("tensor", p_b.name + str(prx))
                    for d in range(DT):
                        P.op("tensor", "matmul", p_b[:, prx * 512:prx * 512 + hn],
                             c.XT[:, fmc(d, tt0, 128)],
                             WB[:, 4096 + d * 512:4096 + d * 512 + hn],
                             start=(d == 0), stop=(d == DT - 1),
                             inc=(c.s_t, 1) if d == DT - 1 else None)
                    P.wait("scalar", c.s_t, c.s_t.n - 1)
                    P.wait_mark("scalar", SGT.name)
                    P.op("scalar", "activation", SGT[:, 0:hn],
                         p_a[:, prx * 512:prx * 512 + hn], AF.Silu, bias=0.0,
                         scale=1.0, inc=(c.s_s, 1))
                    P.set_mark(p_a.name + str(prx), c.s_s)
                    P.wait("vector", c.s_s)
                    P.wait("vector", c.s_t)
                    P.wait_mark("vector", HTM.name + str(ttl))
                    P.op("vector", "tensor_mul",
                         HTM[:, ttl * 512:ttl * 512 + hn], SGT[:, 0:hn],
                         p_b[:, prx * 512:prx * 512 + hn], inc=(c.s_v, 1))
                    P.set_mark(p_b.name + str(prx), c.s_v)
                    P.set_mark(SGT.name, c.s_v)
                P.set_mark(WB.name, c.s_t)
                P.wait("vector", c.s_v)
                for ttl in range(4):
                    _emit_quant(c, q1k, HTM[:, ttl * 512:ttl * 512 + hn],
                                HQT[:, ttl * 512:ttl * 512 + hn], hn)
                    P.set_mark(HTM.name + str(ttl), c.s_v)
                P.wait("tensor", c.s_v)
                nht = hn // 128
                for ttl in range(4):
                    for hh in range(nht):
                        pr2 = hh % 2
                        P.wait_mark("tensor", p_t.name + str(pr2))
                        P.op("tensor", "transpose", p_t[:, pr2 * 1024:pr2 * 1024 + 128],
                             HQT[:, ttl * 512 + hh * 128:ttl * 512 + (hh + 1) * 128],
                             c.CIDB[:, :], inc=(c.s_t, 1))
                        P.wait("vector", c.s_t)
                        P.op("vector", "tensor_copy",
                             HQF[:, (h0 // 128 + hh) * TCH + ttl * 128:
                                 (h0 // 128 + hh) * TCH + (ttl + 1) * 128],
                             p_t[:, pr2 * 1024:pr2 * 1024 + 128],
                             inc=(c.s_v, 1))
                        P.set_mark(p_t.name + str(pr2), c.s_v)
            P.set_mark(WB.name, c.s_t)
            P.wait_mark("sync", WB.name)
            _ld3(P, WB[:, :], 1024, c.st["l1f_dT"], D, 0, 22, 0, s_wd)
            P.wait("tensor", s_wd)
            P.wait("tensor", c.s_v)
            for dcx in range(DT):
                prx = dcx % 2
                P.wait_mark("tensor", p_d.name + str(prx))
                for ht in range(22):
                    P.op("tensor", "matmul",
                         p_d[:, prx * 512:(prx + 1) * 512],
                         WB[:, ht * 1024 + dcx * 128:ht * 1024 + (dcx + 1) * 128],
                         HQF[:, ht * TCH:(ht + 1) * TCH],
                         start=(ht == 0), stop=(ht == 21),
                         inc=(c.s_t, 1) if ht == 21 else None)
                P.wait("vector", c.s_t)
                P.op("vector", "tensor_add", c.XR[:, fmc(dcx, t0, TCH)],
                     c.XR[:, fmc(dcx, t0, TCH)],
                     p_d[:, prx * 512:(prx + 1) * 512], inc=(c.s_v, 1))
                P.set_mark(p_d.name + str(prx), c.s_v)
            P.set_mark(WB.name, c.s_t)


def kernel(**inputs):
    active = SUBLAYERS
    c = build_program(active)
    consts = c.consts_np
    in_maps = []
    x_full = np.ascontiguousarray(inputs["L0_attn_a_x"])
    for core in range(NCORE):
        m = {}
        for k, v in inputs.items():
            if k == "L0_attn_a_x":
                m[k] = np.ascontiguousarray(
                    x_full[core * SEQ_PER_CORE:(core + 1) * SEQ_PER_CORE])
            else:
                m[k] = np.ascontiguousarray(v)
        for k, v in consts.items():
            m[k] = v
        in_maps.append(m)
    res = run_bass_kernel_spmd(c.nc, in_maps, core_ids=list(range(NCORE)))
    return np.concatenate([r["out"] for r in res.results], axis=0)
